# version 03
# baseline (speedup 1.0000x reference)
import numpy as np
import concourse.bass as bass
import concourse.mybir as mybir
from concourse.bass_utils import run_bass_kernel_spmd
from contextlib import ExitStack

F32 = mybir.dt.float32
BF16 = mybir.dt.bfloat16
I8 = mybir.dt.int8
AF = mybir.ActivationFunctionType
ALU = mybir.AluOpType
AX = mybir.AxisListType

D = 1024
H = 16
DH = 64
FF = 4096
C = 128
TPT = 256
NCHUNK = 2

E_SET = 6
G_SET = 10

ENGS = ("d", "t", "v", "s", "g")


class P:
    def __init__(self, nc):
        self.nc = nc
        self.q = {e: [] for e in ENGS}
        self.cnt = {}
        self.sems = {}
        self.lastwait = {e: {} for e in ENGS}
        self.act_set = None

    def new_sem(self, name):
        self.cnt[name] = 0

    def bump(self, sem, n):
        self.cnt[sem] += n
        return (sem, self.cnt[sem])

    def emit(self, eng, fn):
        self.q[eng].append(fn)

    def wait(self, eng, tok):
        if tok is None:
            return
        sem, val = tok
        if self.lastwait[eng].get(sem, 0) >= val:
            return
        self.lastwait[eng][sem] = val

        def f(e, sem=sem, val=val):
            e.wait_ge(self.sems[sem], val)
        self.emit(eng, f)

    def op(self, eng, fn, waits=(), inc=None, n=None):
        for w in waits:
            self.wait(eng, w)
        tok = None
        if inc:
            amt = n if n is not None else (16 if inc in ("SW", "SO") else 1)
            tok = self.bump(inc, amt)

        def g(e, fn=fn, tok=tok):
            i = fn(e)
            if tok is not None:
                amt = 16 if tok[0] in ("SW", "SO") else 1
                i.then_inc(self.sems[tok[0]], amt)
        self.emit(eng, g)
        return tok

    def tok(self, sem):
        return (sem, self.cnt[sem])

    def table(self, set_id):
        if self.act_set == set_id:
            return
        self.act_set = set_id

        def f(e, set_id=set_id):
            e.add_instruction(mybir.InstLoadActFuncSet(
                name=self.nc.get_next_instruction_name(),
                act_func_set_id=set_id, ins=[], outs=[]))
        self.emit("s", f)


def build(nm=16, nprf=16, dbg=None):
    dbg = dbg or set()
    nc = bass.Bass("TRN2", target_bir_lowering=False)

    di = lambda name, shape, dt=F32: nc.dram_tensor(name, shape, dt, kind="ExternalInput").ap()
    do = lambda name, shape, dt=F32: nc.dram_tensor(name, shape, dt, kind="ExternalOutput").ap()
    dint = lambda name, shape, dt=BF16: nc.dram_tensor(name, shape, dt, kind="Internal").ap()

    x_main = di("x_main", [4096, D])
    x_pref = di("x_pref", [4096, D])
    maskcol = di("maskcol", [128, 1])
    wq = di("wq", [D, D]); wk = di("wk", [D, D]); wv = di("wv", [D, D]); wo = di("wo", [D, D])
    proj = di("proj", [DH, DH])
    aw1 = di("aw1", [D, FF]); aw2 = di("aw2", [FF, D])
    fw1t = di("fw1t", [D, FF]); fw2t = di("fw2t", [FF, D])
    g1c = di("g1c", [128, 8]); g2c = di("g2c", [128, 8])
    b1c = di("b1c", [128, 8]); b2c = di("b2c", [128, 8])
    ab1r = di("ab1r", [1, FF]); ab2r = di("ab2r", [1, D])
    fb1r = di("fb1r", [1, FF]); fb2r = di("fb2r", [1, D])
    ident_bf = di("ident_bf", [128, 128], BF16)
    ident_f32 = di("ident_f32", [128, 128], F32)
    onescol_bf = di("onescol_bf", [1, 128], BF16)
    onescol_f32 = di("onescol_f32", [1, 128], F32)
    onesrow_bf = di("onesrow_bf", [1, 512], BF16)
    onestok_bf = di("onestok_bf", [128, 1], BF16)
    maskT4 = di("maskT4", [128, 512])

    y_out = do("y_out", [4096, D])

    scr = {}
    for name, nq in [("q", 1), ("k", 1), ("v", 1), ("o", 1),
                     ("aw1", 4), ("aw2", 4), ("f1", 4), ("f2", 4)]:
        scr[name] = dint("scr_" + name, [nq, 8, 128, 1024], BF16)

    dbg_outs = {}

    with ExitStack() as st_:
        E = st_.enter_context
        wb = [E(nc.sbuf_tensor(f"wb{i}", [128, 8192], BF16)) for i in range(2)]
        xin = [E(nc.sbuf_tensor(f"xin{i}", [128, 2048], F32)) for i in range(2)]
        xr0 = E(nc.sbuf_tensor("xr0", [128, 2048], F32))
        xr1 = E(nc.sbuf_tensor("xr1", [128, 2048], F32))
        ybuf = E(nc.sbuf_tensor("ybuf", [128, 2048], F32))
        hf = E(nc.sbuf_tensor("hf", [128, 2048], F32))
        hT = E(nc.sbuf_tensor("hT", [128, 2048], BF16))
        QRT = E(nc.sbuf_tensor("QRT", [64, 4096], BF16))
        KRT = E(nc.sbuf_tensor("KRT", [64, 4096], BF16))
        QT = QRT
        KT = KRT
        Kc = E(nc.sbuf_tensor("Kc", [128, 2048], BF16))
        Vc = E(nc.sbuf_tensor("Vc", [128, 2080], BF16))
        Ssb = E(nc.sbuf_tensor("Ssb", [128, 2048], BF16))
        attnT = E(nc.sbuf_tensor("attnT", [128, 2048], BF16))
        attn_tm = E(nc.sbuf_tensor("attn_tm", [128, 2048], BF16))
        stS = E(nc.sbuf_tensor("stS", [64, 1024], BF16))
        ksS = E(nc.sbuf_tensor("ksS", [64, 16], BF16))
        garena = E(nc.sbuf_tensor("garena", [128, 8192], BF16))
        ints2 = E(nc.sbuf_tensor("ints2", [128, 8192], BF16))
        xqb = attn_tm
        i8t = E(nc.sbuf_tensor("i8t", [128, 2048], I8))
        i8g = [E(nc.sbuf_tensor(f"i8g{i}", [128, 256], I8)) for i in range(2)]
        sqb = [E(nc.sbuf_tensor(f"sqb{i}", [128, 256], BF16)) for i in range(2)]
        gmb = [E(nc.sbuf_tensor(f"gmb{i}", [128, 256], BF16)) for i in range(2)]
        e1 = E(nc.sbuf_tensor("e1", [128, 512], F32))
        e2 = E(nc.sbuf_tensor("e2", [128, 512], F32))
        e3 = E(nc.sbuf_tensor("e3", [128, 512], F32))
        sst = E(nc.sbuf_tensor("sst", [128, 32], F32))
        sqd = E(nc.sbuf_tensor("sqd", [128, 1024], BF16))
        stats = E(nc.sbuf_tensor("stats", [128, 64], F32))
        dens = E(nc.sbuf_tensor("dens", [128, 16], F32))
        recips = E(nc.sbuf_tensor("recips", [128, 16], F32))
        mrow = E(nc.sbuf_tensor("mrow", [1, 256], BF16))
        rowf = E(nc.sbuf_tensor("rowf", [1, 256], F32))
        Bsb = E(nc.sbuf_tensor("Bsb", [128, 256], BF16))
        Id = E(nc.sbuf_tensor("Id", [128, 128], BF16))
        Idf = E(nc.sbuf_tensor("Idf", [128, 128], F32))
        McT = E(nc.sbuf_tensor("McT", [128, 512], F32))
        Onc = E(nc.sbuf_tensor("Onc", [1, 128], BF16))
        Oncf = E(nc.sbuf_tensor("Oncf", [1, 128], F32))
        Onr = E(nc.sbuf_tensor("Onr", [1, 512], BF16))
        Ontk = E(nc.sbuf_tensor("Ontk", [128, 1], BF16))
        Mcol = E(nc.sbuf_tensor("Mcol", [128, 1], F32))
        projb = E(nc.sbuf_tensor("projb", [64, 64], BF16))
        projf = E(nc.sbuf_tensor("projf", [64, 64], F32))
        g1b = E(nc.sbuf_tensor("g1b", [128, 8], F32))
        g2b = E(nc.sbuf_tensor("g2b", [128, 8], F32))
        b1f = E(nc.sbuf_tensor("b1f", [128, 8], F32))
        b2f = E(nc.sbuf_tensor("b2f", [128, 8], F32))
        b1b = E(nc.sbuf_tensor("b1b", [128, 8], BF16))
        b2b = E(nc.sbuf_tensor("b2b", [128, 8], BF16))
        bqr = E(nc.sbuf_tensor("bqr", [1, 1024], BF16))
        bkr = E(nc.sbuf_tensor("bkr", [1, 1024], BF16))
        bvr = E(nc.sbuf_tensor("bvr", [1, 1024], BF16))
        baw1 = E(nc.sbuf_tensor("baw1", [1, 4096], BF16))
        ab2b = E(nc.sbuf_tensor("ab2b", [1, 1024], BF16))
        fb1b = E(nc.sbuf_tensor("fb1b", [1, 4096], BF16))
        fb2tile = E(nc.sbuf_tensor("fb2tile", [128, 1024], F32))
        fb2b = E(nc.sbuf_tensor("fb2b", [1, 1024], BF16))
        rtmp = ybuf[0:1, 0:512]
        rtmp2 = ybuf[0:1, 1024:2048]
        swc = E(nc.sbuf_tensor("swc", [128, 4], F32))
        tiny = E(nc.sbuf_tensor("tiny", [1, 8], F32))
        tiny2 = E(nc.sbuf_tensor("tiny2", [128, 1], F32))
        st0 = E(nc.psum_tensor("st0", [64, 512], F32))
        st1 = E(nc.psum_tensor("st1", [64, 512], F32))
        ksb = E(nc.psum_tensor("ksb", [64, 16], F32))
        pmm = [E(nc.psum_tensor(f"pmm{i}", [128, 512], F32)) for i in range(4)]
        pbf = E(nc.psum_tensor("pbf", [128, 1024], BF16))
        sem_names = ["SW", "SO", "ST", "SV", "SS", "SP"]
        sems = {nm_: E(nc.semaphore(nm_)) for nm_ in sem_names}

        p = P(nc)
        p.sems = sems
        for s_ in sem_names:
            p.new_sem(s_)

        gare_f32 = garena[:].bitcast(F32)
        qkraw = gare_f32[:, 0:2048]
        ptmp = gare_f32[:, 2048:4096]
        ptmp2 = gare_f32[:, 0:1024]
        i8p = ints2[:].bitcast(I8)

        SC = {"ln1": 0, "ln2": 8, "oln": 16, "fln": 24}

        pmm_free = [None] * 4
        pbf_free = [None]
        wb_free = [None, None]
        wb_ready = [None, None]

        def dma_in(dst, src, queue="d", waits=()):
            return p.op(queue, lambda e, d=dst, s=src: e.dma_start(out=d, in_=s),
                        waits=waits, inc="SW" if queue == "d" else "SO")

        def dump(name, src_ap, shape, dt=F32, waits=()):
            if name not in dbg:
                return
            ap = do("dbg_" + name, shape, dt)
            dbg_outs[name] = ap
            tok = p.op("g", lambda e, a=ap, s=src_ap: e.dma_start(out=a, in_=s),
                       waits=waits, inc="SO")
            for eng in ENGS:
                p.wait(eng, tok)

        for dst, src in [(Id[:], ident_bf), (Idf[:], ident_f32), (Onc[:], onescol_bf),
                         (Oncf[:], onescol_f32), (Onr[:], onesrow_bf), (Ontk[:], onestok_bf),
                         (McT[:], maskT4), (Mcol[:], maskcol),
                         (g1b[:], g1c),
                         (g2b[:], g2c),
                         (projf[:], proj),
                         (b1f[:], b1c),
                         (b2f[:], b2c)]:
            dma_in(dst, src)
        tok_consts = p.tok("SW")

        p.op("v", lambda e: e.tensor_copy(out=projb[:], in_=projf[:]),
             waits=[tok_consts], inc="SV")
        tok_projb = p.tok("SV")
        p.op("v", lambda e: e.tensor_copy(out=b1b[:], in_=b1f[:]), inc="SV")
        p.op("v", lambda e: e.tensor_copy(out=b2b[:], in_=b2f[:]), inc="SV")
        for dst_b, src_r, n512 in [(ab2b, ab2r, 2), (fb1b, fb1r, 8), (fb2b, fb2r, 2)]:
            for i in range(n512):
                tok = dma_in(rtmp, src_r[0:1, i * 512:(i + 1) * 512], waits=[p.tok("SV")])
                p.op("v", lambda e, d=dst_b, i=i: e.tensor_copy(
                    out=d[0:1, i * 512:(i + 1) * 512], in_=rtmp), waits=[tok], inc="SV")
        tok_smallcasts = p.tok("SV")

        p.op("g", lambda e: e.memset(Vc[:], 1.0), inc="SP")
        p.op("g", lambda e: e.memset(stS[:], 0.0), inc="SP")
        p.op("g", lambda e: e.memset(ksS[:], 0.0), inc="SP")
        tok_memset = p.tok("SP")

        p.table(E_SET)

        tok_bias = {}

        def prep_linear(w_ap, scr_name, gcol, bias_dst, beta_b, extra_row):
            nq = scr[scr_name].shape[0]
            kmajor = scr_name in ("aw2",)
            for q in range(nq):
                for k in range(8):
                    if kmajor:
                        rows, cols = (q * 8 + k) * 128, 0
                    else:
                        rows, cols = k * 128, q * 1024
                    tok = dma_in(ptmp[:, 0:1024], w_ap[rows:rows + 128, cols:cols + 1024],
                                 waits=[p.tok("SV")])
                    if bias_dst is not None:
                        p.op("v", lambda e, k=k: e.tensor_copy(
                            out=wb[1][:, k * 1024:(k + 1) * 1024], in_=ptmp[:, 0:1024]),
                            waits=[tok] + ([wb_free[1]] if wb_free[1] else []), inc="SV")
                    if gcol is not None:
                        p.op("v", lambda e, k=k, g=gcol: e.tensor_scalar(
                            out=wb[0][:, k * 1024:(k + 1) * 1024], in0=ptmp[:, 0:1024],
                            scalar1=g[:, k:k + 1], scalar2=None, op0=ALU.mult),
                            waits=[tok, tok_consts] + ([wb_free[0]] if wb_free[0] else []),
                            inc="SV")
                    else:
                        p.op("v", lambda e, k=k: e.tensor_copy(
                            out=wb[0][:, k * 1024:(k + 1) * 1024], in_=ptmp[:, 0:1024]),
                            waits=[tok] + ([wb_free[0]] if wb_free[0] else []), inc="SV")
                tok_stage = p.tok("SV")
                tok_bmm = None
                if bias_dst is not None:
                    for n in range(2):
                        for k in range(8):
                            p.op("t", lambda e, k=k, n=n, bb=beta_b: e.matmul(
                                out=pmm[3][0:1, 0:512],
                                lhsT=bb[:, k:k + 1],
                                rhs=wb[1][:, k * 1024 + n * 512: k * 1024 + (n + 1) * 512],
                                start=(k == 0), stop=(k == 7)),
                                waits=([tok_stage, tok_smallcasts] +
                                       ([pmm_free[3]] if pmm_free[3] else [])) if k == 0 else [],
                                inc=("ST" if k == 7 else None))
                        tok_bmm = p.tok("ST")
                        off = q * 1024 + n * 512
                        if extra_row is None:
                            p.op("v", lambda e, off=off, bd=bias_dst: e.tensor_copy(
                                out=bd[0:1, off:off + 512],
                                in_=pmm[3][0:1, 0:512]), waits=[tok_bmm], inc="SV")
                        else:
                            tok_er = dma_in(rtmp, extra_row[0:1, off:off + 512],
                                            waits=[p.tok("SV")])
                            p.op("v", lambda e, off=off, bd=bias_dst: e.tensor_tensor(
                                out=bd[0:1, off:off + 512],
                                in0=pmm[3][0:1, 0:512],
                                in1=rtmp, op=ALU.add),
                                waits=[tok_bmm, tok_er], inc="SV")
                        pmm_free[3] = p.tok("SV")
                    tok_bias[scr_name] = p.tok("SV")
                    wb_free[1] = tok_bmm
                w_ = [tok_stage] + ([tok_bmm] if tok_bmm else [])
                for k in range(8):
                    wb_free[0] = p.op("d", lambda e, q=q, sn=scr_name, k=k: e.dma_start(
                        out=scr[sn][q, k], in_=wb[0][:, k * 1024:(k + 1) * 1024]),
                        waits=(w_ if k == 0 else []), inc="SW")

        prep_linear(wq, "q", g1b, bqr, b1b, None)
        prep_linear(wk, "k", g1b, bkr, b1b, None)
        prep_linear(wv, "v", g1b, bvr, b1b, None)
        prep_linear(wo, "o", None, None, None, None)
        prep_linear(aw1, "aw1", g2b, baw1, b2b, ab1r)
        prep_linear(aw2, "aw2", None, None, None, None)

        for n in range(2):
            p.op("t", lambda e, n=n: e.matmul(
                out=pmm[3][:, 0:512], lhsT=Onc[:],
                rhs=fb2b[0:1, n * 512:(n + 1) * 512], start=True, stop=True),
                waits=[tok_smallcasts, tok_consts] + ([pmm_free[3]] if pmm_free[3] else []),
                inc="ST")
            tok_fb2 = p.tok("ST")
            p.op("v", lambda e, n=n: e.tensor_copy(
                out=fb2tile[:, n * 512:(n + 1) * 512], in_=pmm[3][:, 0:512]),
                waits=[tok_fb2], inc="SV")
            pmm_free[3] = p.tok("SV")

        def prep_ternary(w_ap, scr_name, sw_col_idx):
            abscol = stats[:, 32:64]
            nr, ncol = w_ap.shape[0] // 128, w_ap.shape[1] // 1024
            ci = 0
            for rr in range(nr):
                for cc in range(ncol):
                    buf = ptmp[:, (ci % 2) * 1024:(ci % 2) * 1024 + 1024]
                    tok = dma_in(buf, w_ap[rr * 128:(rr + 1) * 128, cc * 1024:(cc + 1) * 1024],
                                 waits=[p.tok("SS")] if ci >= 2 else [])
                    p.op("s", lambda e, ci=ci, buf=buf: e.activation(
                        out=sqd[:], in_=buf,
                        func=AF.Abs, accum_out=abscol[:, ci:ci + 1]),
                        waits=[tok], inc="SS")
                    ci += 1
            tok_abs = p.tok("SS")
            p.op("v", lambda e: e.reduce_sum(out=tiny2[:, 0:1], in_=abscol, axis=AX.X),
                 waits=[tok_abs], inc="SV")
            tok_red = p.tok("SV")
            p.op("t", lambda e: e.transpose(out=pmm[3][0:1, 0:128].bitcast(F32),
                                            in_=tiny2[:, 0:1], identity=Idf[:]),
                 waits=([tok_red, tok_consts] + ([pmm_free[3]] if pmm_free[3] else [])),
                 inc="ST")
            tok_tr = p.tok("ST")
            p.op("v", lambda e: e.tensor_copy(out=rowf[0:1, 0:128], in_=pmm[3][0:1, 0:128]),
                 waits=[tok_tr], inc="SV")
            pmm_free[3] = p.tok("SV")
            p.op("v", lambda e: e.reduce_sum(out=tiny[0:1, 0:1], in_=rowf[0:1, 0:128],
                                             axis=AX.X), waits=[p.tok("SV")], inc="SV")
            p.op("v", lambda e: e.tensor_scalar(out=tiny[0:1, 1:2], in0=tiny[0:1, 0:1],
                                                scalar1=1.0 / (4096.0 * 1024.0), scalar2=1e-5,
                                                op0=ALU.mult, op1=ALU.max),
                 waits=[p.tok("SV")], inc="SV")
            p.op("v", lambda e: e.reciprocal(out=tiny[0:1, 2:3], in_=tiny[0:1, 1:2]),
                 waits=[p.tok("SV")], inc="SV")
            tok_sc = p.tok("SV")
            p.op("t", lambda e: e.matmul(out=pmm[3][:, 0:2].bitcast(F32), lhsT=Oncf[:],
                                         rhs=tiny[0:1, 1:3], start=True, stop=True),
                 waits=[tok_sc, pmm_free[3]], inc="ST")
            tok_bc = p.tok("ST")
            p.op("v", lambda e, i=sw_col_idx: e.tensor_copy(out=swc[:, i:i + 1],
                                                            in_=pmm[3][:, 0:1]),
                 waits=[tok_bc], inc="SV")
            p.op("v", lambda e, i=sw_col_idx: e.tensor_copy(out=swc[:, i + 2:i + 3],
                                                            in_=pmm[3][:, 1:2]), inc="SV")
            pmm_free[3] = p.tok("SV")
            tok_sw = p.tok("SV")
            kmajor = scr_name == "f2"
            for q in range(4):
                for k in range(8):
                    if kmajor:
                        rows, cols = (q * 8 + k) * 128, 0
                    else:
                        rows, cols = k * 128, q * 1024
                    tok = dma_in(ptmp[:, 0:1024], w_ap[rows:rows + 128, cols:cols + 1024],
                                 waits=[p.tok("SV")])
                    p.op("v", lambda e, i=sw_col_idx: e.tensor_scalar(
                        out=i8p[:, 0:1024], in0=ptmp[:, 0:1024],
                        scalar1=swc[:, i + 2:i + 3], scalar2=None, op0=ALU.mult),
                        waits=[tok, tok_sw, p.tok("SV")], inc="SV")
                    p.op("v", lambda e: e.tensor_copy(out=ptmp2[:], in_=i8p[:, 0:1024]),
                         waits=[p.tok("SV")], inc="SV")
                    p.op("v", lambda e, k=k: e.tensor_scalar(
                        out=wb[0][:, k * 1024:(k + 1) * 1024], in0=ptmp2[:],
                        scalar1=1.0, scalar2=-1.0, op0=ALU.min, op1=ALU.max),
                        waits=([wb_free[0]] if wb_free[0] else []) + [p.tok("SV")], inc="SV")
                tok_stage = p.tok("SV")
                for k in range(8):
                    wb_free[0] = p.op("d", lambda e, q=q, sn=scr_name, k=k: e.dma_start(
                        out=scr[sn][q, k], in_=wb[0][:, k * 1024:(k + 1) * 1024]),
                        waits=([tok_stage] if k == 0 else []), inc="SW")

        prep_ternary(fw1t, "f1", 0)
        prep_ternary(fw2t, "f2", 1)

        dump("bqr", bqr[:], [1, 1024], BF16, waits=[p.tok("SV"), p.tok("ST")])
        dump("swc", swc[:], [128, 4], F32, waits=[p.tok("SV")])
        dump("scrf1", scr["f1"][0, 0], [128, 1024], BF16, waits=[p.tok("SW")])
        dump("scrq", scr["q"][0, 0], [128, 1024], BF16, waits=[p.tok("SW")])
        dump("scrqall", scr["q"][0].rearrange("k p f -> (k p) f"), [1024, 1024], BF16,
             waits=[p.tok("SW")])
        dump("scraw1", scr["aw1"][:].rearrange("q k p f -> (q k p) f"), [4096, 1024], BF16,
             waits=[p.tok("SW")])
        dump("baw1", baw1[:], [1, 4096], BF16, waits=[p.tok("SV"), p.tok("ST")])

        gslot = {"i": 0}

        def load_slot(scr_name, q):
            i = gslot["i"] % 2
            gslot["i"] += 1
            tok = None
            for k in range(8):
                tok = p.op("d", lambda e, sn=scr_name, q=q, i=i, k=k: e.dma_start(
                    out=wb[i][:, k * 1024:(k + 1) * 1024], in_=scr[sn][q, k]),
                    waits=([wb_free[i]] if (k == 0 and wb_free[i]) else []), inc="SW")
            wb_ready[i] = tok
            return i

        def done_slot(i, tok):
            wb_free[i] = tok

        stats_guard = {"tok": None}

        def ln_stats(xap, site, waits, eps=1e-5):
            b = SC[site]
            gw = [stats_guard["tok"]] if stats_guard["tok"] else []
            for c in range(NCHUNK):
                p.op("s", lambda e, c=c, b=b, xap=xap: e.activation(
                    out=sqd[:], in_=xap[:, c * 1024:(c + 1) * 1024], func=AF.Square,
                    accum_out=stats[:, b + 2 + c:b + 3 + c]),
                    waits=list(waits) + gw, inc="SS")
                p.op("v", lambda e, c=c, b=b, xap=xap: e.reduce_sum(
                    out=stats[:, b + c:b + 1 + c], in_=xap[:, c * 1024:(c + 1) * 1024],
                    axis=AX.X), waits=waits, inc="SV")
            tok_s = p.tok("SS")
            tok_v = p.tok("SV")
            p.op("v", lambda e, b=b: e.tensor_scalar(
                out=stats[:, b + 4:b + 6], in0=stats[:, b:b + 2],
                scalar1=1.0 / D, scalar2=None, op0=ALU.mult),
                waits=[tok_v, p.tok("SV")], inc="SV")
            p.op("v", lambda e, b=b: e.tensor_scalar(
                out=stats[:, b + 6:b + 8], in0=stats[:, b + 2:b + 4],
                scalar1=1.0 / D, scalar2=None, op0=ALU.mult),
                waits=[tok_s, p.tok("SV")], inc="SV")
            p.op("v", lambda e, b=b: e.tensor_tensor(
                out=sst[:, 0:2], in0=stats[:, b + 4:b + 6], in1=stats[:, b + 4:b + 6],
                op=ALU.mult), waits=[p.tok("SV")], inc="SV")
            p.op("v", lambda e, b=b: e.tensor_tensor(
                out=sst[:, 28:30], in0=stats[:, b + 6:b + 8], in1=sst[:, 0:2],
                op=ALU.subtract), waits=[p.tok("SV")], inc="SV")
            p.op("v", lambda e, b=b, eps=eps: e.tensor_scalar(
                out=stats[:, b + 6:b + 8], in0=sst[:, 28:30],
                scalar1=eps, scalar2=None, op0=ALU.add), waits=[p.tok("SV")], inc="SV")
            tok_var = p.tok("SV")
            p.table(E_SET)
            p.op("s", lambda e, b=b: e.activation(out=sst[:, 2:4], in_=stats[:, b + 6:b + 8],
                                                  func=AF.Ln),
                 waits=[tok_var], inc="SS")
            p.op("s", lambda e, b=b: e.activation(out=stats[:, b + 6:b + 8], in_=sst[:, 2:4],
                                                  func=AF.Exp, scale=-0.5),
                 waits=[p.tok("SS")], inc="SS")
            return p.tok("SS")

        def ln_apply(xap, site, outap, tok_rstd, waits=()):
            b = SC[site]
            for c in range(NCHUNK):
                p.op("v", lambda e, c=c, b=b, xap=xap, outap=outap: e.tensor_scalar(
                    out=outap[:, c * 1024:(c + 1) * 1024],
                    in0=xap[:, c * 1024:(c + 1) * 1024],
                    scalar1=stats[:, b + 4 + c:b + 5 + c],
                    scalar2=stats[:, b + 6 + c:b + 7 + c],
                    op0=ALU.subtract, op1=ALU.mult),
                    waits=list(waits) + [tok_rstd, p.tok("SV")], inc="SV")
            return p.tok("SV")

        def transpose_2k(src_f32, tok_src, dst_bf):
            for w in range(4):
                bank = pmm[w % 2]
                for i in range(4):
                    d = w * 4 + i
                    c, j = d % 2, d // 2
                    p.op("t", lambda e, c=c, j=j, i=i, bank=bank, src_f32=src_f32: e.transpose(
                        out=bank[:, i * 128:(i + 1) * 128],
                        in_=src_f32[:, c * 1024 + j * 128: c * 1024 + (j + 1) * 128],
                        identity=Idf[:]),
                        waits=([tok_src, tok_consts] +
                               ([pmm_free[w % 2]] if pmm_free[w % 2] else [])) if i == 0 else [],
                        inc=("ST" if i == 3 else None))
                tok_t = p.tok("ST")
                p.op("v", lambda e, w=w, bank=bank, dst_bf=dst_bf: e.tensor_copy(
                    out=dst_bf[:, w * 512:(w + 1) * 512], in_=bank[:]),
                    waits=[tok_t], inc="SV")
                pmm_free[w % 2] = p.tok("SV")
            return p.tok("SV")

        def proj_tm(wslot_i, tok_w, tok_hT, bias_row, tok_b, out_f32=None, out_vc=False):
            last = None
            for c in range(NCHUNK):
                for n in range(2):
                    bi = (c * 2 + n) % 2
                    bank = pmm[bi]
                    p.op("t", lambda e, n=n, bank=bank, br=bias_row: e.matmul(
                        out=bank[:], lhsT=Onc[:], rhs=br[0:1, n * 512:(n + 1) * 512],
                        start=True, stop=False),
                        waits=[tok_w, tok_hT, tok_b, tok_consts] +
                              ([pmm_free[bi]] if pmm_free[bi] else []))
                    for k in range(8):
                        p.op("t", lambda e, c=c, n=n, k=k, i=wslot_i, bank=bank: e.matmul(
                            out=bank[:], lhsT=hT[:, (k * 2 + c) * 128:(k * 2 + c + 1) * 128],
                            rhs=wb[i][:, k * 1024 + n * 512: k * 1024 + (n + 1) * 512],
                            start=False, stop=(k == 7)), inc=("ST" if k == 7 else None))
                    tok_mm = p.tok("ST")
                    if out_vc:
                        vview = Vc[:].rearrange("p (cc hh s) -> p cc hh s", cc=2, hh=16)[:, c, n * 8:(n + 1) * 8, 0:64]
                        pview = bank[:].rearrange("p (hh s) -> p hh s", hh=8)
                        p.op("v", lambda e, vv=vview, pv=pview: e.tensor_copy(out=vv, in_=pv),
                             waits=[tok_mm, tok_memset], inc="SV")
                    else:
                        p.op("v", lambda e, c=c, n=n, bank=bank, of=out_f32: e.tensor_copy(
                            out=of[:, c * 1024 + n * 512: c * 1024 + (n + 1) * 512],
                            in_=bank[:]), waits=[tok_mm], inc="SV")
                    pmm_free[bi] = p.tok("SV")
                    last = p.tok("SV")
            return last

        def headT(src_f32, tok_src, dst_bf64):
            for w in range(8):
                bank = pmm[w % 2]
                for i in range(4):
                    d = w * 4 + i
                    h, c = d // 2, d % 2
                    p.op("t", lambda e, h=h, c=c, i=i, bank=bank, src_f32=src_f32: e.transpose(
                        out=bank[0:64, i * 128:(i + 1) * 128],
                        in_=src_f32[:, c * 1024 + h * 64: c * 1024 + (h + 1) * 64],
                        identity=Idf[:]),
                        waits=([tok_src, tok_consts] +
                               ([pmm_free[w % 2]] if pmm_free[w % 2] else [])) if i == 0 else [],
                        inc=("ST" if i == 3 else None))
                tok_t = p.tok("ST")
                p.op("v", lambda e, w=w, bank=bank, dst=dst_bf64: e.tensor_copy(
                    out=dst[:, w * 512:(w + 1) * 512], in_=bank[0:64, :]),
                    waits=[tok_t], inc="SV")
                pmm_free[w % 2] = p.tok("SV")
            return p.tok("SV")

        def phi(src64, tok_src, dst64):
            p.table(E_SET)
            for w in range(8):
                bi = 2 + (w % 2)
                bank = pmm[bi]
                for i in range(2):
                    h = w * 2 + i
                    p.op("t", lambda e, h=h, i=i, bank=bank, src64=src64: e.matmul(
                        out=bank[0:64, i * 256:(i + 1) * 256], lhsT=projb[:],
                        rhs=src64[:, h * 256:(h + 1) * 256], start=True, stop=True),
                        waits=([tok_src, tok_projb] +
                               ([pmm_free[bi]] if pmm_free[bi] else [])) if i == 0 else [],
                        inc=("ST" if i == 1 else None))
                tok_mm = p.tok("ST")
                p.op("v", lambda e, bank=bank: e.tensor_scalar(
                    out=e1[0:64, :], in0=bank[0:64, :], scalar1=0.0, scalar2=None,
                    op0=ALU.min), waits=[tok_mm, p.tok("SP")], inc="SV")
                tok_min = p.tok("SV")
                p.op("v", lambda e, bank=bank: e.tensor_scalar(
                    out=e2[0:64, :], in0=bank[0:64, :], scalar1=0.0, scalar2=None,
                    op0=ALU.max), inc="SV")
                pmm_free[bi] = p.tok("SV")
                tok_max = p.tok("SV")
                p.op("s", lambda e: e.activation(out=e3[0:64, :], in_=e1[0:64, :],
                                                 func=AF.Exp),
                     waits=[tok_min, p.tok("SP")], inc="SS")
                tok_exp = p.tok("SS")
                p.op("g", lambda e, w=w, dst64=dst64: e.tensor_tensor(
                    out=dst64[:, w * 512:(w + 1) * 512], in0=e3[0:64, :], in1=e2[0:64, :],
                    op=ALU.add), waits=[tok_exp, tok_max], inc="SP")
            return p.tok("SP")

        def kc_transposes(tok_kt):
            for c in range(NCHUNK):
                for h in range(16):
                    p.op("t", lambda e, c=c, h=h: e.transpose(
                        out=pbf[:, h * 64:(h + 1) * 64],
                        in_=KT[:, h * 256 + c * 128: h * 256 + (c + 1) * 128],
                        identity=Id[0:64, 0:64]),
                        waits=([tok_kt, tok_consts] +
                               ([pbf_free[0]] if pbf_free[0] else [])) if h == 0 else [],
                        inc=("ST" if h == 15 else None))
                tok_t = p.tok("ST")
                p.op("v", lambda e, c=c: e.tensor_copy(
                    out=Kc[:, c * 1024:(c + 1) * 1024], in_=pbf[:, 0:1024]),
                    waits=[tok_t], inc="SV")
                pbf_free[0] = p.tok("SV")
            return p.tok("SV")

        state_started = {"st0": False, "st1": False, "ksb": False}
        state_ready = {"tok": None}
        stS_read = {"tok": None}

        def state_update(kc_src, tok_kc, tok_vc):
            for c in range(NCHUNK):
                emit_state_chunk(kc_src, c, tok_kc, tok_vc)

        def emit_state_chunk(kc_src, c, tok_kc, tok_vc):
            for h in range(16):
                bank = st0 if h < 8 else st1
                sname = "st0" if h < 8 else "st1"
                first = not state_started[sname]
                state_started[sname] = True
                w = []
                if h == 0:
                    w = [tok_kc, tok_vc, tok_memset]
                    if stS_read["tok"] is not None:
                        w.append(stS_read["tok"])
                p.op("t", lambda e, c=c, h=h, bank=bank, first=first, kc_src=kc_src: e.matmul(
                    out=bank[0:64, (h % 8) * 64:(h % 8 + 1) * 64],
                    lhsT=kc_src[:, (c * 16 + h) * 64:(c * 16 + h + 1) * 64],
                    rhs=Vc[:, c * 1040 + h * 65: c * 1040 + h * 65 + 64],
                    start=first, stop=True), waits=w)
                firstk = not state_started["ksb"]
                state_started["ksb"] = True
                p.op("t", lambda e, c=c, h=h, firstk=firstk, kc_src=kc_src: e.matmul(
                    out=ksb[0:64, h:h + 1],
                    lhsT=kc_src[:, (c * 16 + h) * 64:(c * 16 + h + 1) * 64],
                    rhs=Ontk[:], start=firstk, stop=True),
                    inc=("ST" if h == 15 else None))
            state_ready["tok"] = p.tok("ST")

        tok_x = [None, None]
        pref_xin_used = [None, None]
        slot_kv = [None, None]

        if nprf > 0:
            tok_x[0] = dma_in(xin[0][:].rearrange("p (c dd) -> p c dd", dd=1024),
                              x_pref[0:TPT].rearrange("(c p) dd -> p c dd", p=128))
            slot_kv[0] = load_slot("k", 0)
            slot_kv[1] = load_slot("v", 0)
            tok_wk = wb_ready[slot_kv[0]]
            tok_wv = wb_ready[slot_kv[1]]
            for tile in range(nprf):
                i = tile % 2
                tok_here = tok_x[i]
                if tile + 1 < nprf:
                    w = [pref_xin_used[(tile + 1) % 2]] if pref_xin_used[(tile + 1) % 2] else []
                    tok_x[(tile + 1) % 2] = p.op("d", lambda e, t2=tile + 1: e.dma_start(
                        out=xin[t2 % 2][:].rearrange("p (c dd) -> p c dd", dd=1024),
                        in_=x_pref[t2 * TPT:(t2 + 1) * TPT].rearrange("(c p) dd -> p c dd", p=128)),
                        waits=w, inc="SW")
                xa = xin[i][:]
                tok_rstd = ln_stats(xa, "ln1", [tok_here])
                tok_h = ln_apply(xa, "ln1", hf[:], tok_rstd, waits=[tok_here])
                pref_xin_used[i] = tok_h
                tok_hT = transpose_2k(hf[:], tok_h, hT[:])
                tok_kraw = proj_tm(slot_kv[0], tok_wk, tok_hT, bkr, tok_bias["k"],
                                   out_f32=qkraw)
                tok_v = proj_tm(slot_kv[1], tok_wv, tok_hT, bvr, tok_bias["v"], out_vc=True)
                tok_krt = headT(qkraw, tok_kraw, KRT[:])
                tok_kt = phi(KRT[:], tok_krt, KT[:])
                tok_kc = kc_transposes(tok_kt)
                p.op("v", lambda e: e.tensor_scalar(
                    out=attn_tm[:], in0=Kc[:], scalar1=Mcol[:, 0:1], scalar2=None,
                    op0=ALU.mult), waits=[tok_kc, tok_consts], inc="SV")
                tok_kcm = p.tok("SV")
                if tile == nprf - 1:
                    dump("kcm", attn_tm[:], [128, 2048], BF16, waits=[tok_kcm])
                stats_guard["tok"] = tok_kcm
                state_update(attn_tm[:], tok_kcm, tok_v)
            done_slot(slot_kv[0], state_ready["tok"])
            done_slot(slot_kv[1], state_ready["tok"])

        tok_x = [None, None]
        xin_used = [None, None]
        w0 = [state_ready["tok"]] if state_ready["tok"] else []
        if nprf > 0:
            w0 += [pref_xin_used[0]] if pref_xin_used[0] else []
        tok_x[0] = p.op("d", lambda e: e.dma_start(
            out=xin[0][:].rearrange("p (c dd) -> p c dd", dd=1024),
            in_=x_main[0:TPT].rearrange("(c p) dd -> p c dd", p=128)),
            waits=w0, inc="SW")
        tok_y_dma = None

        for tile in range(nm):
            i = tile % 2
            tok_here = tok_x[i]
            if tile + 1 < nm:
                wprev = []
                if tile == 0 and nprf > 0 and pref_xin_used[1]:
                    wprev = [pref_xin_used[1]]
                elif xin_used[(tile + 1) % 2]:
                    wprev = [xin_used[(tile + 1) % 2]]
                tok_x[(tile + 1) % 2] = p.op("d", lambda e, t2=tile + 1: e.dma_start(
                    out=xin[t2 % 2][:].rearrange("p (c dd) -> p c dd", dd=1024),
                    in_=x_main[t2 * TPT:(t2 + 1) * TPT].rearrange("(c p) dd -> p c dd", p=128)),
                    waits=wprev, inc="SW")
            xa = xin[i][:]

            tok_rstd = ln_stats(xa, "ln1", [tok_here])
            tok_h = ln_apply(xa, "ln1", hf[:], tok_rstd, waits=[tok_here])
            tok_hT = transpose_2k(hf[:], tok_h, hT[:])

            if tile == 0:
                dump("xin", xa, [128, 2048], F32, waits=[tok_here])
                dump("stats", stats[:], [128, 64], F32, waits=[tok_rstd, tok_h])
                dump("hf", hf[:], [128, 2048], F32, waits=[tok_h])
                dump("hT", hT[:], [128, 2048], BF16, waits=[tok_hT])
            sq_ = load_slot("q", 0)
            tok_qraw = proj_tm(sq_, wb_ready[sq_], tok_hT, bqr, tok_bias["q"], out_f32=qkraw)
            if tile == 0:
                dump("qraw", qkraw, [128, 2048], F32, waits=[tok_qraw])
            tok_qrt = headT(qkraw, tok_qraw, QRT[:])
            done_slot(sq_, tok_qrt)
            sk_ = load_slot("k", 0)
            tok_kraw = proj_tm(sk_, wb_ready[sk_], tok_hT, bkr, tok_bias["k"], out_f32=qkraw)
            tok_krt = headT(qkraw, tok_kraw, KRT[:])
            done_slot(sk_, tok_krt)
            sv_ = load_slot("v", 0)
            tok_v = proj_tm(sv_, wb_ready[sv_], tok_hT, bvr, tok_bias["v"], out_vc=True)
            done_slot(sv_, tok_v)
            tok_qt = phi(QRT[:], tok_qrt, QT[:])
            tok_kt = phi(KRT[:], tok_krt, KT[:])
            tok_kc = kc_transposes(tok_kt)

            if tile == 0:
                dump("QT", QT[:], [64, 4096], BF16, waits=[tok_qt])
                dump("KT", KT[:], [64, 4096], BF16, waits=[tok_kt])
                dump("Vc", Vc[:], [128, 2080], BF16, waits=[tok_v])
                dump("Kc", Kc[:], [128, 2048], BF16, waits=[tok_kc])

            for c in range(NCHUNK):
                if state_ready["tok"] is not None:
                    p.op("v", lambda e: e.tensor_copy(out=stS[0:64, 0:512], in_=st0[:]),
                         waits=[state_ready["tok"], tok_memset], inc="SV")
                    p.op("v", lambda e: e.tensor_copy(out=stS[0:64, 512:1024], in_=st1[:]),
                         inc="SV")
                    p.op("v", lambda e: e.tensor_copy(out=ksS[:], in_=ksb[:]), inc="SV")
                    stS_read["tok"] = p.tok("SV")
                tok_st_copy = p.tok("SV")
                if tile == 0 and c == 0:
                    dump("stS0", stS[:], [64, 1024], BF16, waits=[tok_st_copy])
                    dump("ksS0", ksS[:], [64, 16], BF16, waits=[tok_st_copy])
                    dump("mcol", Mcol[:], [128, 1], F32, waits=[tok_consts])
                for grp in range(4):
                    sbank = pmm[grp % 2]
                    nbank = pmm[2 + grp % 2]
                    for ii in range(4):
                        h = grp * 4 + ii
                        p.op("t", lambda e, h=h, ii=ii, c=c, sbank=sbank: e.matmul(
                            out=sbank[:, ii * 128:(ii + 1) * 128],
                            lhsT=KT[0:64, h * 256 + c * 128: h * 256 + (c + 1) * 128],
                            rhs=QT[0:64, h * 256 + c * 128: h * 256 + (c + 1) * 128],
                            start=True, stop=True),
                            waits=([tok_qt, tok_kt] +
                                   ([pmm_free[grp % 2]] if pmm_free[grp % 2] else [])) if ii == 0 else [],
                            inc=("ST" if ii == 3 else None))
                    tok_sc = p.tok("ST")
                    p.op("v", lambda e, grp=grp, sbank=sbank: e.tensor_tensor(
                        out=Ssb[:, grp * 512:(grp + 1) * 512], in0=sbank[:], in1=McT[:],
                        op=ALU.mult), waits=[tok_sc, tok_consts], inc="SV")
                    tok_mask = p.tok("SV")
                    pmm_free[grp % 2] = tok_mask
                    for ii in range(4):
                        h = grp * 4 + ii
                        p.op("t", lambda e, h=h, ii=ii, c=c, nbank=nbank: e.matmul(
                            out=nbank[:, ii * 128:ii * 128 + 65],
                            lhsT=Ssb[:, h * 128:(h + 1) * 128],
                            rhs=Vc[:, c * 1040 + h * 65: c * 1040 + (h + 1) * 65],
                            start=True, stop=False),
                            waits=([tok_mask, tok_v, tok_st_copy, tok_memset] +
                                   ([pmm_free[2 + grp % 2]] if pmm_free[2 + grp % 2] else [])) if ii == 0 else [])
                        p.op("t", lambda e, h=h, ii=ii, c=c, nbank=nbank: e.matmul(
                            out=nbank[:, ii * 128:ii * 128 + 64],
                            lhsT=QT[0:64, h * 256 + c * 128: h * 256 + (c + 1) * 128],
                            rhs=stS[0:64, h * 64:(h + 1) * 64], start=False, stop=False))
                        p.op("t", lambda e, h=h, ii=ii, c=c, nbank=nbank: e.matmul(
                            out=nbank[:, ii * 128 + 64:ii * 128 + 65],
                            lhsT=QT[0:64, h * 256 + c * 128: h * 256 + (c + 1) * 128],
                            rhs=ksS[0:64, h:h + 1], start=False, stop=True),
                            inc=("ST" if ii == 3 else None))
                    tok_n = p.tok("ST")
                    dview = nbank[:].rearrange("p (hh s) -> p hh s", hh=4)[:, :, 64:65]
                    p.op("v", lambda e, grp=grp, dv=dview: e.tensor_scalar(
                        out=dens[:, grp * 4:(grp + 1) * 4], in0=dv,
                        scalar1=1e-6, scalar2=None, op0=ALU.add),
                        waits=[tok_n, p.tok("SV")], inc="SV")
                    p.op("v", lambda e, grp=grp: e.reciprocal(
                        out=recips[:, grp * 4:(grp + 1) * 4],
                        in_=dens[:, grp * 4:(grp + 1) * 4]),
                        waits=[p.tok("SV")], inc="SV")
                    for ii in range(4):
                        h = grp * 4 + ii
                        p.op("v", lambda e, h=h, ii=ii, c=c, nbank=nbank: e.tensor_scalar(
                            out=attn_tm[:, c * 1024 + h * 64: c * 1024 + (h + 1) * 64],
                            in0=nbank[:, ii * 128:ii * 128 + 64],
                            scalar1=recips[:, h:h + 1], scalar2=None, op0=ALU.mult),
                            waits=([p.tok("SV")] if ii == 0 else []),
                            inc=("SV" if ii == 3 else None))
                    pmm_free[2 + grp % 2] = p.tok("SV")
                tok_div = p.tok("SV")
                emit_state_chunk(Kc[:], c, tok_kc, tok_v)
                for j in range(8):
                    p.op("t", lambda e, c=c, j=j: e.transpose(
                        out=pbf[:, j * 128:(j + 1) * 128],
                        in_=attn_tm[:, c * 1024 + j * 128: c * 1024 + (j + 1) * 128],
                        identity=Id[:]),
                        waits=([tok_div] + ([pbf_free[0]] if pbf_free[0] else [])) if j == 0 else [],
                        inc=("ST" if j == 7 else None))
                tok_at = p.tok("ST")
                p.op("v", lambda e, c=c: e.tensor_copy(
                    out=attnT[:, c * 1024:(c + 1) * 1024], in_=pbf[:]),
                    waits=[tok_at], inc="SV")
                pbf_free[0] = p.tok("SV")
            tok_attnT = p.tok("SV")

            if tile == 0:
                dump("attn_tm", attn_tm[:], [128, 2048], BF16, waits=[tok_attnT])

            so_ = load_slot("o", 0)
            for c in range(NCHUNK):
                for n in range(2):
                    bi = (c * 2 + n) % 2
                    bank = pmm[bi]
                    for k in range(8):
                        p.op("t", lambda e, c=c, n=n, k=k, i=so_, bank=bank: e.matmul(
                            out=bank[:],
                            lhsT=attnT[:, c * 1024 + k * 128: c * 1024 + (k + 1) * 128],
                            rhs=wb[i][:, k * 1024 + n * 512: k * 1024 + (n + 1) * 512],
                            start=(k == 0), stop=(k == 7)),
                            waits=([tok_attnT, wb_ready[so_]] +
                                   ([pmm_free[bi]] if pmm_free[bi] else [])) if k == 0 else [],
                            inc=("ST" if k == 7 else None))
                    tok_mm = p.tok("ST")
                    p.op("v", lambda e, c=c, n=n, bank=bank, xa=xa: e.tensor_tensor(
                        out=xr0[:, c * 1024 + n * 512: c * 1024 + (n + 1) * 512],
                        in0=bank[:], in1=xa[:, c * 1024 + n * 512: c * 1024 + (n + 1) * 512],
                        op=ALU.add), waits=[tok_mm, tok_here], inc="SV")
                    pmm_free[bi] = p.tok("SV")
            tok_x2 = p.tok("SV")
            xin_used[i] = tok_x2
            done_slot(so_, tok_x2)

            if tile == 0:
                dump("x2", xr0[:], [128, 2048], F32, waits=[tok_x2])

            tok_rstd2 = ln_stats(xr0[:], "ln2", [tok_x2])
            tok_h2 = ln_apply(xr0[:], "ln2", hf[:], tok_rstd2, waits=[tok_x2])
            if tile == 0:
                dump("h2f", hf[:], [128, 2048], F32, waits=[tok_h2])
            tok_h2T = transpose_2k(hf[:], tok_h2, hT[:])
            if tile == 0:
                dump("h2T", hT[:], [128, 2048], BF16, waits=[tok_h2T])

            p.table(G_SET)
            for qf in range(4):
                sa = load_slot("aw1", qf)
                for m in range(8):
                    fc = qf * 8 + m
                    bi = m % 2
                    bank = pmm[bi]
                    p.op("t", lambda e, fc=fc, bank=bank: e.matmul(
                        out=bank[:, 0:256], lhsT=baw1[0:1, fc * 128:(fc + 1) * 128],
                        rhs=Onr[0:1, 0:256], start=True, stop=False),
                        waits=[wb_ready[sa], tok_h2T, tok_bias["aw1"], tok_consts] +
                              ([pmm_free[bi]] if pmm_free[bi] else []))
                    for k in range(8):
                        p.op("t", lambda e, m=m, k=k, i=sa, bank=bank: e.matmul(
                            out=bank[:, 0:256],
                            lhsT=wb[i][:, k * 1024 + m * 128: k * 1024 + (m + 1) * 128],
                            rhs=hT[:, k * 256:(k + 1) * 256], start=False, stop=(k == 7)),
                            inc=("ST" if k == 7 else None))
                    tok_mm = p.tok("ST")
                    p.op("s", lambda e, fc=fc, bank=bank: e.activation(
                        out=garena[:, fc * 256:(fc + 1) * 256], in_=bank[:, 0:256],
                        func=AF.Gelu), waits=[tok_mm], inc="SS")
                    pmm_free[bi] = p.tok("SS")
                done_slot(sa, p.tok("SS"))
            tok_g = p.tok("SS")

            if tile == 0:
                dump("gattn", garena[:], [128, 8192], BF16, waits=[tok_g])

            sa2 = [None] * 4
            for kq in range(4):
                sa2[kq] = load_slot("aw2", kq)
                for cn in range(4):
                    c, n = cn // 2, cn % 2
                    bank = pmm[cn]
                    if kq == 0:
                        p.op("t", lambda e, n=n, bank=bank: e.matmul(
                            out=bank[:], lhsT=Onc[:], rhs=ab2b[0:1, n * 512:(n + 1) * 512],
                            start=True, stop=False),
                            waits=[wb_ready[sa2[0]], tok_g, tok_smallcasts, tok_consts] +
                                  ([pmm_free[cn]] if pmm_free[cn] else []))
                    for k8 in range(8):
                        fc = kq * 8 + k8
                        p.op("t", lambda e, fc=fc, c=c, n=n, k8=k8, i=sa2[kq], bank=bank: e.matmul(
                            out=bank[:],
                            lhsT=garena[:, fc * 256 + c * 128: fc * 256 + (c + 1) * 128],
                            rhs=wb[i][:, k8 * 1024 + n * 512: k8 * 1024 + (n + 1) * 512],
                            start=False, stop=(fc == 31)),
                            waits=([wb_ready[sa2[kq]]] if (k8 == 0 and cn == 0 and kq > 0) else []),
                            inc=("ST" if (cn == 3 and k8 == 7) else None))
                tok_kq = p.tok("ST")
                done_slot(sa2[kq], tok_kq)
            tok_mm2_all = p.tok("ST")
            for cn in range(4):
                c, n = cn // 2, cn % 2
                bank = pmm[cn]
                p.op("v", lambda e, c=c, n=n, bank=bank: e.tensor_tensor(
                    out=e1[:, 0:512], in0=bank[:],
                    in1=xr0[:, c * 1024 + n * 512: c * 1024 + (n + 1) * 512],
                    op=ALU.add), waits=[tok_mm2_all, p.tok("SV")], inc="SV")
                p.op("v", lambda e, c=c, n=n, xa=xa: e.tensor_tensor(
                    out=xr1[:, c * 1024 + n * 512: c * 1024 + (n + 1) * 512],
                    in0=e1[:, 0:512],
                    in1=xa[:, c * 1024 + n * 512: c * 1024 + (n + 1) * 512],
                    op=ALU.add), waits=[p.tok("SV")], inc="SV")
                pmm_free[cn] = p.tok("SV")
            tok_x3 = p.tok("SV")
            xin_used[i] = tok_x3

            tok_rstd3 = ln_stats(xr1[:], "oln", [tok_x3])
            tok_y1 = ln_apply(xr1[:], "oln", xr0[:], tok_rstd3, waits=[tok_x3])
            for c in range(NCHUNK):
                p.op("v", lambda e, c=c: e.reduce_max(
                    out=stats[:, 32 + c:33 + c], in_=xr0[:, c * 1024:(c + 1) * 1024],
                    axis=AX.X, apply_absolute_value=True), waits=[tok_y1], inc="SV")
            tok_maxy = p.tok("SV")

            if tile == 0:
                dump("y1", xr0[:], [128, 2048], F32, waits=[tok_y1])

            b3 = SC["oln"]
            p.op("v", lambda e, b3=b3: e.tensor_tensor(
                out=sst[:, 4:6], in0=stats[:, b3 + 6:b3 + 8], in1=stats[:, b3 + 6:b3 + 8],
                op=ALU.mult), waits=[tok_y1, p.tok("SV")], inc="SV")
            p.op("v", lambda e: e.tensor_scalar(
                out=stats[:, 36:38], in0=sst[:, 4:6], scalar1=-1e-5, scalar2=1.0 + 1e-5,
                op0=ALU.mult, op1=ALU.add), waits=[p.tok("SV")], inc="SV")
            tok_msq = p.tok("SV")
            p.table(E_SET)
            p.op("s", lambda e: e.activation(out=sst[:, 6:8], in_=stats[:, 36:38],
                                             func=AF.Ln),
                 waits=[tok_msq] + ([stats_guard["tok"]] if stats_guard["tok"] else []),
                 inc="SS")
            p.op("s", lambda e: e.activation(out=stats[:, 38:40], in_=sst[:, 6:8],
                                             func=AF.Exp, scale=-0.5),
                 waits=[p.tok("SS")], inc="SS")
            tok_rms2 = p.tok("SS")
            p.op("v", lambda e: e.tensor_tensor(out=sst[:, 8:10], in0=stats[:, 38:40],
                                                in1=stats[:, 32:34], op=ALU.mult),
                 waits=[tok_rms2, tok_maxy, p.tok("SV")], inc="SV")
            p.op("v", lambda e: e.tensor_scalar(out=sst[:, 30:32], in0=sst[:, 8:10],
                                                scalar1=1e-5, scalar2=None, op0=ALU.max),
                 waits=[p.tok("SV")], inc="SV")
            p.op("v", lambda e: e.reciprocal(out=sst[:, 10:12], in_=sst[:, 30:32]),
                 waits=[p.tok("SV")], inc="SV")
            p.op("v", lambda e: e.tensor_scalar(out=sst[:, 26:28], in0=sst[:, 10:12],
                                                scalar1=127.0, scalar2=None, op0=ALU.mult),
                 waits=[p.tok("SV")], inc="SV")
            p.op("v", lambda e: e.tensor_tensor(out=sst[:, 12:14], in0=stats[:, 38:40],
                                                in1=sst[:, 26:28], op=ALU.mult),
                 waits=[p.tok("SV")], inc="SV")
            p.op("v", lambda e: e.tensor_scalar(out=sst[:, 8:10], in0=sst[:, 30:32],
                                                scalar1=1.0 / 127.0, scalar2=None,
                                                op0=ALU.mult), waits=[p.tok("SV")], inc="SV")
            p.op("v", lambda e: e.tensor_scalar(out=sst[:, 14:16], in0=sst[:, 8:10],
                                                scalar1=swc[:, 0:1], scalar2=None,
                                                op0=ALU.mult), waits=[p.tok("SV")], inc="SV")
            tok_qs = p.tok("SV")
            for c in range(NCHUNK):
                p.op("v", lambda e, c=c: e.tensor_scalar(
                    out=i8t[:, c * 1024:(c + 1) * 1024], in0=xr0[:, c * 1024:(c + 1) * 1024],
                    scalar1=sst[:, 12 + c:13 + c], scalar2=None, op0=ALU.mult),
                    waits=[tok_qs, p.tok("SV")], inc="SV")
                p.op("v", lambda e, c=c: e.tensor_scalar(
                    out=xqb[:, c * 1024:(c + 1) * 1024], in0=i8t[:, c * 1024:(c + 1) * 1024],
                    scalar1=sst[:, 14 + c:15 + c], scalar2=None, op0=ALU.mult),
                    waits=[p.tok("SV")], inc="SV")
            tok_xq = p.tok("SV")
            if tile == 0:
                dump("xqb", xqb[:], [128, 2048], BF16, waits=[tok_xq])
            for c in range(NCHUNK):
                for j in range(8):
                    p.op("t", lambda e, c=c, j=j: e.transpose(
                        out=pbf[:, j * 128:(j + 1) * 128],
                        in_=xqb[:, c * 1024 + j * 128: c * 1024 + (j + 1) * 128],
                        identity=Id[:]),
                        waits=([tok_xq] + ([pbf_free[0]] if pbf_free[0] else [])) if j == 0 else [],
                        inc=("ST" if j == 7 else None))
                tok_tt = p.tok("ST")
                hview = hT[:].rearrange("p (j cc f) -> p j cc f", j=8, cc=2)[:, :, c, :]
                pview = pbf[:].rearrange("p (j f) -> p j f", j=8)
                p.op("v", lambda e, hv=hview, pv=pview: e.tensor_copy(out=hv, in_=pv),
                     waits=[tok_tt], inc="SV")
                pbf_free[0] = p.tok("SV")
            tok_xqT = p.tok("SV")

            p.table(G_SET)
            for qf in range(4):
                sf = load_slot("f1", qf)
                for m in range(8):
                    fc = qf * 8 + m
                    bi = m % 2
                    bank = pmm[bi]
                    p.op("t", lambda e, fc=fc, bank=bank: e.matmul(
                        out=bank[:, 0:256], lhsT=fb1b[0:1, fc * 128:(fc + 1) * 128],
                        rhs=Onr[0:1, 0:256], start=True, stop=False),
                        waits=[wb_ready[sf], tok_xqT, tok_smallcasts, tok_consts] +
                              ([pmm_free[bi]] if pmm_free[bi] else []))
                    for k in range(8):
                        p.op("t", lambda e, m=m, k=k, i=sf, bank=bank: e.matmul(
                            out=bank[:, 0:256],
                            lhsT=wb[i][:, k * 1024 + m * 128: k * 1024 + (m + 1) * 128],
                            rhs=hT[:, k * 256:(k + 1) * 256], start=False, stop=(k == 7)),
                            inc=("ST" if k == 7 else None))
                    tok_mm = p.tok("ST")
                    p.op("s", lambda e, fc=fc, bank=bank: e.activation(
                        out=garena[:, fc * 256:(fc + 1) * 256], in_=bank[:, 0:256],
                        func=AF.Gelu), waits=[tok_mm], inc="SS")
                    pmm_free[bi] = p.tok("SS")
                done_slot(sf, p.tok("SS"))
            tok_g2 = p.tok("SS")
            if tile == 0:
                dump("g2", garena[:], [128, 8192], BF16, waits=[tok_g2])

            for fc in range(32):
                sb_ = sqb[fc % 2]
                p.op("s", lambda e, fc=fc, sb_=sb_: e.activation(
                    out=sb_[:], in_=garena[:, fc * 256:(fc + 1) * 256], func=AF.Square),
                    waits=([tok_g2] if fc < 2 else [p.tok("ST"), p.tok("SV")]), inc="SS")
                tok_sq = p.tok("SS")
                p.op("t", lambda e, fc=fc, sb_=sb_: e.matmul(
                    out=pmm[3][0:1, 0:256], lhsT=Ontk[:], rhs=sb_[:],
                    start=(fc == 0), stop=(fc == 31)),
                    waits=[tok_sq, tok_consts] +
                          (([pmm_free[3]] if pmm_free[3] else []) if fc == 0 else []),
                    inc="ST")
                if fc == 0:
                    p.op("g", lambda e, sb_=sb_: e.tensor_copy(out=gmb[0][:], in_=sb_[:]),
                         waits=[tok_sq], inc="SP")
                else:
                    p.op("v", lambda e, fc=fc, sb_=sb_: e.tensor_tensor(
                        out=gmb[fc % 2][:], in0=gmb[(fc + 1) % 2][:], in1=sb_[:],
                        op=ALU.max), waits=[tok_sq, p.tok("SP"), p.tok("SV")], inc="SV")
            tok_ssq = p.tok("ST")
            gm_fin = gmb[31 % 2]
            tok_gm = [p.tok("SV"), p.tok("SP")]
            for c in range(NCHUNK):
                p.op("t", lambda e, c=c, gm_fin=gm_fin: e.transpose(
                    out=pbf[:, c * 128:(c + 1) * 128], in_=gm_fin[:, c * 128:(c + 1) * 128],
                    identity=Id[:]),
                    waits=(tok_gm + [tok_consts] + ([pbf_free[0]] if pbf_free[0] else [])) if c == 0 else [],
                    inc=("ST" if c == 1 else None))
            tok_gt = p.tok("ST")
            for c in range(NCHUNK):
                p.op("v", lambda e, c=c: e.reduce_max(
                    out=stats[:, 52 + c:53 + c], in_=pbf[:, c * 128:(c + 1) * 128],
                    axis=AX.X), waits=[tok_gt], inc="SV")
            pbf_free[0] = p.tok("SV")
            tok_gmaxsq = p.tok("SV")
            p.table(E_SET)
            p.op("s", lambda e: e.activation(out=sst[:, 26:28], in_=stats[:, 52:54],
                                             func=AF.Ln),
                 waits=[tok_gmaxsq] + ([stats_guard["tok"]] if stats_guard["tok"] else []),
                 inc="SS")
            p.op("s", lambda e: e.activation(out=stats[:, 34:36], in_=sst[:, 26:28],
                                             func=AF.Exp, scale=0.5),
                 waits=[p.tok("SS")], inc="SS")
            tok_gmax = p.tok("SS")
            p.op("v", lambda e: e.tensor_copy(out=rowf[0:1, 0:256], in_=pmm[3][0:1, 0:256]),
                 waits=[tok_ssq], inc="SV")
            pmm_free[3] = p.tok("SV")
            tok_row = p.tok("SV")
            for c in range(NCHUNK):
                p.op("t", lambda e, c=c: e.matmul(
                    out=pmm[3][:, c:c + 1].bitcast(F32), lhsT=rowf[0:1, c * 128:(c + 1) * 128],
                    rhs=Oncf[0:1, 0:1], start=True, stop=True),
                    waits=([tok_row, tok_consts] + ([pmm_free[3]] if pmm_free[3] else [])) if c == 0 else [],
                    inc=("ST" if c == 1 else None))
            tok_ssqT = p.tok("ST")
            p.op("v", lambda e: e.tensor_copy(out=stats[:, 44:46], in_=pmm[3][:, 0:2]),
                 waits=[tok_ssqT], inc="SV")
            pmm_free[3] = p.tok("SV")
            p.op("v", lambda e: e.tensor_scalar(out=stats[:, 42:44], in0=stats[:, 44:46],
                                                scalar1=1.0 / FF, scalar2=1e-5,
                                                op0=ALU.mult, op1=ALU.add),
                 waits=[p.tok("SV")], inc="SV")
            tok_ms2 = p.tok("SV")
            p.table(E_SET)
            p.op("s", lambda e: e.activation(out=sst[:, 16:18], in_=stats[:, 42:44],
                                             func=AF.Ln),
                 waits=[tok_ms2] + ([stats_guard["tok"]] if stats_guard["tok"] else []),
                 inc="SS")
            p.op("s", lambda e: e.activation(out=stats[:, 46:48], in_=sst[:, 16:18],
                                             func=AF.Exp, scale=-0.5),
                 waits=[p.tok("SS")], inc="SS")
            tok_rmsg = p.tok("SS")
            p.op("v", lambda e: e.tensor_tensor(out=sst[:, 18:20], in0=stats[:, 46:48],
                                                in1=stats[:, 34:36], op=ALU.mult),
                 waits=[tok_rmsg, tok_gmax, p.tok("SV")], inc="SV")
            p.op("v", lambda e: e.tensor_scalar(out=sst[:, 30:32], in0=sst[:, 18:20],
                                                scalar1=1e-5, scalar2=None, op0=ALU.max),
                 waits=[p.tok("SV")], inc="SV")
            p.op("v", lambda e: e.reciprocal(out=sst[:, 20:22], in_=sst[:, 30:32]),
                 waits=[p.tok("SV")], inc="SV")
            p.op("v", lambda e: e.tensor_scalar(out=sst[:, 16:18], in0=sst[:, 20:22],
                                                scalar1=127.0, scalar2=None, op0=ALU.mult),
                 waits=[p.tok("SV")], inc="SV")
            p.op("v", lambda e: e.tensor_tensor(out=sst[:, 22:24], in0=stats[:, 46:48],
                                                in1=sst[:, 16:18], op=ALU.mult),
                 waits=[p.tok("SV")], inc="SV")
            p.op("v", lambda e: e.tensor_scalar(out=sst[:, 18:20], in0=sst[:, 30:32],
                                                scalar1=1.0 / 127.0, scalar2=None,
                                                op0=ALU.mult), waits=[p.tok("SV")], inc="SV")
            p.op("v", lambda e: e.tensor_scalar(out=sst[:, 24:26], in0=sst[:, 18:20],
                                                scalar1=swc[:, 1:2], scalar2=None,
                                                op0=ALU.mult), waits=[p.tok("SV")], inc="SV")
            tok_qs2 = p.tok("SV")
            for c in range(NCHUNK):
                p.op("t", lambda e, c=c: e.transpose(
                    out=pmm[3][0:1, c * 128:(c + 1) * 128].bitcast(F32),
                    in_=sst[:, 22 + c:23 + c], identity=Idf[:]),
                    waits=([tok_qs2, tok_consts] + ([pmm_free[3]] if pmm_free[3] else [])) if c == 0 else [],
                    inc=("ST" if c == 1 else None))
            tok_mrT = p.tok("ST")
            p.op("v", lambda e: e.tensor_copy(out=mrow[:], in_=pmm[3][0:1, 0:256]),
                 waits=[tok_mrT], inc="SV")
            pmm_free[3] = p.tok("SV")
            tok_mrow = p.tok("SV")
            p.op("t", lambda e: e.matmul(out=pmm[3][:, 0:256], lhsT=Onc[:], rhs=mrow[:],
                                         start=True, stop=True),
                 waits=[tok_mrow] + ([pmm_free[3]] if pmm_free[3] else []), inc="ST")
            tok_B = p.tok("ST")
            p.op("v", lambda e: e.tensor_copy(out=Bsb[:], in_=pmm[3][:, 0:256]),
                 waits=[tok_B], inc="SV")
            pmm_free[3] = p.tok("SV")
            tok_Bsb = p.tok("SV")
            for fc in range(32):
                ib = i8g[fc % 2]
                p.op("v", lambda e, fc=fc, ib=ib: e.tensor_tensor(
                    out=ib[:], in0=garena[:, fc * 256:(fc + 1) * 256], in1=Bsb[:],
                    op=ALU.mult),
                    waits=[tok_Bsb] + ([p.tok("SP")] if fc >= 2 else []), inc="SV")
                tok_i8 = p.tok("SV")
                p.op("g", lambda e, fc=fc, ib=ib: e.tensor_copy(
                    out=ints2[:, fc * 256:(fc + 1) * 256], in_=ib[:]),
                    waits=[tok_i8], inc="SP")
            tok_ints2 = p.tok("SP")
            if tile == 0:
                dump("ints2", ints2[:], [128, 8192], BF16, waits=[tok_ints2])

            sf2 = [None] * 4
            for kq in range(4):
                sf2[kq] = load_slot("f2", kq)
                for cn in range(4):
                    c, n = cn // 2, cn % 2
                    bank = pmm[cn]
                    for k8 in range(8):
                        fc = kq * 8 + k8
                        w = []
                        if k8 == 0 and cn == 0:
                            w = [wb_ready[sf2[kq]], tok_ints2]
                        if kq == 0 and k8 == 0:
                            w = [wb_ready[sf2[0]], tok_ints2] + \
                                ([pmm_free[cn]] if pmm_free[cn] else [])
                        p.op("t", lambda e, fc=fc, c=c, n=n, k8=k8, i=sf2[kq], bank=bank: e.matmul(
                            out=bank[:],
                            lhsT=ints2[:, fc * 256 + c * 128: fc * 256 + (c + 1) * 128],
                            rhs=wb[i][:, k8 * 1024 + n * 512: k8 * 1024 + (n + 1) * 512],
                            start=(fc == 0), stop=(fc == 31)), waits=w,
                            inc=("ST" if (cn == 3 and k8 == 7) else None))
                tok_kq = p.tok("ST")
                done_slot(sf2[kq], tok_kq)
            tok_bl2 = p.tok("ST")
            for cn in range(4):
                c, n = cn // 2, cn % 2
                bank = pmm[cn]
                p.op("v", lambda e, c=c, n=n, bank=bank: e.scalar_tensor_tensor(
                    out=ybuf[:, c * 1024 + n * 512: c * 1024 + (n + 1) * 512],
                    in0=bank[:], scalar=sst[:, 24 + c:25 + c],
                    in1=fb2tile[:, n * 512:(n + 1) * 512], op0=ALU.mult, op1=ALU.add),
                    waits=[tok_bl2, tok_qs2] + ([tok_y_dma] if tok_y_dma else []), inc="SV")
                p.op("v", lambda e, c=c, n=n: e.tensor_tensor(
                    out=xr1[:, c * 1024 + n * 512: c * 1024 + (n + 1) * 512],
                    in0=ybuf[:, c * 1024 + n * 512: c * 1024 + (n + 1) * 512],
                    in1=xr0[:, c * 1024 + n * 512: c * 1024 + (n + 1) * 512],
                    op=ALU.add), inc="SV")
                pmm_free[cn] = p.tok("SV")
            tok_z = p.tok("SV")

            tok_rstd4 = ln_stats(xr1[:], "fln", [tok_z])
            tok_y2 = ln_apply(xr1[:], "fln", ybuf[:], tok_rstd4, waits=[tok_z])
            stats_guard["tok"] = tok_y2
            tok_y_dma = p.op("g", lambda e, t=tile: e.dma_start(
                out=y_out[t * TPT:(t + 1) * TPT].rearrange("(c p) dd -> p c dd", p=128),
                in_=ybuf[:].rearrange("p (c dd) -> p c dd", dd=1024)),
                waits=[tok_y2], inc="SO")

        p.wait("g", p.tok("SO"))
        p.wait("d", p.tok("SW"))

        with nc.Block() as block:
            @block.sync
            def _(sync):
                for f in p.q["d"]:
                    f(sync)

            @block.tensor
            def _(t):
                for f in p.q["t"]:
                    f(t)

            @block.vector
            def _(v):
                for f in p.q["v"]:
                    f(v)

            @block.scalar
            def _(s):
                for f in p.q["s"]:
                    f(s)

            @block.gpsimd
            def _(g):
                for f in p.q["g"]:
                    f(g)

    return nc, dbg_outs


def make_inputs(x_full, weights, core):
    import ml_dtypes
    b, half = core // 2, core % 2
    d = {}
    d["x_main"] = np.ascontiguousarray(x_full[b, half * 4096:(half + 1) * 4096])
    d["x_pref"] = np.ascontiguousarray(x_full[b, 0:4096])
    d["maskcol"] = np.full((128, 1), float(half), np.float32)
    d["wq"] = weights["L0_attn_wq"]; d["wk"] = weights["L0_attn_wk"]
    d["wv"] = weights["L0_attn_wv"]; d["wo"] = weights["L0_attn_wo"]
    d["proj"] = weights["L0_attn_proj"]
    d["aw1"] = weights["L0_attn_w1"]; d["aw2"] = weights["L0_attn_w2"]
    d["fw1t"] = np.ascontiguousarray(np.asarray(weights["L0_ffn_w1"]).T)
    d["fw2t"] = np.ascontiguousarray(np.asarray(weights["L0_ffn_w2"]).T)
    d["g1c"] = np.asarray(weights["L0_attn_g1"]).reshape(8, 128).T
    d["g2c"] = np.asarray(weights["L0_attn_g2"]).reshape(8, 128).T
    d["b1c"] = np.asarray(weights["L0_attn_beta1"]).reshape(8, 128).T
    d["b2c"] = np.asarray(weights["L0_attn_beta2"]).reshape(8, 128).T
    d["ab1r"] = np.asarray(weights["L0_attn_b1"]).reshape(1, -1)
    d["ab2r"] = np.asarray(weights["L0_attn_b2"]).reshape(1, -1)
    d["fb1r"] = np.asarray(weights["L0_ffn_b1"]).reshape(1, -1)
    d["fb2r"] = np.asarray(weights["L0_ffn_b2"]).reshape(1, -1)
    d["ident_bf"] = np.eye(128, dtype=ml_dtypes.bfloat16)
    d["ident_f32"] = np.eye(128, dtype=np.float32)
    d["onescol_bf"] = np.ones((1, 128), ml_dtypes.bfloat16)
    d["onescol_f32"] = np.ones((1, 128), np.float32)
    d["onesrow_bf"] = np.ones((1, 512), ml_dtypes.bfloat16)
    d["onestok_bf"] = np.ones((128, 1), ml_dtypes.bfloat16)
    mk = np.triu(np.ones((128, 128), np.float32))
    d["maskT4"] = np.ascontiguousarray(np.tile(mk, (1, 4)))
    out = {}
    for k in d:
        a = np.asarray(d[k])
        if a.dtype == np.float64:
            a = a.astype(np.float32)
        out[k] = np.ascontiguousarray(a)
    return out


def kernel(**inputs):
    x = np.asarray(inputs["L0_attn_x"])
    nc, _ = build(nm=16, nprf=16)
    in_maps = [make_inputs(x, inputs, core) for core in range(8)]
    res = run_bass_kernel_spmd(nc, in_maps, core_ids=list(range(8)))
    out = np.zeros((4, 8192, 1024), np.float32)
    for core in range(8):
        b, half = core // 2, core % 2
        out[b, half * 4096:(half + 1) * 4096] = res.results[core]["y_out"]
    return out


# version 06
# speedup vs baseline: 1.1528x; 1.1528x over previous
import numpy as np
import concourse.bass as bass
import concourse.mybir as mybir
from concourse.bass_utils import run_bass_kernel_spmd
from contextlib import ExitStack

F32 = mybir.dt.float32
BF16 = mybir.dt.bfloat16
I8 = mybir.dt.int8
AF = mybir.ActivationFunctionType
ALU = mybir.AluOpType
AX = mybir.AxisListType

D = 1024
H = 16
DH = 64
FF = 4096
C = 128
TPT = 256
NCHUNK = 2

E_SET = 6
G_SET = 10

ENGS = ("d", "t", "v", "s", "g")


class P:
    def __init__(self, nc):
        self.nc = nc
        self.q = {e: [] for e in ENGS}
        self.cnt = {}
        self.sems = {}
        self.lastwait = {e: {} for e in ENGS}
        self.act_set = None

    def new_sem(self, name):
        self.cnt[name] = 0

    def bump(self, sem, n):
        self.cnt[sem] += n
        return (sem, self.cnt[sem])

    def emit(self, eng, fn):
        self.q[eng].append(fn)

    def wait(self, eng, tok):
        if tok is None:
            return
        sem, val = tok
        if self.lastwait[eng].get(sem, 0) >= val:
            return
        self.lastwait[eng][sem] = val

        def f(e, sem=sem, val=val):
            e.wait_ge(self.sems[sem], val)
        self.emit(eng, f)

    def op(self, eng, fn, waits=(), inc=None, n=None):
        for w in waits:
            self.wait(eng, w)
        tok = None
        if inc:
            amt = n if n is not None else (16 if inc in ("SW", "SO") else 1)
            tok = self.bump(inc, amt)

        def g(e, fn=fn, tok=tok):
            i = fn(e)
            if tok is not None:
                amt = 16 if tok[0] in ("SW", "SO") else 1
                i.then_inc(self.sems[tok[0]], amt)
        self.emit(eng, g)
        return tok

    def tok(self, sem):
        return (sem, self.cnt[sem])

    def table(self, set_id):
        if self.act_set == set_id:
            return
        self.act_set = set_id

        def f(e, set_id=set_id):
            e.add_instruction(mybir.InstLoadActFuncSet(
                name=self.nc.get_next_instruction_name(),
                act_func_set_id=set_id, ins=[], outs=[]))
        self.emit("s", f)


def build(nm=16, nprf=16, dbg=None):
    dbg = dbg or set()
    nc = bass.Bass("TRN2", target_bir_lowering=False)

    di = lambda name, shape, dt=F32: nc.dram_tensor(name, shape, dt, kind="ExternalInput").ap()
    do = lambda name, shape, dt=F32: nc.dram_tensor(name, shape, dt, kind="ExternalOutput").ap()
    dint = lambda name, shape, dt=BF16: nc.dram_tensor(name, shape, dt, kind="Internal").ap()

    x_main = di("x_main", [4096, D])
    x_pref = di("x_pref", [4096, D])
    maskcol = di("maskcol", [128, 1])
    wq = di("wq", [D, D]); wk = di("wk", [D, D]); wv = di("wv", [D, D]); wo = di("wo", [D, D])
    proj = di("proj", [DH, DH])
    aw1 = di("aw1", [D, FF]); aw2 = di("aw2", [FF, D])
    fw1t = di("fw1t", [D, FF]); fw2t = di("fw2t", [FF, D])
    g1c = di("g1c", [128, 8]); g2c = di("g2c", [128, 8])
    b1c = di("b1c", [128, 8]); b2c = di("b2c", [128, 8])
    ab1r = di("ab1r", [1, FF]); ab2r = di("ab2r", [1, D])
    fb1r = di("fb1r", [1, FF]); fb2r = di("fb2r", [1, D])
    ident_bf = di("ident_bf", [128, 128], BF16)
    ident_f32 = di("ident_f32", [128, 128], F32)
    onescol_bf = di("onescol_bf", [1, 128], BF16)
    onescol_f32 = di("onescol_f32", [1, 128], F32)
    onesrow_bf = di("onesrow_bf", [1, 512], BF16)
    onestok_bf = di("onestok_bf", [128, 1], BF16)
    maskT4 = di("maskT4", [128, 512])

    y_out = do("y_out", [4096, D])

    scr = {}
    for name, nq in [("q", 1), ("k", 1), ("v", 1), ("o", 1),
                     ("aw1", 4), ("aw2", 4), ("f1", 4), ("f2", 4)]:
        scr[name] = dint("scr_" + name, [nq, 8, 128, 1024], BF16)

    dbg_outs = {}

    with ExitStack() as st_:
        E = st_.enter_context
        wb = [E(nc.sbuf_tensor(f"wb{i}", [128, 8192], BF16)) for i in range(2)]
        xin = [E(nc.sbuf_tensor(f"xin{i}", [128, 2048], F32)) for i in range(2)]
        xr0 = E(nc.sbuf_tensor("xr0", [128, 2048], F32))
        xr1 = E(nc.sbuf_tensor("xr1", [128, 2048], F32))
        ybuf = E(nc.sbuf_tensor("ybuf", [128, 2048], F32))
        hf = E(nc.sbuf_tensor("hf", [128, 2048], F32))
        hT = E(nc.sbuf_tensor("hT", [128, 2048], BF16))
        QRT = E(nc.sbuf_tensor("QRT", [64, 4096], BF16))
        KRT = E(nc.sbuf_tensor("KRT", [64, 4096], BF16))
        QT = QRT
        KT = KRT
        Kc = E(nc.sbuf_tensor("Kc", [128, 2048], BF16))
        Vc = E(nc.sbuf_tensor("Vc", [128, 2080], BF16))
        Ssb = E(nc.sbuf_tensor("Ssb", [128, 2048], BF16))
        attnT = E(nc.sbuf_tensor("attnT", [128, 2048], BF16))
        attn_tm = E(nc.sbuf_tensor("attn_tm", [128, 2048], BF16))
        stS = E(nc.sbuf_tensor("stS", [64, 1024], BF16))
        ksS = E(nc.sbuf_tensor("ksS", [64, 16], BF16))
        garena = E(nc.sbuf_tensor("garena", [128, 8192], BF16))
        ints2 = E(nc.sbuf_tensor("ints2", [128, 8192], BF16))
        xqb = attn_tm
        i8t = E(nc.sbuf_tensor("i8t", [128, 2048], I8))
        i8g = [E(nc.sbuf_tensor(f"i8g{i}", [128, 256], I8)) for i in range(2)]
        sqb = [E(nc.sbuf_tensor(f"sqb{i}", [128, 256], BF16)) for i in range(2)]
        gmb = [E(nc.sbuf_tensor(f"gmb{i}", [128, 256], BF16)) for i in range(2)]
        e1 = E(nc.sbuf_tensor("e1", [128, 512], F32))
        e2 = E(nc.sbuf_tensor("e2", [128, 512], F32))
        e3 = E(nc.sbuf_tensor("e3", [128, 512], F32))
        sst = E(nc.sbuf_tensor("sst", [128, 32], F32))
        sqd = E(nc.sbuf_tensor("sqd", [128, 1024], BF16))
        stats = E(nc.sbuf_tensor("stats", [128, 64], F32))
        dens = E(nc.sbuf_tensor("dens", [128, 16], F32))
        recips = E(nc.sbuf_tensor("recips", [128, 16], F32))
        mrow = E(nc.sbuf_tensor("mrow", [1, 256], BF16))
        rowf = E(nc.sbuf_tensor("rowf", [1, 256], F32))
        Bsb = E(nc.sbuf_tensor("Bsb", [128, 256], BF16))
        Id = E(nc.sbuf_tensor("Id", [128, 128], BF16))
        Idf = E(nc.sbuf_tensor("Idf", [128, 128], F32))
        McT = E(nc.sbuf_tensor("McT", [128, 512], F32))
        Onc = E(nc.sbuf_tensor("Onc", [1, 128], BF16))
        Oncf = E(nc.sbuf_tensor("Oncf", [1, 128], F32))
        Onr = E(nc.sbuf_tensor("Onr", [1, 512], BF16))
        Ontk = E(nc.sbuf_tensor("Ontk", [128, 1], BF16))
        Mcol = E(nc.sbuf_tensor("Mcol", [128, 1], F32))
        projb = E(nc.sbuf_tensor("projb", [64, 64], BF16))
        projf = E(nc.sbuf_tensor("projf", [64, 64], F32))
        g1b = E(nc.sbuf_tensor("g1b", [128, 8], F32))
        g2b = E(nc.sbuf_tensor("g2b", [128, 8], F32))
        b1f = E(nc.sbuf_tensor("b1f", [128, 8], F32))
        b2f = E(nc.sbuf_tensor("b2f", [128, 8], F32))
        b1b = E(nc.sbuf_tensor("b1b", [128, 8], BF16))
        b2b = E(nc.sbuf_tensor("b2b", [128, 8], BF16))
        bqr = E(nc.sbuf_tensor("bqr", [1, 1024], BF16))
        bkr = E(nc.sbuf_tensor("bkr", [1, 1024], BF16))
        bvr = E(nc.sbuf_tensor("bvr", [1, 1024], BF16))
        baw1 = E(nc.sbuf_tensor("baw1", [1, 4096], BF16))
        ab2b = E(nc.sbuf_tensor("ab2b", [1, 1024], BF16))
        fb1b = E(nc.sbuf_tensor("fb1b", [1, 4096], BF16))
        fb2tile = E(nc.sbuf_tensor("fb2tile", [128, 1024], F32))
        fb2b = E(nc.sbuf_tensor("fb2b", [1, 1024], BF16))
        rtmp = ybuf[0:1, 0:512]
        rtmp2 = ybuf[0:1, 1024:2048]
        swc = E(nc.sbuf_tensor("swc", [128, 4], F32))
        tiny = E(nc.sbuf_tensor("tiny", [1, 8], F32))
        tiny2 = E(nc.sbuf_tensor("tiny2", [128, 1], F32))
        st0 = E(nc.psum_tensor("st0", [64, 512], F32))
        st1 = E(nc.psum_tensor("st1", [64, 512], F32))
        ksb = E(nc.psum_tensor("ksb", [64, 16], F32))
        pmm = [E(nc.psum_tensor(f"pmm{i}", [128, 512], F32)) for i in range(4)]
        pbf = E(nc.psum_tensor("pbf", [128, 1024], BF16))
        sem_names = ["SW", "SO", "ST", "SV", "SS", "SP"]
        sems = {nm_: E(nc.semaphore(nm_)) for nm_ in sem_names}

        p = P(nc)
        p.sems = sems
        for s_ in sem_names:
            p.new_sem(s_)

        gare_f32 = garena[:].bitcast(F32)
        qkraw = gare_f32[:, 0:2048]
        ptmp = gare_f32[:, 2048:4096]
        ptmp2 = gare_f32[:, 0:1024]
        i8p = ints2[:].bitcast(I8)

        SC = {"ln1": 0, "ln2": 8, "oln": 16, "fln": 24}

        pmm_free = [None] * 4
        pbf_free = [None]
        wb_free = [None, None]
        wb_ready = [None, None]

        def dma_in(dst, src, queue="d", waits=()):
            return p.op(queue, lambda e, d=dst, s=src: e.dma_start(out=d, in_=s),
                        waits=waits, inc="SW" if queue == "d" else "SO")

        def dump(name, src_ap, shape, dt=F32, waits=()):
            if name not in dbg:
                return
            ap = do("dbg_" + name, shape, dt)
            dbg_outs[name] = ap
            tok = p.op("g", lambda e, a=ap, s=src_ap: e.dma_start(out=a, in_=s),
                       waits=waits, inc="SO")
            for eng in ENGS:
                p.wait(eng, tok)

        for dst, src in [(Id[:], ident_bf), (Idf[:], ident_f32), (Onc[:], onescol_bf),
                         (Oncf[:], onescol_f32), (Onr[:], onesrow_bf), (Ontk[:], onestok_bf),
                         (McT[:], maskT4), (Mcol[:], maskcol),
                         (g1b[:], g1c),
                         (g2b[:], g2c),
                         (projf[:], proj),
                         (b1f[:], b1c),
                         (b2f[:], b2c)]:
            dma_in(dst, src)
        tok_consts = p.tok("SW")

        p.op("v", lambda e: e.tensor_copy(out=projb[:], in_=projf[:]),
             waits=[tok_consts], inc="SV")
        tok_projb = p.tok("SV")
        p.op("v", lambda e: e.tensor_copy(out=b1b[:], in_=b1f[:]), inc="SV")
        p.op("v", lambda e: e.tensor_copy(out=b2b[:], in_=b2f[:]), inc="SV")
        for dst_b, src_r, n512 in [(ab2b, ab2r, 2), (fb1b, fb1r, 8), (fb2b, fb2r, 2)]:
            for i in range(n512):
                tok = dma_in(rtmp, src_r[0:1, i * 512:(i + 1) * 512], waits=[p.tok("SV")])
                p.op("v", lambda e, d=dst_b, i=i: e.tensor_copy(
                    out=d[0:1, i * 512:(i + 1) * 512], in_=rtmp), waits=[tok], inc="SV")
        tok_smallcasts = p.tok("SV")

        p.op("g", lambda e: e.memset(Vc[:], 1.0), inc="SP")
        p.op("g", lambda e: e.memset(stS[:], 0.0), inc="SP")
        p.op("g", lambda e: e.memset(ksS[:], 0.0), inc="SP")
        tok_memset = p.tok("SP")

        p.table(E_SET)

        tok_bias = {}

        stage_guard = [None, None]

        def prep_linear(w_ap, scr_name, gcol, bias_dst, beta_b, extra_row):
            nq = scr[scr_name].shape[0]
            kmajor = scr_name in ("aw2",)
            for q in range(nq):
                cast_tok = [None] * 8
                for k in range(8):
                    if kmajor:
                        rows, cols = (q * 8 + k) * 128, 0
                    else:
                        rows, cols = k * 128, q * 1024
                    hh = k % 2
                    w = [cast_tok[k - 2]] if k >= 2 else                         ([stage_guard[hh]] if stage_guard[hh] else [])
                    pv = ptmp[:, hh * 1024:hh * 1024 + 1024]
                    tok = dma_in(pv, w_ap[rows:rows + 128, cols:cols + 1024], waits=w)
                    if bias_dst is not None:
                        p.op("v", lambda e, k=k, pv=pv: e.tensor_copy(
                            out=wb[1][:, k * 1024:(k + 1) * 1024], in_=pv),
                            waits=[tok] + ([wb_free[1]] if wb_free[1] else []), inc="SV")
                    if gcol is not None:
                        p.op("v", lambda e, k=k, g=gcol, pv=pv: e.tensor_scalar(
                            out=wb[0][:, k * 1024:(k + 1) * 1024], in0=pv,
                            scalar1=g[:, k:k + 1], scalar2=None, op0=ALU.mult),
                            waits=[tok, tok_consts] + ([wb_free[0]] if wb_free[0] else []),
                            inc="SV")
                    else:
                        p.op("v", lambda e, k=k, pv=pv: e.tensor_copy(
                            out=wb[0][:, k * 1024:(k + 1) * 1024], in_=pv),
                            waits=[tok] + ([wb_free[0]] if wb_free[0] else []), inc="SV")
                    cast_tok[k] = p.tok("SV")
                stage_guard[0] = cast_tok[6]
                stage_guard[1] = cast_tok[7]
                tok_stage = p.tok("SV")
                tok_bmm = None
                if bias_dst is not None:
                    for n in range(2):
                        for k in range(8):
                            p.op("t", lambda e, k=k, n=n, bb=beta_b: e.matmul(
                                out=pmm[3][0:1, 0:512],
                                lhsT=bb[:, k:k + 1],
                                rhs=wb[1][:, k * 1024 + n * 512: k * 1024 + (n + 1) * 512],
                                start=(k == 0), stop=(k == 7)),
                                waits=([tok_stage, tok_smallcasts] +
                                       ([pmm_free[3]] if pmm_free[3] else [])) if k == 0 else [],
                                inc=("ST" if k == 7 else None))
                        tok_bmm = p.tok("ST")
                        off = q * 1024 + n * 512
                        if extra_row is None:
                            p.op("v", lambda e, off=off, bd=bias_dst: e.tensor_copy(
                                out=bd[0:1, off:off + 512],
                                in_=pmm[3][0:1, 0:512]), waits=[tok_bmm], inc="SV")
                        else:
                            tok_er = dma_in(rtmp, extra_row[0:1, off:off + 512],
                                            waits=[p.tok("SV")])
                            p.op("v", lambda e, off=off, bd=bias_dst: e.tensor_tensor(
                                out=bd[0:1, off:off + 512],
                                in0=pmm[3][0:1, 0:512],
                                in1=rtmp, op=ALU.add),
                                waits=[tok_bmm, tok_er], inc="SV")
                        pmm_free[3] = p.tok("SV")
                    tok_bias[scr_name] = p.tok("SV")
                    wb_free[1] = tok_bmm
                w_ = [tok_stage] + ([tok_bmm] if tok_bmm else [])
                for k in range(8):
                    wb_free[0] = p.op("d", lambda e, q=q, sn=scr_name, k=k: e.dma_start(
                        out=scr[sn][q, k], in_=wb[0][:, k * 1024:(k + 1) * 1024]),
                        waits=(w_ if k == 0 else []), inc="SW")

        prep_linear(wq, "q", g1b, bqr, b1b, None)
        prep_linear(wk, "k", g1b, bkr, b1b, None)
        prep_linear(wv, "v", g1b, bvr, b1b, None)
        prep_linear(wo, "o", None, None, None, None)
        prep_linear(aw1, "aw1", g2b, baw1, b2b, ab1r)
        prep_linear(aw2, "aw2", None, None, None, None)

        for n in range(2):
            p.op("t", lambda e, n=n: e.matmul(
                out=pmm[3][:, 0:512], lhsT=Onc[:],
                rhs=fb2b[0:1, n * 512:(n + 1) * 512], start=True, stop=True),
                waits=[tok_smallcasts, tok_consts] + ([pmm_free[3]] if pmm_free[3] else []),
                inc="ST")
            tok_fb2 = p.tok("ST")
            p.op("v", lambda e, n=n: e.tensor_copy(
                out=fb2tile[:, n * 512:(n + 1) * 512], in_=pmm[3][:, 0:512]),
                waits=[tok_fb2], inc="SV")
            pmm_free[3] = p.tok("SV")

        def prep_ternary(w_ap, scr_name, sw_col_idx):
            abscol = stats[:, 32:64]
            nr, ncol = w_ap.shape[0] // 128, w_ap.shape[1] // 1024
            ci = 0
            abs_tok = [None] * 34
            for rr in range(nr):
                for cc in range(ncol):
                    buf = ptmp[:, (ci % 2) * 1024:(ci % 2) * 1024 + 1024]
                    w = [abs_tok[ci - 2]] if ci >= 2 else                         ([stage_guard[ci % 2]] if stage_guard[ci % 2] else [])
                    tok = dma_in(buf, w_ap[rr * 128:(rr + 1) * 128, cc * 1024:(cc + 1) * 1024],
                                 waits=w)
                    p.op("s", lambda e, ci=ci, buf=buf: e.activation(
                        out=sqd[:], in_=buf,
                        func=AF.Abs, accum_out=abscol[:, ci:ci + 1]),
                        waits=[tok], inc="SS")
                    abs_tok[ci] = p.tok("SS")
                    ci += 1
            tok_abs = p.tok("SS")
            stage_guard[0] = abs_tok[ci - 2]
            stage_guard[1] = abs_tok[ci - 1]
            p.op("v", lambda e: e.reduce_sum(out=tiny2[:, 0:1], in_=abscol, axis=AX.X),
                 waits=[tok_abs], inc="SV")
            tok_red = p.tok("SV")
            p.op("t", lambda e: e.transpose(out=pmm[3][0:1, 0:128].bitcast(F32),
                                            in_=tiny2[:, 0:1], identity=Idf[:]),
                 waits=([tok_red, tok_consts] + ([pmm_free[3]] if pmm_free[3] else [])),
                 inc="ST")
            tok_tr = p.tok("ST")
            p.op("v", lambda e: e.tensor_copy(out=rowf[0:1, 0:128], in_=pmm[3][0:1, 0:128]),
                 waits=[tok_tr], inc="SV")
            pmm_free[3] = p.tok("SV")
            p.op("v", lambda e: e.reduce_sum(out=tiny[0:1, 0:1], in_=rowf[0:1, 0:128],
                                             axis=AX.X), waits=[p.tok("SV")], inc="SV")
            p.op("v", lambda e: e.tensor_scalar(out=tiny[0:1, 1:2], in0=tiny[0:1, 0:1],
                                                scalar1=1.0 / (4096.0 * 1024.0), scalar2=1e-5,
                                                op0=ALU.mult, op1=ALU.max),
                 waits=[p.tok("SV")], inc="SV")
            p.op("v", lambda e: e.reciprocal(out=tiny[0:1, 2:3], in_=tiny[0:1, 1:2]),
                 waits=[p.tok("SV")], inc="SV")
            tok_sc = p.tok("SV")
            p.op("t", lambda e: e.matmul(out=pmm[3][:, 0:2].bitcast(F32), lhsT=Oncf[:],
                                         rhs=tiny[0:1, 1:3], start=True, stop=True),
                 waits=[tok_sc, pmm_free[3]], inc="ST")
            tok_bc = p.tok("ST")
            p.op("v", lambda e, i=sw_col_idx: e.tensor_copy(out=swc[:, i:i + 1],
                                                            in_=pmm[3][:, 0:1]),
                 waits=[tok_bc], inc="SV")
            p.op("v", lambda e, i=sw_col_idx: e.tensor_copy(out=swc[:, i + 2:i + 3],
                                                            in_=pmm[3][:, 1:2]), inc="SV")
            pmm_free[3] = p.tok("SV")
            tok_sw = p.tok("SV")
            kmajor = scr_name == "f2"
            t2_tok = [None] * 34
            gi = 0
            for q in range(4):
                for k in range(8):
                    if kmajor:
                        rows, cols = (q * 8 + k) * 128, 0
                    else:
                        rows, cols = k * 128, q * 1024
                    hh = gi % 2
                    pv = ptmp[:, hh * 1024:hh * 1024 + 1024]
                    w = [t2_tok[gi - 2]] if gi >= 2 else                         ([stage_guard[hh]] if stage_guard[hh] else [])
                    tok = dma_in(pv, w_ap[rows:rows + 128, cols:cols + 1024], waits=w)
                    p.op("v", lambda e, i=sw_col_idx, pv=pv: e.tensor_scalar(
                        out=i8p[:, 0:1024], in0=pv,
                        scalar1=swc[:, i + 2:i + 3], scalar2=None, op0=ALU.mult),
                        waits=[tok, tok_sw, p.tok("SV")], inc="SV")
                    t2_tok[gi] = p.tok("SV")
                    gi += 1
                    p.op("v", lambda e: e.tensor_copy(out=ptmp2[:], in_=i8p[:, 0:1024]),
                         waits=[p.tok("SV")], inc="SV")
                    p.op("v", lambda e, k=k: e.tensor_scalar(
                        out=wb[0][:, k * 1024:(k + 1) * 1024], in0=ptmp2[:],
                        scalar1=1.0, scalar2=-1.0, op0=ALU.min, op1=ALU.max),
                        waits=([wb_free[0]] if wb_free[0] else []) + [p.tok("SV")], inc="SV")
                tok_stage = p.tok("SV")
                stage_guard[0] = stage_guard[1] = tok_stage
                for k in range(8):
                    wb_free[0] = p.op("d", lambda e, q=q, sn=scr_name, k=k: e.dma_start(
                        out=scr[sn][q, k], in_=wb[0][:, k * 1024:(k + 1) * 1024]),
                        waits=([tok_stage] if k == 0 else []), inc="SW")

        prep_ternary(fw1t, "f1", 0)
        prep_ternary(fw2t, "f2", 1)

        dump("bqr", bqr[:], [1, 1024], BF16, waits=[p.tok("SV"), p.tok("ST")])
        dump("swc", swc[:], [128, 4], F32, waits=[p.tok("SV")])
        dump("scrf1", scr["f1"][0, 0], [128, 1024], BF16, waits=[p.tok("SW")])
        dump("scrq", scr["q"][0, 0], [128, 1024], BF16, waits=[p.tok("SW")])
        dump("scrqall", scr["q"][0].rearrange("k p f -> (k p) f"), [1024, 1024], BF16,
             waits=[p.tok("SW")])
        dump("scraw1", scr["aw1"][:].rearrange("q k p f -> (q k p) f"), [4096, 1024], BF16,
             waits=[p.tok("SW")])
        dump("baw1", baw1[:], [1, 4096], BF16, waits=[p.tok("SV"), p.tok("ST")])

        gslot = {"i": 0}

        def load_slot(scr_name, q):
            i = gslot["i"] % 2
            gslot["i"] += 1
            tok = None
            for k in range(8):
                tok = p.op("d", lambda e, sn=scr_name, q=q, i=i, k=k: e.dma_start(
                    out=wb[i][:, k * 1024:(k + 1) * 1024], in_=scr[sn][q, k]),
                    waits=([wb_free[i]] if (k == 0 and wb_free[i]) else []), inc="SW")
            wb_ready[i] = tok
            return i

        def done_slot(i, tok):
            wb_free[i] = tok

        site_guard = {"ln1": None, "ln2": None, "oln": None, "fln": None,
                      "q1": None, "q2": None}

        def ln_stats(xap, site, waits, eps=1e-5):
            b = SC[site]
            gw = [site_guard[site]] if site_guard[site] else []
            for c in range(NCHUNK):
                p.op("s", lambda e, c=c, b=b, xap=xap: e.activation(
                    out=sqd[:], in_=xap[:, c * 1024:(c + 1) * 1024], func=AF.Square,
                    accum_out=stats[:, b + 2 + c:b + 3 + c]),
                    waits=list(waits) + gw, inc="SS")
                p.op("v", lambda e, c=c, b=b, xap=xap: e.reduce_sum(
                    out=stats[:, b + c:b + 1 + c], in_=xap[:, c * 1024:(c + 1) * 1024],
                    axis=AX.X), waits=waits, inc="SV")
            tok_s = p.tok("SS")
            tok_v = p.tok("SV")
            p.op("v", lambda e, b=b: e.tensor_scalar(
                out=stats[:, b + 4:b + 6], in0=stats[:, b:b + 2],
                scalar1=1.0 / D, scalar2=None, op0=ALU.mult),
                waits=[tok_v, p.tok("SV")], inc="SV")
            p.op("v", lambda e, b=b: e.tensor_scalar(
                out=stats[:, b + 6:b + 8], in0=stats[:, b + 2:b + 4],
                scalar1=1.0 / D, scalar2=None, op0=ALU.mult),
                waits=[tok_s, p.tok("SV")], inc="SV")
            p.op("v", lambda e, b=b: e.tensor_tensor(
                out=sst[:, 0:2], in0=stats[:, b + 4:b + 6], in1=stats[:, b + 4:b + 6],
                op=ALU.mult), waits=[p.tok("SV")], inc="SV")
            p.op("v", lambda e, b=b: e.tensor_tensor(
                out=sst[:, 28:30], in0=stats[:, b + 6:b + 8], in1=sst[:, 0:2],
                op=ALU.subtract), waits=[p.tok("SV")], inc="SV")
            p.op("v", lambda e, b=b, eps=eps: e.tensor_scalar(
                out=stats[:, b + 6:b + 8], in0=sst[:, 28:30],
                scalar1=eps, scalar2=None, op0=ALU.add), waits=[p.tok("SV")], inc="SV")
            tok_var = p.tok("SV")
            p.table(E_SET)
            p.op("s", lambda e, b=b: e.activation(out=sst[:, 2:4], in_=stats[:, b + 6:b + 8],
                                                  func=AF.Ln),
                 waits=[tok_var], inc="SS")
            p.op("s", lambda e, b=b: e.activation(out=stats[:, b + 6:b + 8], in_=sst[:, 2:4],
                                                  func=AF.Exp, scale=-0.5),
                 waits=[p.tok("SS")], inc="SS")
            return p.tok("SS")

        def ln_apply(xap, site, outap, tok_rstd, waits=()):
            b = SC[site]
            for c in range(NCHUNK):
                p.op("v", lambda e, c=c, b=b, xap=xap, outap=outap: e.tensor_scalar(
                    out=outap[:, c * 1024:(c + 1) * 1024],
                    in0=xap[:, c * 1024:(c + 1) * 1024],
                    scalar1=stats[:, b + 4 + c:b + 5 + c],
                    scalar2=stats[:, b + 6 + c:b + 7 + c],
                    op0=ALU.subtract, op1=ALU.mult),
                    waits=list(waits) + [tok_rstd, p.tok("SV")], inc="SV")
            return p.tok("SV")

        def transpose_2k(src_f32, tok_src, dst_bf):
            for w in range(4):
                bank = pmm[w % 2]
                for i in range(4):
                    d = w * 4 + i
                    c, j = d % 2, d // 2
                    p.op("t", lambda e, c=c, j=j, i=i, bank=bank, src_f32=src_f32: e.transpose(
                        out=bank[:, i * 128:(i + 1) * 128],
                        in_=src_f32[:, c * 1024 + j * 128: c * 1024 + (j + 1) * 128],
                        identity=Idf[:]),
                        waits=([tok_src, tok_consts] +
                               ([pmm_free[w % 2]] if pmm_free[w % 2] else [])) if i == 0 else [],
                        inc=("ST" if i == 3 else None))
                tok_t = p.tok("ST")
                p.op("v", lambda e, w=w, bank=bank, dst_bf=dst_bf: e.tensor_copy(
                    out=dst_bf[:, w * 512:(w + 1) * 512], in_=bank[:]),
                    waits=[tok_t], inc="SV")
                pmm_free[w % 2] = p.tok("SV")
            return p.tok("SV")

        def proj_tm(wslot_i, tok_w, tok_hT, bias_row, tok_b, out_f32=None, out_vc=False,
                    evac_waits=()):
            last = None
            for c in range(NCHUNK):
                for n in range(2):
                    bi = (c * 2 + n) % 2
                    bank = pmm[bi]
                    p.op("t", lambda e, n=n, bank=bank, br=bias_row: e.matmul(
                        out=bank[:], lhsT=Onc[:], rhs=br[0:1, n * 512:(n + 1) * 512],
                        start=True, stop=False),
                        waits=[tok_w, tok_hT, tok_b, tok_consts] +
                              ([pmm_free[bi]] if pmm_free[bi] else []))
                    for k in range(8):
                        p.op("t", lambda e, c=c, n=n, k=k, i=wslot_i, bank=bank: e.matmul(
                            out=bank[:], lhsT=hT[:, (k * 2 + c) * 128:(k * 2 + c + 1) * 128],
                            rhs=wb[i][:, k * 1024 + n * 512: k * 1024 + (n + 1) * 512],
                            start=False, stop=(k == 7)), inc=("ST" if k == 7 else None))
                    tok_mm = p.tok("ST")
                    if out_vc:
                        vview = Vc[:].rearrange("p (cc hh s) -> p cc hh s", cc=2, hh=16)[:, c, n * 8:(n + 1) * 8, 0:64]
                        pview = bank[:].rearrange("p (hh s) -> p hh s", hh=8)
                        p.op("v", lambda e, vv=vview, pv=pview: e.tensor_copy(out=vv, in_=pv),
                             waits=[tok_mm, tok_memset], inc="SV")
                    else:
                        p.op("v", lambda e, c=c, n=n, bank=bank, of=out_f32: e.tensor_copy(
                            out=of[:, c * 1024 + n * 512: c * 1024 + (n + 1) * 512],
                            in_=bank[:]), waits=[tok_mm] + list(evac_waits), inc="SV")
                    pmm_free[bi] = p.tok("SV")
                    last = p.tok("SV")
            return last

        def headT(src_f32, tok_src, dst_bf64):
            for w in range(8):
                bank = pmm[w % 2]
                for i in range(4):
                    d = w * 4 + i
                    h, c = d // 2, d % 2
                    p.op("t", lambda e, h=h, c=c, i=i, bank=bank, src_f32=src_f32: e.transpose(
                        out=bank[0:64, i * 128:(i + 1) * 128],
                        in_=src_f32[:, c * 1024 + h * 64: c * 1024 + (h + 1) * 64],
                        identity=Idf[:]),
                        waits=([tok_src, tok_consts] +
                               ([pmm_free[w % 2]] if pmm_free[w % 2] else [])) if i == 0 else [],
                        inc=("ST" if i == 3 else None))
                tok_t = p.tok("ST")
                p.op("v", lambda e, w=w, bank=bank, dst=dst_bf64: e.tensor_copy(
                    out=dst[:, w * 512:(w + 1) * 512], in_=bank[0:64, :]),
                    waits=[tok_t], inc="SV")
                pmm_free[w % 2] = p.tok("SV")
            return p.tok("SV")

        def phi(src64, tok_src, dst64):
            p.table(E_SET)
            for w in range(8):
                bi = 2 + (w % 2)
                bank = pmm[bi]
                for i in range(2):
                    h = w * 2 + i
                    p.op("t", lambda e, h=h, i=i, bank=bank, src64=src64: e.matmul(
                        out=bank[0:64, i * 256:(i + 1) * 256], lhsT=projb[:],
                        rhs=src64[:, h * 256:(h + 1) * 256], start=True, stop=True),
                        waits=([tok_src, tok_projb] +
                               ([pmm_free[bi]] if pmm_free[bi] else [])) if i == 0 else [],
                        inc=("ST" if i == 1 else None))
                tok_mm = p.tok("ST")
                p.op("v", lambda e, bank=bank: e.tensor_scalar(
                    out=e1[0:64, :], in0=bank[0:64, :], scalar1=0.0, scalar2=None,
                    op0=ALU.min), waits=[tok_mm, p.tok("SP")], inc="SV")
                tok_min = p.tok("SV")
                p.op("v", lambda e, bank=bank: e.tensor_scalar(
                    out=e2[0:64, :], in0=bank[0:64, :], scalar1=0.0, scalar2=None,
                    op0=ALU.max), inc="SV")
                pmm_free[bi] = p.tok("SV")
                tok_max = p.tok("SV")
                p.op("s", lambda e: e.activation(out=e3[0:64, :], in_=e1[0:64, :],
                                                 func=AF.Exp),
                     waits=[tok_min, p.tok("SP")], inc="SS")
                tok_exp = p.tok("SS")
                p.op("g", lambda e, w=w, dst64=dst64: e.tensor_tensor(
                    out=dst64[:, w * 512:(w + 1) * 512], in0=e3[0:64, :], in1=e2[0:64, :],
                    op=ALU.add), waits=[tok_exp, tok_max], inc="SP")
            return p.tok("SP")

        def kc_transposes(tok_kt):
            for c in range(NCHUNK):
                for h in range(16):
                    p.op("t", lambda e, c=c, h=h: e.transpose(
                        out=pbf[:, h * 64:(h + 1) * 64],
                        in_=KT[:, h * 256 + c * 128: h * 256 + (c + 1) * 128],
                        identity=Id[0:64, 0:64]),
                        waits=([tok_kt, tok_consts] +
                               ([pbf_free[0]] if pbf_free[0] else [])) if h == 0 else [],
                        inc=("ST" if h == 15 else None))
                tok_t = p.tok("ST")
                p.op("v", lambda e, c=c: e.tensor_copy(
                    out=Kc[:, c * 1024:(c + 1) * 1024], in_=pbf[:, 0:1024]),
                    waits=[tok_t], inc="SV")
                pbf_free[0] = p.tok("SV")
            return p.tok("SV")

        state_started = {"st0": False, "st1": False, "ksb": False}
        state_ready = {"tok": None}
        stS_read = {"tok": None}

        def state_update(kc_src, tok_kc, tok_vc):
            for c in range(NCHUNK):
                emit_state_chunk(kc_src, c, tok_kc, tok_vc)

        def emit_state_chunk(kc_src, c, tok_kc, tok_vc):
            for h in range(16):
                bank = st0 if h < 8 else st1
                sname = "st0" if h < 8 else "st1"
                first = not state_started[sname]
                state_started[sname] = True
                w = []
                if h == 0:
                    w = [tok_kc, tok_vc, tok_memset]
                    if stS_read["tok"] is not None:
                        w.append(stS_read["tok"])
                p.op("t", lambda e, c=c, h=h, bank=bank, first=first, kc_src=kc_src: e.matmul(
                    out=bank[0:64, (h % 8) * 64:(h % 8 + 1) * 64],
                    lhsT=kc_src[:, (c * 16 + h) * 64:(c * 16 + h + 1) * 64],
                    rhs=Vc[:, c * 1040 + h * 65: c * 1040 + h * 65 + 64],
                    start=first, stop=True), waits=w)
                firstk = not state_started["ksb"]
                state_started["ksb"] = True
                p.op("t", lambda e, c=c, h=h, firstk=firstk, kc_src=kc_src: e.matmul(
                    out=ksb[0:64, h:h + 1],
                    lhsT=kc_src[:, (c * 16 + h) * 64:(c * 16 + h + 1) * 64],
                    rhs=Ontk[:], start=firstk, stop=True),
                    inc=("ST" if h == 15 else None))
            state_ready["tok"] = p.tok("ST")

        tok_x = [None, None]
        pref_xin_used = [None, None]
        slot_kv = [None, None]

        if nprf > 0:
            tok_x[0] = dma_in(xin[0][:].rearrange("p (c dd) -> p c dd", dd=1024),
                              x_pref[0:TPT].rearrange("(c p) dd -> p c dd", p=128))
            slot_kv[0] = load_slot("k", 0)
            slot_kv[1] = load_slot("v", 0)
            tok_wk = wb_ready[slot_kv[0]]
            tok_wv = wb_ready[slot_kv[1]]
            for tile in range(nprf):
                i = tile % 2
                tok_here = tok_x[i]
                if tile + 1 < nprf:
                    w = [pref_xin_used[(tile + 1) % 2]] if pref_xin_used[(tile + 1) % 2] else []
                    tok_x[(tile + 1) % 2] = p.op("d", lambda e, t2=tile + 1: e.dma_start(
                        out=xin[t2 % 2][:].rearrange("p (c dd) -> p c dd", dd=1024),
                        in_=x_pref[t2 * TPT:(t2 + 1) * TPT].rearrange("(c p) dd -> p c dd", p=128)),
                        waits=w, inc="SW")
                xa = xin[i][:]
                tok_rstd = ln_stats(xa, "ln1", [tok_here])
                tok_h = ln_apply(xa, "ln1", hf[:], tok_rstd, waits=[tok_here])
                pref_xin_used[i] = tok_h
                tok_hT = transpose_2k(hf[:], tok_h, hT[:])
                tok_kraw = proj_tm(slot_kv[0], tok_wk, tok_hT, bkr, tok_bias["k"],
                                   out_f32=qkraw)
                tok_v = proj_tm(slot_kv[1], tok_wv, tok_hT, bvr, tok_bias["v"], out_vc=True)
                tok_krt = headT(qkraw, tok_kraw, KRT[:])
                tok_kt = phi(KRT[:], tok_krt, KT[:])
                tok_kc = kc_transposes(tok_kt)
                p.op("v", lambda e: e.tensor_scalar(
                    out=attn_tm[:], in0=Kc[:], scalar1=Mcol[:, 0:1], scalar2=None,
                    op0=ALU.mult), waits=[tok_kc, tok_consts], inc="SV")
                tok_kcm = p.tok("SV")
                if tile == nprf - 1:
                    dump("kcm", attn_tm[:], [128, 2048], BF16, waits=[tok_kcm])
                site_guard["ln1"] = tok_kcm
                state_update(attn_tm[:], tok_kcm, tok_v)
            done_slot(slot_kv[0], state_ready["tok"])
            done_slot(slot_kv[1], state_ready["tok"])

        tok_x = [None, None]
        xin_used = [None, None]
        tok_bl2_prev = {"tok": None}
        w0 = [state_ready["tok"]] if state_ready["tok"] else []
        if nprf > 0:
            w0 += [pref_xin_used[0]] if pref_xin_used[0] else []
        tok_x[0] = p.op("d", lambda e: e.dma_start(
            out=xin[0][:].rearrange("p (c dd) -> p c dd", dd=1024),
            in_=x_main[0:TPT].rearrange("(c p) dd -> p c dd", p=128)),
            waits=w0, inc="SW")
        tok_y_dma = None

        front_toks = {}

        def emit_front(tile, garena_free):
            i = tile % 2
            tok_here = tok_x[i]
            xa = xin[i][:]
            tok_rstd = ln_stats(xa, "ln1", [tok_here])
            tok_h = ln_apply(xa, "ln1", hf[:], tok_rstd, waits=[tok_here])
            tok_hT = transpose_2k(hf[:], tok_h, hT[:])

            if tile == 0:
                dump("xin", xa, [128, 2048], F32, waits=[tok_here])
                dump("stats", stats[:], [128, 64], F32, waits=[tok_rstd, tok_h])
                dump("hf", hf[:], [128, 2048], F32, waits=[tok_h])
                dump("hT", hT[:], [128, 2048], BF16, waits=[tok_hT])
            gfw = [garena_free] if garena_free else []
            sq_ = load_slot("q", 0)
            tok_qraw = proj_tm(sq_, wb_ready[sq_], tok_hT, bqr, tok_bias["q"], out_f32=qkraw,
                               evac_waits=gfw)
            if tile == 0:
                dump("qraw", qkraw, [128, 2048], F32, waits=[tok_qraw])
            tok_qrt = headT(qkraw, tok_qraw, QRT[:])
            done_slot(sq_, tok_qrt)
            sk_ = load_slot("k", 0)
            tok_kraw = proj_tm(sk_, wb_ready[sk_], tok_hT, bkr, tok_bias["k"], out_f32=qkraw,
                               evac_waits=gfw)
            tok_krt = headT(qkraw, tok_kraw, KRT[:])
            done_slot(sk_, tok_krt)
            sv_ = load_slot("v", 0)
            tok_v = proj_tm(sv_, wb_ready[sv_], tok_hT, bvr, tok_bias["v"], out_vc=True)
            done_slot(sv_, tok_v)
            tok_qt = phi(QRT[:], tok_qrt, QT[:])
            tok_kt = phi(KRT[:], tok_krt, KT[:])
            tok_kc = kc_transposes(tok_kt)

            if tile == 0:
                dump("QT", QT[:], [64, 4096], BF16, waits=[tok_qt])
                dump("KT", KT[:], [64, 4096], BF16, waits=[tok_kt])
                dump("Vc", Vc[:], [128, 2080], BF16, waits=[tok_v])
                dump("Kc", Kc[:], [128, 2048], BF16, waits=[tok_kc])

            site_guard["ln1"] = tok_h
            front_toks[tile] = dict(i=i, tok_here=tok_here, xa=xa,
                                    tok_qt=tok_qt, tok_kt=tok_kt, tok_kc=tok_kc,
                                    tok_v=tok_v)

        def emit_back1(tile):
            F = front_toks[tile]
            i = F["i"]; tok_here = F["tok_here"]; xa = F["xa"]
            tok_qt = F["tok_qt"]; tok_kt = F["tok_kt"]
            tok_kc = F["tok_kc"]; tok_v = F["tok_v"]
            if tile + 1 < nm:
                wprev = []
                if tile == 0 and nprf > 0 and pref_xin_used[1]:
                    wprev = [pref_xin_used[1]]
                elif xin_used[(tile + 1) % 2]:
                    wprev = [xin_used[(tile + 1) % 2]]
                tok_x[(tile + 1) % 2] = p.op("d", lambda e, t2=tile + 1: e.dma_start(
                    out=xin[t2 % 2][:].rearrange("p (c dd) -> p c dd", dd=1024),
                    in_=x_main[t2 * TPT:(t2 + 1) * TPT].rearrange("(c p) dd -> p c dd", p=128)),
                    waits=wprev, inc="SW")
            xa = xin[i][:]
            for c in range(NCHUNK):
                if state_ready["tok"] is not None:
                    p.op("v", lambda e: e.tensor_copy(out=stS[0:64, 0:512], in_=st0[:]),
                         waits=[state_ready["tok"], tok_memset], inc="SV")
                    p.op("v", lambda e: e.tensor_copy(out=stS[0:64, 512:1024], in_=st1[:]),
                         inc="SV")
                    p.op("v", lambda e: e.tensor_copy(out=ksS[:], in_=ksb[:]), inc="SV")
                    stS_read["tok"] = p.tok("SV")
                tok_st_copy = p.tok("SV")
                if tile == 0 and c == 0:
                    dump("stS0", stS[:], [64, 1024], BF16, waits=[tok_st_copy])
                    dump("ksS0", ksS[:], [64, 16], BF16, waits=[tok_st_copy])
                    dump("mcol", Mcol[:], [128, 1], F32, waits=[tok_consts])
                tok_masks = [None] * 4
                for grp in range(4):
                    sbank = pmm[grp % 2]
                    for ii in range(4):
                        h = grp * 4 + ii
                        p.op("t", lambda e, h=h, ii=ii, c=c, sbank=sbank: e.matmul(
                            out=sbank[:, ii * 128:(ii + 1) * 128],
                            lhsT=KT[0:64, h * 256 + c * 128: h * 256 + (c + 1) * 128],
                            rhs=QT[0:64, h * 256 + c * 128: h * 256 + (c + 1) * 128],
                            start=True, stop=True),
                            waits=([tok_qt, tok_kt] +
                                   ([pmm_free[grp % 2]] if pmm_free[grp % 2] else [])) if ii == 0 else [],
                            inc=("ST" if ii == 3 else None))
                    tok_sc = p.tok("ST")
                    p.op("v", lambda e, grp=grp, sbank=sbank: e.tensor_tensor(
                        out=Ssb[:, grp * 512:(grp + 1) * 512], in0=sbank[:], in1=McT[:],
                        op=ALU.mult), waits=[tok_sc, tok_consts], inc="SV")
                    tok_masks[grp] = p.tok("SV")
                    pmm_free[grp % 2] = tok_masks[grp]
                for grp in range(4):
                    nbank = pmm[2 + grp % 2]
                    tok_mask = tok_masks[grp]
                    for ii in range(4):
                        h = grp * 4 + ii
                        p.op("t", lambda e, h=h, ii=ii, c=c, nbank=nbank: e.matmul(
                            out=nbank[:, ii * 128:ii * 128 + 65],
                            lhsT=Ssb[:, h * 128:(h + 1) * 128],
                            rhs=Vc[:, c * 1040 + h * 65: c * 1040 + (h + 1) * 65],
                            start=True, stop=False),
                            waits=([tok_mask, tok_v, tok_st_copy, tok_memset] +
                                   ([pmm_free[2 + grp % 2]] if pmm_free[2 + grp % 2] else [])) if ii == 0 else [])
                        p.op("t", lambda e, h=h, ii=ii, c=c, nbank=nbank: e.matmul(
                            out=nbank[:, ii * 128:ii * 128 + 64],
                            lhsT=QT[0:64, h * 256 + c * 128: h * 256 + (c + 1) * 128],
                            rhs=stS[0:64, h * 64:(h + 1) * 64], start=False, stop=False))
                        p.op("t", lambda e, h=h, ii=ii, c=c, nbank=nbank: e.matmul(
                            out=nbank[:, ii * 128 + 64:ii * 128 + 65],
                            lhsT=QT[0:64, h * 256 + c * 128: h * 256 + (c + 1) * 128],
                            rhs=ksS[0:64, h:h + 1], start=False, stop=True),
                            inc=("ST" if ii == 3 else None))
                    tok_n = p.tok("ST")
                    dview = nbank[:].rearrange("p (hh s) -> p hh s", hh=4)[:, :, 64:65]
                    p.op("v", lambda e, grp=grp, dv=dview: e.tensor_scalar(
                        out=dens[:, grp * 4:(grp + 1) * 4], in0=dv,
                        scalar1=1e-6, scalar2=None, op0=ALU.add),
                        waits=[tok_n, p.tok("SV")], inc="SV")
                    p.op("v", lambda e, grp=grp: e.reciprocal(
                        out=recips[:, grp * 4:(grp + 1) * 4],
                        in_=dens[:, grp * 4:(grp + 1) * 4]),
                        waits=[p.tok("SV")], inc="SV")
                    for ii in range(4):
                        h = grp * 4 + ii
                        p.op("v", lambda e, h=h, ii=ii, c=c, nbank=nbank: e.tensor_scalar(
                            out=attn_tm[:, c * 1024 + h * 64: c * 1024 + (h + 1) * 64],
                            in0=nbank[:, ii * 128:ii * 128 + 64],
                            scalar1=recips[:, h:h + 1], scalar2=None, op0=ALU.mult),
                            waits=([p.tok("SV")] if ii == 0 else []),
                            inc=("SV" if ii == 3 else None))
                    pmm_free[2 + grp % 2] = p.tok("SV")
                tok_div = p.tok("SV")
                emit_state_chunk(Kc[:], c, tok_kc, tok_v)
                for j in range(8):
                    p.op("t", lambda e, c=c, j=j: e.transpose(
                        out=pbf[:, j * 128:(j + 1) * 128],
                        in_=attn_tm[:, c * 1024 + j * 128: c * 1024 + (j + 1) * 128],
                        identity=Id[:]),
                        waits=([tok_div] + ([pbf_free[0]] if pbf_free[0] else [])) if j == 0 else [],
                        inc=("ST" if j == 7 else None))
                tok_at = p.tok("ST")
                p.op("v", lambda e, c=c: e.tensor_copy(
                    out=attnT[:, c * 1024:(c + 1) * 1024], in_=pbf[:]),
                    waits=[tok_at], inc="SV")
                pbf_free[0] = p.tok("SV")
            tok_attnT = p.tok("SV")

            if tile == 0:
                dump("attn_tm", attn_tm[:], [128, 2048], BF16, waits=[tok_attnT])

            so_ = load_slot("o", 0)
            for c in range(NCHUNK):
                for n in range(2):
                    bi = (c * 2 + n) % 2
                    bank = pmm[bi]
                    for k in range(8):
                        p.op("t", lambda e, c=c, n=n, k=k, i=so_, bank=bank: e.matmul(
                            out=bank[:],
                            lhsT=attnT[:, c * 1024 + k * 128: c * 1024 + (k + 1) * 128],
                            rhs=wb[i][:, k * 1024 + n * 512: k * 1024 + (n + 1) * 512],
                            start=(k == 0), stop=(k == 7)),
                            waits=([tok_attnT, wb_ready[so_]] +
                                   ([pmm_free[bi]] if pmm_free[bi] else [])) if k == 0 else [],
                            inc=("ST" if k == 7 else None))
                    tok_mm = p.tok("ST")
                    p.op("v", lambda e, c=c, n=n, bank=bank, xa=xa: e.tensor_tensor(
                        out=xr0[:, c * 1024 + n * 512: c * 1024 + (n + 1) * 512],
                        in0=bank[:], in1=xa[:, c * 1024 + n * 512: c * 1024 + (n + 1) * 512],
                        op=ALU.add), waits=[tok_mm, tok_here], inc="SV")
                    pmm_free[bi] = p.tok("SV")
            tok_x2 = p.tok("SV")
            xin_used[i] = tok_x2
            done_slot(so_, tok_x2)

            if tile == 0:
                dump("x2", xr0[:], [128, 2048], F32, waits=[tok_x2])

            tok_rstd2 = ln_stats(xr0[:], "ln2", [tok_x2])
            tok_h2 = ln_apply(xr0[:], "ln2", hf[:], tok_rstd2, waits=[tok_x2])
            if tile == 0:
                dump("h2f", hf[:], [128, 2048], F32, waits=[tok_h2])
            tok_h2T = transpose_2k(hf[:], tok_h2, hT[:])
            if tile == 0:
                dump("h2T", hT[:], [128, 2048], BF16, waits=[tok_h2T])

            p.table(G_SET)
            for qf in range(4):
                sa = load_slot("aw1", qf)
                for m in range(8):
                    fc = qf * 8 + m
                    bi = m % 2
                    bank = pmm[bi]
                    p.op("t", lambda e, fc=fc, bank=bank: e.matmul(
                        out=bank[:, 0:256], lhsT=baw1[0:1, fc * 128:(fc + 1) * 128],
                        rhs=Onr[0:1, 0:256], start=True, stop=False),
                        waits=[wb_ready[sa], tok_h2T, tok_bias["aw1"], tok_consts] +
                              ([pmm_free[bi]] if pmm_free[bi] else []))
                    for k in range(8):
                        p.op("t", lambda e, m=m, k=k, i=sa, bank=bank: e.matmul(
                            out=bank[:, 0:256],
                            lhsT=wb[i][:, k * 1024 + m * 128: k * 1024 + (m + 1) * 128],
                            rhs=hT[:, k * 256:(k + 1) * 256], start=False, stop=(k == 7)),
                            inc=("ST" if k == 7 else None))
                    tok_mm = p.tok("ST")
                    p.op("s", lambda e, fc=fc, bank=bank: e.activation(
                        out=garena[:, fc * 256:(fc + 1) * 256], in_=bank[:, 0:256],
                        func=AF.Gelu), waits=[tok_mm], inc="SS")
                    pmm_free[bi] = p.tok("SS")
                done_slot(sa, p.tok("SS"))
            tok_g = p.tok("SS")

            if tile == 0:
                dump("gattn", garena[:], [128, 8192], BF16, waits=[tok_g])

            sa2 = [None] * 4
            for kq in range(4):
                sa2[kq] = load_slot("aw2", kq)
                for cn in range(4):
                    c, n = cn // 2, cn % 2
                    bank = pmm[cn]
                    if kq == 0:
                        p.op("t", lambda e, n=n, bank=bank: e.matmul(
                            out=bank[:], lhsT=Onc[:], rhs=ab2b[0:1, n * 512:(n + 1) * 512],
                            start=True, stop=False),
                            waits=[wb_ready[sa2[0]], tok_g, tok_smallcasts, tok_consts] +
                                  ([pmm_free[cn]] if pmm_free[cn] else []))
                    for k8 in range(8):
                        fc = kq * 8 + k8
                        p.op("t", lambda e, fc=fc, c=c, n=n, k8=k8, i=sa2[kq], bank=bank: e.matmul(
                            out=bank[:],
                            lhsT=garena[:, fc * 256 + c * 128: fc * 256 + (c + 1) * 128],
                            rhs=wb[i][:, k8 * 1024 + n * 512: k8 * 1024 + (n + 1) * 512],
                            start=False, stop=(fc == 31)),
                            waits=([wb_ready[sa2[kq]]] if (k8 == 0 and cn == 0 and kq > 0) else []),
                            inc=("ST" if (cn == 3 and k8 == 7) else None))
                tok_kq = p.tok("ST")
                done_slot(sa2[kq], tok_kq)
            tok_mm2_all = p.tok("ST")
            for cn in range(4):
                c, n = cn // 2, cn % 2
                bank = pmm[cn]
                p.op("v", lambda e, c=c, n=n, bank=bank: e.tensor_tensor(
                    out=e1[:, 0:512], in0=bank[:],
                    in1=xr0[:, c * 1024 + n * 512: c * 1024 + (n + 1) * 512],
                    op=ALU.add), waits=[tok_mm2_all, p.tok("SV")], inc="SV")
                p.op("v", lambda e, c=c, n=n, xa=xa: e.tensor_tensor(
                    out=xr1[:, c * 1024 + n * 512: c * 1024 + (n + 1) * 512],
                    in0=e1[:, 0:512],
                    in1=xa[:, c * 1024 + n * 512: c * 1024 + (n + 1) * 512],
                    op=ALU.add), waits=[p.tok("SV")], inc="SV")
                pmm_free[cn] = p.tok("SV")
            tok_x3 = p.tok("SV")
            xin_used[i] = tok_x3

            tok_rstd3 = ln_stats(xr1[:], "oln", [tok_x3])
            tok_y1 = ln_apply(xr1[:], "oln", xr0[:], tok_rstd3, waits=[tok_x3])
            for c in range(NCHUNK):
                p.op("v", lambda e, c=c: e.reduce_max(
                    out=stats[:, 32 + c:33 + c], in_=xr0[:, c * 1024:(c + 1) * 1024],
                    axis=AX.X, apply_absolute_value=True), waits=[tok_y1], inc="SV")
            tok_maxy = p.tok("SV")

            if tile == 0:
                dump("y1", xr0[:], [128, 2048], F32, waits=[tok_y1])

            b3 = SC["oln"]
            p.op("v", lambda e, b3=b3: e.tensor_tensor(
                out=sst[:, 4:6], in0=stats[:, b3 + 6:b3 + 8], in1=stats[:, b3 + 6:b3 + 8],
                op=ALU.mult), waits=[tok_y1, p.tok("SV")], inc="SV")
            p.op("v", lambda e: e.tensor_scalar(
                out=stats[:, 36:38], in0=sst[:, 4:6], scalar1=-1e-5, scalar2=1.0 + 1e-5,
                op0=ALU.mult, op1=ALU.add), waits=[p.tok("SV")], inc="SV")
            tok_msq = p.tok("SV")
            p.table(E_SET)
            p.op("s", lambda e: e.activation(out=sst[:, 6:8], in_=stats[:, 36:38],
                                             func=AF.Ln),
                 waits=[tok_msq] + ([site_guard["q1"]] if site_guard["q1"] else []),
                 inc="SS")
            p.op("s", lambda e: e.activation(out=stats[:, 38:40], in_=sst[:, 6:8],
                                             func=AF.Exp, scale=-0.5),
                 waits=[p.tok("SS")], inc="SS")
            tok_rms2 = p.tok("SS")
            p.op("v", lambda e: e.tensor_tensor(out=sst[:, 8:10], in0=stats[:, 38:40],
                                                in1=stats[:, 32:34], op=ALU.mult),
                 waits=[tok_rms2, tok_maxy, p.tok("SV")], inc="SV")
            p.op("v", lambda e: e.tensor_scalar(out=sst[:, 30:32], in0=sst[:, 8:10],
                                                scalar1=1e-5, scalar2=None, op0=ALU.max),
                 waits=[p.tok("SV")], inc="SV")
            p.op("v", lambda e: e.reciprocal(out=sst[:, 10:12], in_=sst[:, 30:32]),
                 waits=[p.tok("SV")], inc="SV")
            p.op("v", lambda e: e.tensor_scalar(out=sst[:, 26:28], in0=sst[:, 10:12],
                                                scalar1=127.0, scalar2=None, op0=ALU.mult),
                 waits=[p.tok("SV")], inc="SV")
            p.op("v", lambda e: e.tensor_tensor(out=sst[:, 12:14], in0=stats[:, 38:40],
                                                in1=sst[:, 26:28], op=ALU.mult),
                 waits=[p.tok("SV")], inc="SV")
            p.op("v", lambda e: e.tensor_scalar(out=sst[:, 8:10], in0=sst[:, 30:32],
                                                scalar1=1.0 / 127.0, scalar2=None,
                                                op0=ALU.mult), waits=[p.tok("SV")], inc="SV")
            p.op("v", lambda e: e.tensor_scalar(out=sst[:, 14:16], in0=sst[:, 8:10],
                                                scalar1=swc[:, 0:1], scalar2=None,
                                                op0=ALU.mult), waits=[p.tok("SV")], inc="SV")
            tok_qs = p.tok("SV")
            for c in range(NCHUNK):
                p.op("v", lambda e, c=c: e.tensor_scalar(
                    out=i8t[:, c * 1024:(c + 1) * 1024], in0=xr0[:, c * 1024:(c + 1) * 1024],
                    scalar1=sst[:, 12 + c:13 + c], scalar2=None, op0=ALU.mult),
                    waits=[tok_qs, p.tok("SV")], inc="SV")
                p.op("v", lambda e, c=c: e.tensor_scalar(
                    out=xqb[:, c * 1024:(c + 1) * 1024], in0=i8t[:, c * 1024:(c + 1) * 1024],
                    scalar1=sst[:, 14 + c:15 + c], scalar2=None, op0=ALU.mult),
                    waits=[p.tok("SV")], inc="SV")
            tok_xq = p.tok("SV")
            if tile == 0:
                dump("xqb", xqb[:], [128, 2048], BF16, waits=[tok_xq])
            for c in range(NCHUNK):
                for j in range(8):
                    p.op("t", lambda e, c=c, j=j: e.transpose(
                        out=pbf[:, j * 128:(j + 1) * 128],
                        in_=xqb[:, c * 1024 + j * 128: c * 1024 + (j + 1) * 128],
                        identity=Id[:]),
                        waits=([tok_xq] + ([pbf_free[0]] if pbf_free[0] else [])) if j == 0 else [],
                        inc=("ST" if j == 7 else None))
                tok_tt = p.tok("ST")
                hview = hT[:].rearrange("p (j cc f) -> p j cc f", j=8, cc=2)[:, :, c, :]
                pview = pbf[:].rearrange("p (j f) -> p j f", j=8)
                p.op("v", lambda e, hv=hview, pv=pview: e.tensor_copy(out=hv, in_=pv),
                     waits=[tok_tt], inc="SV")
                pbf_free[0] = p.tok("SV")
            tok_xqT = p.tok("SV")

            p.table(G_SET)
            for qf in range(4):
                sf = load_slot("f1", qf)
                for m in range(8):
                    fc = qf * 8 + m
                    bi = m % 2
                    bank = pmm[bi]
                    p.op("t", lambda e, fc=fc, bank=bank: e.matmul(
                        out=bank[:, 0:256], lhsT=fb1b[0:1, fc * 128:(fc + 1) * 128],
                        rhs=Onr[0:1, 0:256], start=True, stop=False),
                        waits=[wb_ready[sf], tok_xqT, tok_smallcasts, tok_consts] +
                              ([pmm_free[bi]] if pmm_free[bi] else []))
                    for k in range(8):
                        p.op("t", lambda e, m=m, k=k, i=sf, bank=bank: e.matmul(
                            out=bank[:, 0:256],
                            lhsT=wb[i][:, k * 1024 + m * 128: k * 1024 + (m + 1) * 128],
                            rhs=hT[:, k * 256:(k + 1) * 256], start=False, stop=(k == 7)),
                            inc=("ST" if k == 7 else None))
                    tok_mm = p.tok("ST")
                    p.op("s", lambda e, fc=fc, bank=bank: e.activation(
                        out=garena[:, fc * 256:(fc + 1) * 256], in_=bank[:, 0:256],
                        func=AF.Gelu), waits=[tok_mm], inc="SS")
                    pmm_free[bi] = p.tok("SS")
                done_slot(sf, p.tok("SS"))
            tok_g2 = p.tok("SS")
            if tile == 0:
                dump("g2", garena[:], [128, 8192], BF16, waits=[tok_g2])

            for fc in range(32):
                p.op("s", lambda e, fc=fc: e.activation(
                    out=ints2[:, fc * 256:(fc + 1) * 256],
                    in_=garena[:, fc * 256:(fc + 1) * 256], func=AF.Square),
                    waits=(([tok_g2] + ([tok_bl2_prev["tok"]] if tok_bl2_prev["tok"] else []))
                           if fc == 0 else []),
                    inc=("SS" if fc in (0, 31) else None))
            tok_sqall = p.tok("SS")
            for fc in range(32):
                p.op("t", lambda e, fc=fc: e.matmul(
                    out=pmm[3][0:1, 0:256], lhsT=Ontk[:], rhs=ints2[:, fc * 256:(fc + 1) * 256],
                    start=(fc == 0), stop=(fc == 31)),
                    waits=(([tok_sqall, tok_consts] +
                           ([pmm_free[3]] if pmm_free[3] else [])) if fc == 0 else []),
                    inc=("ST" if fc == 31 else None))
                if fc == 0:
                    p.op("g", lambda e: e.tensor_copy(out=gmb[0][:], in_=ints2[:, 0:256]),
                         waits=[tok_sqall], inc="SP")
                else:
                    p.op("v", lambda e, fc=fc: e.tensor_tensor(
                        out=gmb[fc % 2][:], in0=gmb[(fc + 1) % 2][:],
                        in1=ints2[:, fc * 256:(fc + 1) * 256],
                        op=ALU.max), waits=[tok_sqall, p.tok("SP"), p.tok("SV")], inc="SV")
            tok_ssq = p.tok("ST")
            gm_fin = gmb[31 % 2]
            tok_gm = [p.tok("SV"), p.tok("SP")]
            for c in range(NCHUNK):
                p.op("t", lambda e, c=c, gm_fin=gm_fin: e.transpose(
                    out=pbf[:, c * 128:(c + 1) * 128], in_=gm_fin[:, c * 128:(c + 1) * 128],
                    identity=Id[:]),
                    waits=(tok_gm + [tok_consts] + ([pbf_free[0]] if pbf_free[0] else [])) if c == 0 else [],
                    inc=("ST" if c == 1 else None))
            tok_gt = p.tok("ST")
            for c in range(NCHUNK):
                p.op("v", lambda e, c=c: e.reduce_max(
                    out=stats[:, 52 + c:53 + c], in_=pbf[:, c * 128:(c + 1) * 128],
                    axis=AX.X), waits=[tok_gt], inc="SV")
            pbf_free[0] = p.tok("SV")
            tok_gmaxsq = p.tok("SV")
            p.table(E_SET)
            p.op("s", lambda e: e.activation(out=sst[:, 26:28], in_=stats[:, 52:54],
                                             func=AF.Ln),
                 waits=[tok_gmaxsq] + ([site_guard["q2"]] if site_guard["q2"] else []),
                 inc="SS")
            p.op("s", lambda e: e.activation(out=stats[:, 34:36], in_=sst[:, 26:28],
                                             func=AF.Exp, scale=0.5),
                 waits=[p.tok("SS")], inc="SS")
            tok_gmax = p.tok("SS")
            p.op("v", lambda e: e.tensor_copy(out=rowf[0:1, 0:256], in_=pmm[3][0:1, 0:256]),
                 waits=[tok_ssq], inc="SV")
            pmm_free[3] = p.tok("SV")
            tok_row = p.tok("SV")
            for c in range(NCHUNK):
                p.op("t", lambda e, c=c: e.matmul(
                    out=pmm[3][:, c:c + 1].bitcast(F32), lhsT=rowf[0:1, c * 128:(c + 1) * 128],
                    rhs=Oncf[0:1, 0:1], start=True, stop=True),
                    waits=([tok_row, tok_consts] + ([pmm_free[3]] if pmm_free[3] else [])) if c == 0 else [],
                    inc=("ST" if c == 1 else None))
            tok_ssqT = p.tok("ST")
            p.op("v", lambda e: e.tensor_copy(out=stats[:, 44:46], in_=pmm[3][:, 0:2]),
                 waits=[tok_ssqT], inc="SV")
            pmm_free[3] = p.tok("SV")
            p.op("v", lambda e: e.tensor_scalar(out=stats[:, 42:44], in0=stats[:, 44:46],
                                                scalar1=1.0 / FF, scalar2=1e-5,
                                                op0=ALU.mult, op1=ALU.add),
                 waits=[p.tok("SV")], inc="SV")
            tok_ms2 = p.tok("SV")
            p.table(E_SET)
            p.op("s", lambda e: e.activation(out=sst[:, 16:18], in_=stats[:, 42:44],
                                             func=AF.Ln),
                 waits=[tok_ms2] + ([site_guard["q2"]] if site_guard["q2"] else []),
                 inc="SS")
            p.op("s", lambda e: e.activation(out=stats[:, 46:48], in_=sst[:, 16:18],
                                             func=AF.Exp, scale=-0.5),
                 waits=[p.tok("SS")], inc="SS")
            tok_rmsg = p.tok("SS")
            p.op("v", lambda e: e.tensor_tensor(out=sst[:, 18:20], in0=stats[:, 46:48],
                                                in1=stats[:, 34:36], op=ALU.mult),
                 waits=[tok_rmsg, tok_gmax, p.tok("SV")], inc="SV")
            p.op("v", lambda e: e.tensor_scalar(out=sst[:, 30:32], in0=sst[:, 18:20],
                                                scalar1=1e-5, scalar2=None, op0=ALU.max),
                 waits=[p.tok("SV")], inc="SV")
            p.op("v", lambda e: e.reciprocal(out=sst[:, 20:22], in_=sst[:, 30:32]),
                 waits=[p.tok("SV")], inc="SV")
            p.op("v", lambda e: e.tensor_scalar(out=sst[:, 16:18], in0=sst[:, 20:22],
                                                scalar1=127.0, scalar2=None, op0=ALU.mult),
                 waits=[p.tok("SV")], inc="SV")
            p.op("v", lambda e: e.tensor_tensor(out=sst[:, 22:24], in0=stats[:, 46:48],
                                                in1=sst[:, 16:18], op=ALU.mult),
                 waits=[p.tok("SV")], inc="SV")
            p.op("v", lambda e: e.tensor_scalar(out=sst[:, 18:20], in0=sst[:, 30:32],
                                                scalar1=1.0 / 127.0, scalar2=None,
                                                op0=ALU.mult), waits=[p.tok("SV")], inc="SV")
            p.op("v", lambda e: e.tensor_scalar(out=sst[:, 24:26], in0=sst[:, 18:20],
                                                scalar1=swc[:, 1:2], scalar2=None,
                                                op0=ALU.mult), waits=[p.tok("SV")], inc="SV")
            tok_qs2 = p.tok("SV")
            for c in range(NCHUNK):
                p.op("t", lambda e, c=c: e.transpose(
                    out=pmm[3][0:1, c * 128:(c + 1) * 128].bitcast(F32),
                    in_=sst[:, 22 + c:23 + c], identity=Idf[:]),
                    waits=([tok_qs2, tok_consts] + ([pmm_free[3]] if pmm_free[3] else [])) if c == 0 else [],
                    inc=("ST" if c == 1 else None))
            tok_mrT = p.tok("ST")
            p.op("v", lambda e: e.tensor_copy(out=mrow[:], in_=pmm[3][0:1, 0:256]),
                 waits=[tok_mrT], inc="SV")
            pmm_free[3] = p.tok("SV")
            tok_mrow = p.tok("SV")
            p.op("t", lambda e: e.matmul(out=pmm[3][:, 0:256], lhsT=Onc[:], rhs=mrow[:],
                                         start=True, stop=True),
                 waits=[tok_mrow] + ([pmm_free[3]] if pmm_free[3] else []), inc="ST")
            tok_B = p.tok("ST")
            p.op("v", lambda e: e.tensor_copy(out=Bsb[:], in_=pmm[3][:, 0:256]),
                 waits=[tok_B], inc="SV")
            pmm_free[3] = p.tok("SV")
            tok_Bsb = p.tok("SV")
            for fc in range(32):
                ib = i8g[fc % 2]
                p.op("v", lambda e, fc=fc, ib=ib: e.tensor_tensor(
                    out=ib[:], in0=garena[:, fc * 256:(fc + 1) * 256], in1=Bsb[:],
                    op=ALU.mult),
                    waits=[tok_Bsb] + ([p.tok("SS")] if fc >= 2 else []), inc="SV")
                tok_i8 = p.tok("SV")
                p.op("s", lambda e, fc=fc, ib=ib: e.copy(
                    out=ints2[:, fc * 256:(fc + 1) * 256], in_=ib[:]),
                    waits=[tok_i8], inc="SS")
            tok_ints2 = p.tok("SS")
            if tile == 0:
                dump("ints2", ints2[:], [128, 8192], BF16, waits=[tok_ints2])

            F["tok_ints2"] = tok_ints2
            F["tok_sqall"] = tok_sqall
            F["tok_qs2"] = tok_qs2
            site_guard["ln2"] = tok_h2
            site_guard["oln"] = tok_qs
            site_guard["q1"] = tok_qs
            site_guard["q2"] = tok_qs2

        def emit_back2(tile):
            nonlocal tok_y_dma
            F = front_toks[tile]
            tok_ints2 = F["tok_ints2"]
            tok_qs2 = F["tok_qs2"]
            sf2 = [None] * 4
            for kq in range(4):
                sf2[kq] = load_slot("f2", kq)
                for cn in range(4):
                    c, n = cn // 2, cn % 2
                    bank = pmm[cn]
                    for k8 in range(8):
                        fc = kq * 8 + k8
                        w = []
                        if k8 == 0 and cn == 0:
                            w = [wb_ready[sf2[kq]], tok_ints2]
                        if kq == 0 and k8 == 0:
                            w = [wb_ready[sf2[0]], tok_ints2] + \
                                ([pmm_free[cn]] if pmm_free[cn] else [])
                        p.op("t", lambda e, fc=fc, c=c, n=n, k8=k8, i=sf2[kq], bank=bank: e.matmul(
                            out=bank[:],
                            lhsT=ints2[:, fc * 256 + c * 128: fc * 256 + (c + 1) * 128],
                            rhs=wb[i][:, k8 * 1024 + n * 512: k8 * 1024 + (n + 1) * 512],
                            start=(fc == 0), stop=(fc == 31)), waits=w,
                            inc=("ST" if (cn == 3 and k8 == 7) else None))
                tok_kq = p.tok("ST")
                done_slot(sf2[kq], tok_kq)
            tok_bl2 = p.tok("ST")
            tok_bl2_prev["tok"] = tok_bl2
            for cn in range(4):
                c, n = cn // 2, cn % 2
                bank = pmm[cn]
                p.op("v", lambda e, c=c, n=n, bank=bank: e.scalar_tensor_tensor(
                    out=ybuf[:, c * 1024 + n * 512: c * 1024 + (n + 1) * 512],
                    in0=bank[:], scalar=sst[:, 24 + c:25 + c],
                    in1=fb2tile[:, n * 512:(n + 1) * 512], op0=ALU.mult, op1=ALU.add),
                    waits=[tok_bl2, tok_qs2] + ([tok_y_dma] if tok_y_dma else []), inc="SV")
                p.op("v", lambda e, c=c, n=n: e.tensor_tensor(
                    out=xr1[:, c * 1024 + n * 512: c * 1024 + (n + 1) * 512],
                    in0=ybuf[:, c * 1024 + n * 512: c * 1024 + (n + 1) * 512],
                    in1=xr0[:, c * 1024 + n * 512: c * 1024 + (n + 1) * 512],
                    op=ALU.add), inc="SV")
                pmm_free[cn] = p.tok("SV")
            tok_z = p.tok("SV")

            tok_rstd4 = ln_stats(xr1[:], "fln", [tok_z])
            tok_y2 = ln_apply(xr1[:], "fln", ybuf[:], tok_rstd4, waits=[tok_z])
            site_guard["fln"] = tok_y2
            tok_y_dma = p.op("g", lambda e, t=tile: e.dma_start(
                out=y_out[t * TPT:(t + 1) * TPT].rearrange("(c p) dd -> p c dd", p=128),
                in_=ybuf[:].rearrange("p (c dd) -> p c dd", dd=1024)),
                waits=[tok_y2], inc="SO")


        emit_front(0, None)
        for tile in range(nm):
            emit_back1(tile)
            if tile + 1 < nm:
                emit_front(tile + 1, front_toks[tile]["tok_sqall"])
            emit_back2(tile)
            front_toks.pop(tile)
        p.wait("g", p.tok("SO"))
        p.wait("d", p.tok("SW"))

        with nc.Block() as block:
            @block.sync
            def _(sync):
                for f in p.q["d"]:
                    f(sync)

            @block.tensor
            def _(t):
                for f in p.q["t"]:
                    f(t)

            @block.vector
            def _(v):
                for f in p.q["v"]:
                    f(v)

            @block.scalar
            def _(s):
                for f in p.q["s"]:
                    f(s)

            @block.gpsimd
            def _(g):
                for f in p.q["g"]:
                    f(g)

    return nc, dbg_outs


def make_inputs(x_full, weights, core):
    import ml_dtypes
    b, half = core // 2, core % 2
    d = {}
    d["x_main"] = np.ascontiguousarray(x_full[b, half * 4096:(half + 1) * 4096])
    d["x_pref"] = np.ascontiguousarray(x_full[b, 0:4096])
    d["maskcol"] = np.full((128, 1), float(half), np.float32)
    d["wq"] = weights["L0_attn_wq"]; d["wk"] = weights["L0_attn_wk"]
    d["wv"] = weights["L0_attn_wv"]; d["wo"] = weights["L0_attn_wo"]
    d["proj"] = weights["L0_attn_proj"]
    d["aw1"] = weights["L0_attn_w1"]; d["aw2"] = weights["L0_attn_w2"]
    d["fw1t"] = np.ascontiguousarray(np.asarray(weights["L0_ffn_w1"]).T)
    d["fw2t"] = np.ascontiguousarray(np.asarray(weights["L0_ffn_w2"]).T)
    d["g1c"] = np.asarray(weights["L0_attn_g1"]).reshape(8, 128).T
    d["g2c"] = np.asarray(weights["L0_attn_g2"]).reshape(8, 128).T
    d["b1c"] = np.asarray(weights["L0_attn_beta1"]).reshape(8, 128).T
    d["b2c"] = np.asarray(weights["L0_attn_beta2"]).reshape(8, 128).T
    d["ab1r"] = np.asarray(weights["L0_attn_b1"]).reshape(1, -1)
    d["ab2r"] = np.asarray(weights["L0_attn_b2"]).reshape(1, -1)
    d["fb1r"] = np.asarray(weights["L0_ffn_b1"]).reshape(1, -1)
    d["fb2r"] = np.asarray(weights["L0_ffn_b2"]).reshape(1, -1)
    d["ident_bf"] = np.eye(128, dtype=ml_dtypes.bfloat16)
    d["ident_f32"] = np.eye(128, dtype=np.float32)
    d["onescol_bf"] = np.ones((1, 128), ml_dtypes.bfloat16)
    d["onescol_f32"] = np.ones((1, 128), np.float32)
    d["onesrow_bf"] = np.ones((1, 512), ml_dtypes.bfloat16)
    d["onestok_bf"] = np.ones((128, 1), ml_dtypes.bfloat16)
    mk = np.triu(np.ones((128, 128), np.float32))
    d["maskT4"] = np.ascontiguousarray(np.tile(mk, (1, 4)))
    out = {}
    for k in d:
        a = np.asarray(d[k])
        if a.dtype == np.float64:
            a = a.astype(np.float32)
        out[k] = np.ascontiguousarray(a)
    return out


def kernel(**inputs):
    x = np.asarray(inputs["L0_attn_x"])
    nc, _ = build(nm=16, nprf=16)
    in_maps = [make_inputs(x, inputs, core) for core in range(8)]
    res = run_bass_kernel_spmd(nc, in_maps, core_ids=list(range(8)))
    out = np.zeros((4, 8192, 1024), np.float32)
    for core in range(8):
        b, half = core // 2, core % 2
        out[b, half * 4096:(half + 1) * 4096] = res.results[core]["y_out"]
    return out


# version 07
# speedup vs baseline: 1.1748x; 1.0191x over previous
import numpy as np
import concourse.bass as bass
import concourse.mybir as mybir
from concourse.bass_utils import run_bass_kernel_spmd
from contextlib import ExitStack

F32 = mybir.dt.float32
BF16 = mybir.dt.bfloat16
I8 = mybir.dt.int8
AF = mybir.ActivationFunctionType
ALU = mybir.AluOpType
AX = mybir.AxisListType

D = 1024
H = 16
DH = 64
FF = 4096
C = 128
TPT = 256
NCHUNK = 2

E_SET = 6
G_SET = 10

ENGS = ("d", "t", "v", "s", "g")


class P:
    def __init__(self, nc):
        self.nc = nc
        self.q = {e: [] for e in ENGS}
        self.cnt = {}
        self.sems = {}
        self.lastwait = {e: {} for e in ENGS}
        self.act_set = None

    def new_sem(self, name):
        self.cnt[name] = 0

    def bump(self, sem, n):
        self.cnt[sem] += n
        return (sem, self.cnt[sem])

    def emit(self, eng, fn):
        self.q[eng].append(fn)

    def wait(self, eng, tok):
        if tok is None:
            return
        sem, val = tok
        if self.lastwait[eng].get(sem, 0) >= val:
            return
        self.lastwait[eng][sem] = val

        def f(e, sem=sem, val=val):
            e.wait_ge(self.sems[sem], val)
        self.emit(eng, f)

    def op(self, eng, fn, waits=(), inc=None, n=None):
        for w in waits:
            self.wait(eng, w)
        tok = None
        if inc:
            amt = n if n is not None else (16 if inc in ("SW", "SO") else 1)
            tok = self.bump(inc, amt)

        def g(e, fn=fn, tok=tok):
            i = fn(e)
            if tok is not None:
                amt = 16 if tok[0] in ("SW", "SO") else 1
                i.then_inc(self.sems[tok[0]], amt)
        self.emit(eng, g)
        return tok

    def tok(self, sem):
        return (sem, self.cnt[sem])

    def table(self, set_id):
        if self.act_set == set_id:
            return
        self.act_set = set_id

        def f(e, set_id=set_id):
            e.add_instruction(mybir.InstLoadActFuncSet(
                name=self.nc.get_next_instruction_name(),
                act_func_set_id=set_id, ins=[], outs=[]))
        self.emit("s", f)


def build(nm=16, nprf=16, dbg=None):
    dbg = dbg or set()
    nc = bass.Bass("TRN2", target_bir_lowering=False)

    di = lambda name, shape, dt=F32: nc.dram_tensor(name, shape, dt, kind="ExternalInput").ap()
    do = lambda name, shape, dt=F32: nc.dram_tensor(name, shape, dt, kind="ExternalOutput").ap()
    dint = lambda name, shape, dt=BF16: nc.dram_tensor(name, shape, dt, kind="Internal").ap()

    x_main = di("x_main", [4096, D])
    x_pref = di("x_pref", [4096, D])
    maskcol = di("maskcol", [128, 1])
    wq = di("wq", [D, D]); wk = di("wk", [D, D]); wv = di("wv", [D, D]); wo = di("wo", [D, D])
    proj = di("proj", [DH, DH])
    aw1 = di("aw1", [D, FF]); aw2 = di("aw2", [FF, D])
    fw1t = di("fw1t", [D, FF]); fw2t = di("fw2t", [FF, D])
    g1c = di("g1c", [128, 8]); g2c = di("g2c", [128, 8])
    b1c = di("b1c", [128, 8]); b2c = di("b2c", [128, 8])
    ab1r = di("ab1r", [1, FF]); ab2r = di("ab2r", [1, D])
    fb1r = di("fb1r", [1, FF]); fb2r = di("fb2r", [1, D])
    ident_bf = di("ident_bf", [128, 128], BF16)
    ident_f32 = di("ident_f32", [128, 128], F32)
    onescol_bf = di("onescol_bf", [1, 128], BF16)
    onescol_f32 = di("onescol_f32", [1, 128], F32)
    onesrow_bf = di("onesrow_bf", [1, 512], BF16)
    onestok_bf = di("onestok_bf", [128, 1], BF16)
    maskT4 = di("maskT4", [128, 512])

    y_out = do("y_out", [4096, D])

    scr = {}
    for name, nq in [("q", 1), ("k", 1), ("v", 1), ("o", 1),
                     ("aw1", 4), ("aw2", 4), ("f1", 4), ("f2", 4)]:
        scr[name] = dint("scr_" + name, [nq, 8, 128, 1024], BF16)

    dbg_outs = {}

    with ExitStack() as st_:
        E = st_.enter_context
        wb = [E(nc.sbuf_tensor(f"wb{i}", [128, 8192], BF16)) for i in range(2)]
        xin = [E(nc.sbuf_tensor(f"xin{i}", [128, 2048], F32)) for i in range(2)]
        xr0 = E(nc.sbuf_tensor("xr0", [128, 2048], F32))
        xr1 = E(nc.sbuf_tensor("xr1", [128, 2048], F32))
        ybuf = E(nc.sbuf_tensor("ybuf", [128, 2048], F32))
        hf = E(nc.sbuf_tensor("hf", [128, 2048], F32))
        hT = E(nc.sbuf_tensor("hT", [128, 2048], BF16))
        QRT = E(nc.sbuf_tensor("QRT", [64, 4096], BF16))
        KRT = E(nc.sbuf_tensor("KRT", [64, 4096], BF16))
        QT = QRT
        KT = KRT
        Kc = E(nc.sbuf_tensor("Kc", [128, 2048], BF16))
        Vc = E(nc.sbuf_tensor("Vc", [128, 2080], BF16))
        Ssb = E(nc.sbuf_tensor("Ssb", [128, 2048], BF16))
        attnT = E(nc.sbuf_tensor("attnT", [128, 2048], BF16))
        attn_tm = E(nc.sbuf_tensor("attn_tm", [128, 2048], BF16))
        stS = E(nc.sbuf_tensor("stS", [64, 1024], BF16))
        ksS = E(nc.sbuf_tensor("ksS", [64, 16], BF16))
        garena = E(nc.sbuf_tensor("garena", [128, 8192], BF16))
        ints2 = E(nc.sbuf_tensor("ints2", [128, 8192], BF16))
        xqb = attn_tm
        i8g = [E(nc.sbuf_tensor(f"i8g{i}", [128, 256], I8)) for i in range(2)]
        qk2 = E(nc.sbuf_tensor("qk2", [128, 2048], F32))
        gmb = [E(nc.sbuf_tensor(f"gmb{i}", [128, 256], BF16)) for i in range(2)]
        e1 = E(nc.sbuf_tensor("e1", [128, 512], F32))
        sst = E(nc.sbuf_tensor("sst", [128, 32], F32))
        sqd = E(nc.sbuf_tensor("sqd", [128, 1024], BF16))
        stats = E(nc.sbuf_tensor("stats", [128, 64], F32))
        dens = E(nc.sbuf_tensor("dens", [128, 16], F32))
        recips = E(nc.sbuf_tensor("recips", [128, 16], F32))
        mrow = E(nc.sbuf_tensor("mrow", [1, 256], BF16))
        rowf = E(nc.sbuf_tensor("rowf", [1, 256], F32))
        Bsb = E(nc.sbuf_tensor("Bsb", [128, 256], BF16))
        Id = E(nc.sbuf_tensor("Id", [128, 128], BF16))
        Idf = E(nc.sbuf_tensor("Idf", [128, 128], F32))
        McT = E(nc.sbuf_tensor("McT", [128, 512], F32))
        Onc = E(nc.sbuf_tensor("Onc", [1, 128], BF16))
        Oncf = E(nc.sbuf_tensor("Oncf", [1, 128], F32))
        Onr = E(nc.sbuf_tensor("Onr", [1, 512], BF16))
        Ontk = E(nc.sbuf_tensor("Ontk", [128, 1], BF16))
        Mcol = E(nc.sbuf_tensor("Mcol", [128, 1], F32))
        projb = E(nc.sbuf_tensor("projb", [64, 64], BF16))
        projf = E(nc.sbuf_tensor("projf", [64, 64], F32))
        g1b = E(nc.sbuf_tensor("g1b", [128, 8], F32))
        g2b = E(nc.sbuf_tensor("g2b", [128, 8], F32))
        b1f = E(nc.sbuf_tensor("b1f", [128, 8], F32))
        b2f = E(nc.sbuf_tensor("b2f", [128, 8], F32))
        b1b = E(nc.sbuf_tensor("b1b", [128, 8], BF16))
        b2b = E(nc.sbuf_tensor("b2b", [128, 8], BF16))
        bqr = E(nc.sbuf_tensor("bqr", [1, 1024], BF16))
        bkr = E(nc.sbuf_tensor("bkr", [1, 1024], BF16))
        bvr = E(nc.sbuf_tensor("bvr", [1, 1024], BF16))
        baw1 = E(nc.sbuf_tensor("baw1", [1, 4096], BF16))
        ab2b = E(nc.sbuf_tensor("ab2b", [1, 1024], BF16))
        fb1b = E(nc.sbuf_tensor("fb1b", [1, 4096], BF16))
        fb2tile = E(nc.sbuf_tensor("fb2tile", [128, 1024], F32))
        fb2b = E(nc.sbuf_tensor("fb2b", [1, 1024], BF16))
        rtmp = ybuf[0:1, 0:512]
        rtmp2 = ybuf[0:1, 1024:2048]
        swc = E(nc.sbuf_tensor("swc", [128, 4], F32))
        tiny = E(nc.sbuf_tensor("tiny", [1, 8], F32))
        tiny2 = E(nc.sbuf_tensor("tiny2", [128, 1], F32))
        st0 = E(nc.psum_tensor("st0", [64, 512], F32))
        st1 = E(nc.psum_tensor("st1", [64, 512], F32))
        ksb = E(nc.psum_tensor("ksb", [64, 16], F32))
        pmm = [E(nc.psum_tensor(f"pmm{i}", [128, 512], F32)) for i in range(4)]
        pbf = E(nc.psum_tensor("pbf", [128, 1024], BF16))
        sem_names = ["SW", "SO", "ST", "SV", "SS", "SP"]
        sems = {nm_: E(nc.semaphore(nm_)) for nm_ in sem_names}

        p = P(nc)
        p.sems = sems
        for s_ in sem_names:
            p.new_sem(s_)

        gare_f32 = garena[:].bitcast(F32)
        qkraw = qk2[:]
        e2 = gare_f32[:, 0:512]
        e3 = gare_f32[:, 512:1024]
        i8t = e1[:].bitcast(I8)
        ptmp = gare_f32[:, 2048:4096]
        ptmp2 = gare_f32[:, 0:1024]
        i8p = ints2[:].bitcast(I8)

        SC = {"ln1": 0, "ln2": 8, "oln": 16, "fln": 24}

        pmm_free = [None] * 4
        pbf_free = [None]
        wb_free = [None, None]
        wb_ready = [None, None]

        def dma_in(dst, src, queue="d", waits=()):
            return p.op(queue, lambda e, d=dst, s=src: e.dma_start(out=d, in_=s),
                        waits=waits, inc="SW" if queue == "d" else "SO")

        def dump(name, src_ap, shape, dt=F32, waits=()):
            if name not in dbg:
                return
            ap = do("dbg_" + name, shape, dt)
            dbg_outs[name] = ap
            tok = p.op("g", lambda e, a=ap, s=src_ap: e.dma_start(out=a, in_=s),
                       waits=waits, inc="SO")
            for eng in ENGS:
                p.wait(eng, tok)

        for dst, src in [(Id[:], ident_bf), (Idf[:], ident_f32), (Onc[:], onescol_bf),
                         (Oncf[:], onescol_f32), (Onr[:], onesrow_bf), (Ontk[:], onestok_bf),
                         (McT[:], maskT4), (Mcol[:], maskcol),
                         (g1b[:], g1c),
                         (g2b[:], g2c),
                         (projf[:], proj),
                         (b1f[:], b1c),
                         (b2f[:], b2c)]:
            dma_in(dst, src)
        tok_consts = p.tok("SW")

        p.op("v", lambda e: e.tensor_copy(out=projb[:], in_=projf[:]),
             waits=[tok_consts], inc="SV")
        tok_projb = p.tok("SV")
        p.op("v", lambda e: e.tensor_copy(out=b1b[:], in_=b1f[:]), inc="SV")
        p.op("v", lambda e: e.tensor_copy(out=b2b[:], in_=b2f[:]), inc="SV")
        for dst_b, src_r, n512 in [(ab2b, ab2r, 2), (fb1b, fb1r, 8), (fb2b, fb2r, 2)]:
            for i in range(n512):
                tok = dma_in(rtmp, src_r[0:1, i * 512:(i + 1) * 512], waits=[p.tok("SV")])
                p.op("v", lambda e, d=dst_b, i=i: e.tensor_copy(
                    out=d[0:1, i * 512:(i + 1) * 512], in_=rtmp), waits=[tok], inc="SV")
        tok_smallcasts = p.tok("SV")

        p.op("g", lambda e: e.memset(Vc[:], 1.0), inc="SP")
        p.op("g", lambda e: e.memset(stS[:], 0.0), inc="SP")
        p.op("g", lambda e: e.memset(ksS[:], 0.0), inc="SP")
        tok_memset = p.tok("SP")

        p.table(E_SET)

        tok_bias = {}

        stage_guard = [None, None]

        def prep_linear(w_ap, scr_name, gcol, bias_dst, beta_b, extra_row):
            nq = scr[scr_name].shape[0]
            kmajor = scr_name in ("aw2",)
            for q in range(nq):
                cast_tok = [None] * 8
                for k in range(8):
                    if kmajor:
                        rows, cols = (q * 8 + k) * 128, 0
                    else:
                        rows, cols = k * 128, q * 1024
                    hh = k % 2
                    w = [cast_tok[k - 2]] if k >= 2 else                         ([stage_guard[hh]] if stage_guard[hh] else [])
                    pv = ptmp[:, hh * 1024:hh * 1024 + 1024]
                    tok = dma_in(pv, w_ap[rows:rows + 128, cols:cols + 1024], waits=w)
                    if bias_dst is not None:
                        p.op("v", lambda e, k=k, pv=pv: e.tensor_copy(
                            out=wb[1][:, k * 1024:(k + 1) * 1024], in_=pv),
                            waits=[tok] + ([wb_free[1]] if wb_free[1] else []), inc="SV")
                    if gcol is not None:
                        p.op("v", lambda e, k=k, g=gcol, pv=pv: e.tensor_scalar(
                            out=wb[0][:, k * 1024:(k + 1) * 1024], in0=pv,
                            scalar1=g[:, k:k + 1], scalar2=None, op0=ALU.mult),
                            waits=[tok, tok_consts] + ([wb_free[0]] if wb_free[0] else []),
                            inc="SV")
                    else:
                        p.op("v", lambda e, k=k, pv=pv: e.tensor_copy(
                            out=wb[0][:, k * 1024:(k + 1) * 1024], in_=pv),
                            waits=[tok] + ([wb_free[0]] if wb_free[0] else []), inc="SV")
                    cast_tok[k] = p.tok("SV")
                stage_guard[0] = cast_tok[6]
                stage_guard[1] = cast_tok[7]
                tok_stage = p.tok("SV")
                tok_bmm = None
                if bias_dst is not None:
                    for n in range(2):
                        for k in range(8):
                            p.op("t", lambda e, k=k, n=n, bb=beta_b: e.matmul(
                                out=pmm[3][0:1, 0:512],
                                lhsT=bb[:, k:k + 1],
                                rhs=wb[1][:, k * 1024 + n * 512: k * 1024 + (n + 1) * 512],
                                start=(k == 0), stop=(k == 7)),
                                waits=([tok_stage, tok_smallcasts] +
                                       ([pmm_free[3]] if pmm_free[3] else [])) if k == 0 else [],
                                inc=("ST" if k == 7 else None))
                        tok_bmm = p.tok("ST")
                        off = q * 1024 + n * 512
                        if extra_row is None:
                            p.op("v", lambda e, off=off, bd=bias_dst: e.tensor_copy(
                                out=bd[0:1, off:off + 512],
                                in_=pmm[3][0:1, 0:512]), waits=[tok_bmm], inc="SV")
                        else:
                            tok_er = dma_in(rtmp, extra_row[0:1, off:off + 512],
                                            waits=[p.tok("SV")])
                            p.op("v", lambda e, off=off, bd=bias_dst: e.tensor_tensor(
                                out=bd[0:1, off:off + 512],
                                in0=pmm[3][0:1, 0:512],
                                in1=rtmp, op=ALU.add),
                                waits=[tok_bmm, tok_er], inc="SV")
                        pmm_free[3] = p.tok("SV")
                    tok_bias[scr_name] = p.tok("SV")
                    wb_free[1] = tok_bmm
                w_ = [tok_stage] + ([tok_bmm] if tok_bmm else [])
                for k in range(8):
                    wb_free[0] = p.op("d", lambda e, q=q, sn=scr_name, k=k: e.dma_start(
                        out=scr[sn][q, k], in_=wb[0][:, k * 1024:(k + 1) * 1024]),
                        waits=(w_ if k == 0 else []), inc="SW")

        prep_linear(wq, "q", g1b, bqr, b1b, None)
        prep_linear(wk, "k", g1b, bkr, b1b, None)
        prep_linear(wv, "v", g1b, bvr, b1b, None)
        prep_linear(wo, "o", None, None, None, None)
        prep_linear(aw1, "aw1", g2b, baw1, b2b, ab1r)
        prep_linear(aw2, "aw2", None, None, None, None)

        for n in range(2):
            p.op("t", lambda e, n=n: e.matmul(
                out=pmm[3][:, 0:512], lhsT=Onc[:],
                rhs=fb2b[0:1, n * 512:(n + 1) * 512], start=True, stop=True),
                waits=[tok_smallcasts, tok_consts] + ([pmm_free[3]] if pmm_free[3] else []),
                inc="ST")
            tok_fb2 = p.tok("ST")
            p.op("v", lambda e, n=n: e.tensor_copy(
                out=fb2tile[:, n * 512:(n + 1) * 512], in_=pmm[3][:, 0:512]),
                waits=[tok_fb2], inc="SV")
            pmm_free[3] = p.tok("SV")

        def prep_ternary(w_ap, scr_name, sw_col_idx):
            abscol = stats[:, 32:64]
            nr, ncol = w_ap.shape[0] // 128, w_ap.shape[1] // 1024
            ci = 0
            abs_tok = [None] * 34
            for rr in range(nr):
                for cc in range(ncol):
                    buf = ptmp[:, (ci % 2) * 1024:(ci % 2) * 1024 + 1024]
                    w = [abs_tok[ci - 2]] if ci >= 2 else                         ([stage_guard[ci % 2]] if stage_guard[ci % 2] else [])
                    tok = dma_in(buf, w_ap[rr * 128:(rr + 1) * 128, cc * 1024:(cc + 1) * 1024],
                                 waits=w)
                    p.op("s", lambda e, ci=ci, buf=buf: e.activation(
                        out=sqd[:], in_=buf,
                        func=AF.Abs, accum_out=abscol[:, ci:ci + 1]),
                        waits=[tok], inc="SS")
                    abs_tok[ci] = p.tok("SS")
                    ci += 1
            tok_abs = p.tok("SS")
            stage_guard[0] = abs_tok[ci - 2]
            stage_guard[1] = abs_tok[ci - 1]
            p.op("v", lambda e: e.reduce_sum(out=tiny2[:, 0:1], in_=abscol, axis=AX.X),
                 waits=[tok_abs], inc="SV")
            tok_red = p.tok("SV")
            p.op("t", lambda e: e.transpose(out=pmm[3][0:1, 0:128].bitcast(F32),
                                            in_=tiny2[:, 0:1], identity=Idf[:]),
                 waits=([tok_red, tok_consts] + ([pmm_free[3]] if pmm_free[3] else [])),
                 inc="ST")
            tok_tr = p.tok("ST")
            p.op("v", lambda e: e.tensor_copy(out=rowf[0:1, 0:128], in_=pmm[3][0:1, 0:128]),
                 waits=[tok_tr], inc="SV")
            pmm_free[3] = p.tok("SV")
            p.op("v", lambda e: e.reduce_sum(out=tiny[0:1, 0:1], in_=rowf[0:1, 0:128],
                                             axis=AX.X), waits=[p.tok("SV")], inc="SV")
            p.op("v", lambda e: e.tensor_scalar(out=tiny[0:1, 1:2], in0=tiny[0:1, 0:1],
                                                scalar1=1.0 / (4096.0 * 1024.0), scalar2=1e-5,
                                                op0=ALU.mult, op1=ALU.max),
                 waits=[p.tok("SV")], inc="SV")
            p.op("v", lambda e: e.reciprocal(out=tiny[0:1, 2:3], in_=tiny[0:1, 1:2]),
                 waits=[p.tok("SV")], inc="SV")
            tok_sc = p.tok("SV")
            p.op("t", lambda e: e.matmul(out=pmm[3][:, 0:2].bitcast(F32), lhsT=Oncf[:],
                                         rhs=tiny[0:1, 1:3], start=True, stop=True),
                 waits=[tok_sc, pmm_free[3]], inc="ST")
            tok_bc = p.tok("ST")
            p.op("v", lambda e, i=sw_col_idx: e.tensor_copy(out=swc[:, i:i + 1],
                                                            in_=pmm[3][:, 0:1]),
                 waits=[tok_bc], inc="SV")
            p.op("v", lambda e, i=sw_col_idx: e.tensor_copy(out=swc[:, i + 2:i + 3],
                                                            in_=pmm[3][:, 1:2]), inc="SV")
            pmm_free[3] = p.tok("SV")
            tok_sw = p.tok("SV")
            kmajor = scr_name == "f2"
            t2_tok = [None] * 34
            gi = 0
            for q in range(4):
                for k in range(8):
                    if kmajor:
                        rows, cols = (q * 8 + k) * 128, 0
                    else:
                        rows, cols = k * 128, q * 1024
                    hh = gi % 2
                    pv = ptmp[:, hh * 1024:hh * 1024 + 1024]
                    w = [t2_tok[gi - 2]] if gi >= 2 else                         ([stage_guard[hh]] if stage_guard[hh] else [])
                    tok = dma_in(pv, w_ap[rows:rows + 128, cols:cols + 1024], waits=w)
                    p.op("v", lambda e, i=sw_col_idx, pv=pv: e.tensor_scalar(
                        out=i8p[:, 0:1024], in0=pv,
                        scalar1=swc[:, i + 2:i + 3], scalar2=None, op0=ALU.mult),
                        waits=[tok, tok_sw, p.tok("SV")], inc="SV")
                    t2_tok[gi] = p.tok("SV")
                    gi += 1
                    p.op("v", lambda e: e.tensor_copy(out=ptmp2[:], in_=i8p[:, 0:1024]),
                         waits=[p.tok("SV")], inc="SV")
                    p.op("v", lambda e, k=k: e.tensor_scalar(
                        out=wb[0][:, k * 1024:(k + 1) * 1024], in0=ptmp2[:],
                        scalar1=1.0, scalar2=-1.0, op0=ALU.min, op1=ALU.max),
                        waits=([wb_free[0]] if wb_free[0] else []) + [p.tok("SV")], inc="SV")
                tok_stage = p.tok("SV")
                stage_guard[0] = stage_guard[1] = tok_stage
                for k in range(8):
                    wb_free[0] = p.op("d", lambda e, q=q, sn=scr_name, k=k: e.dma_start(
                        out=scr[sn][q, k], in_=wb[0][:, k * 1024:(k + 1) * 1024]),
                        waits=([tok_stage] if k == 0 else []), inc="SW")

        prep_ternary(fw1t, "f1", 0)
        prep_ternary(fw2t, "f2", 1)

        dump("bqr", bqr[:], [1, 1024], BF16, waits=[p.tok("SV"), p.tok("ST")])
        dump("swc", swc[:], [128, 4], F32, waits=[p.tok("SV")])
        dump("scrf1", scr["f1"][0, 0], [128, 1024], BF16, waits=[p.tok("SW")])
        dump("scrq", scr["q"][0, 0], [128, 1024], BF16, waits=[p.tok("SW")])
        dump("scrqall", scr["q"][0].rearrange("k p f -> (k p) f"), [1024, 1024], BF16,
             waits=[p.tok("SW")])
        dump("scraw1", scr["aw1"][:].rearrange("q k p f -> (q k p) f"), [4096, 1024], BF16,
             waits=[p.tok("SW")])
        dump("baw1", baw1[:], [1, 4096], BF16, waits=[p.tok("SV"), p.tok("ST")])

        gslot = {"i": 0}

        def load_slot(scr_name, q):
            i = gslot["i"] % 2
            gslot["i"] += 1
            tok = None
            for k in range(8):
                tok = p.op("d", lambda e, sn=scr_name, q=q, i=i, k=k: e.dma_start(
                    out=wb[i][:, k * 1024:(k + 1) * 1024], in_=scr[sn][q, k]),
                    waits=([wb_free[i]] if (k == 0 and wb_free[i]) else []), inc="SW")
            wb_ready[i] = tok
            return i

        def done_slot(i, tok):
            wb_free[i] = tok

        site_guard = {"ln1": None, "ln2": None, "oln": None, "fln": None,
                      "q1": None, "q2": None}

        def ln_stats(xap, site, waits, eps=1e-5):
            b = SC[site]
            gw = [site_guard[site]] if site_guard[site] else []
            for c in range(NCHUNK):
                p.op("s", lambda e, c=c, b=b, xap=xap: e.activation(
                    out=sqd[:], in_=xap[:, c * 1024:(c + 1) * 1024], func=AF.Square,
                    accum_out=stats[:, b + 2 + c:b + 3 + c]),
                    waits=list(waits) + gw, inc="SS")
                p.op("v", lambda e, c=c, b=b, xap=xap: e.reduce_sum(
                    out=stats[:, b + c:b + 1 + c], in_=xap[:, c * 1024:(c + 1) * 1024],
                    axis=AX.X), waits=waits, inc="SV")
            tok_s = p.tok("SS")
            tok_v = p.tok("SV")
            p.op("v", lambda e, b=b: e.tensor_scalar(
                out=stats[:, b + 4:b + 6], in0=stats[:, b:b + 2],
                scalar1=1.0 / D, scalar2=None, op0=ALU.mult),
                waits=[tok_v, p.tok("SV")], inc="SV")
            p.op("v", lambda e, b=b: e.tensor_scalar(
                out=stats[:, b + 6:b + 8], in0=stats[:, b + 2:b + 4],
                scalar1=1.0 / D, scalar2=None, op0=ALU.mult),
                waits=[tok_s, p.tok("SV")], inc="SV")
            p.op("v", lambda e, b=b: e.tensor_tensor(
                out=sst[:, 0:2], in0=stats[:, b + 4:b + 6], in1=stats[:, b + 4:b + 6],
                op=ALU.mult), waits=[p.tok("SV")], inc="SV")
            p.op("v", lambda e, b=b: e.tensor_tensor(
                out=sst[:, 28:30], in0=stats[:, b + 6:b + 8], in1=sst[:, 0:2],
                op=ALU.subtract), waits=[p.tok("SV")], inc="SV")
            p.op("v", lambda e, b=b, eps=eps: e.tensor_scalar(
                out=stats[:, b + 6:b + 8], in0=sst[:, 28:30],
                scalar1=eps, scalar2=None, op0=ALU.add), waits=[p.tok("SV")], inc="SV")
            tok_var = p.tok("SV")
            p.table(E_SET)
            p.op("s", lambda e, b=b: e.activation(out=sst[:, 2:4], in_=stats[:, b + 6:b + 8],
                                                  func=AF.Ln),
                 waits=[tok_var], inc="SS")
            p.op("s", lambda e, b=b: e.activation(out=stats[:, b + 6:b + 8], in_=sst[:, 2:4],
                                                  func=AF.Exp, scale=-0.5),
                 waits=[p.tok("SS")], inc="SS")
            return p.tok("SS")

        def ln_apply(xap, site, outap, tok_rstd, waits=()):
            b = SC[site]
            for c in range(NCHUNK):
                p.op("v", lambda e, c=c, b=b, xap=xap, outap=outap: e.tensor_scalar(
                    out=outap[:, c * 1024:(c + 1) * 1024],
                    in0=xap[:, c * 1024:(c + 1) * 1024],
                    scalar1=stats[:, b + 4 + c:b + 5 + c],
                    scalar2=stats[:, b + 6 + c:b + 7 + c],
                    op0=ALU.subtract, op1=ALU.mult),
                    waits=list(waits) + [tok_rstd, p.tok("SV")], inc="SV")
            return p.tok("SV")

        def transpose_2k(src_f32, tok_src, dst_bf):
            for w in range(4):
                bank = pmm[w % 2]
                for i in range(4):
                    d = w * 4 + i
                    c, j = d % 2, d // 2
                    p.op("t", lambda e, c=c, j=j, i=i, bank=bank, src_f32=src_f32: e.transpose(
                        out=bank[:, i * 128:(i + 1) * 128],
                        in_=src_f32[:, c * 1024 + j * 128: c * 1024 + (j + 1) * 128],
                        identity=Idf[:]),
                        waits=([tok_src, tok_consts] +
                               ([pmm_free[w % 2]] if pmm_free[w % 2] else [])) if i == 0 else [],
                        inc=("ST" if i == 3 else None))
                tok_t = p.tok("ST")
                p.op("v", lambda e, w=w, bank=bank, dst_bf=dst_bf: e.tensor_copy(
                    out=dst_bf[:, w * 512:(w + 1) * 512], in_=bank[:]),
                    waits=[tok_t], inc="SV")
                pmm_free[w % 2] = p.tok("SV")
            return p.tok("SV")

        def proj_tm(wslot_i, tok_w, tok_hT, bias_row, tok_b, out_f32=None, out_vc=False,
                    evac_waits=()):
            last = None
            for c in range(NCHUNK):
                for n in range(2):
                    bi = (c * 2 + n) % 2
                    bank = pmm[bi]
                    p.op("t", lambda e, n=n, bank=bank, br=bias_row: e.matmul(
                        out=bank[:], lhsT=Onc[:], rhs=br[0:1, n * 512:(n + 1) * 512],
                        start=True, stop=False),
                        waits=[tok_w, tok_hT, tok_b, tok_consts] +
                              ([pmm_free[bi]] if pmm_free[bi] else []))
                    for k in range(8):
                        p.op("t", lambda e, c=c, n=n, k=k, i=wslot_i, bank=bank: e.matmul(
                            out=bank[:], lhsT=hT[:, (k * 2 + c) * 128:(k * 2 + c + 1) * 128],
                            rhs=wb[i][:, k * 1024 + n * 512: k * 1024 + (n + 1) * 512],
                            start=False, stop=(k == 7)), inc=("ST" if k == 7 else None))
                    tok_mm = p.tok("ST")
                    if out_vc:
                        vview = Vc[:].rearrange("p (cc hh s) -> p cc hh s", cc=2, hh=16)[:, c, n * 8:(n + 1) * 8, 0:64]
                        pview = bank[:].rearrange("p (hh s) -> p hh s", hh=8)
                        p.op("v", lambda e, vv=vview, pv=pview: e.tensor_copy(out=vv, in_=pv),
                             waits=[tok_mm, tok_memset], inc="SV")
                    else:
                        p.op("v", lambda e, c=c, n=n, bank=bank, of=out_f32: e.tensor_copy(
                            out=of[:, c * 1024 + n * 512: c * 1024 + (n + 1) * 512],
                            in_=bank[:]), waits=[tok_mm] + list(evac_waits), inc="SV")
                    pmm_free[bi] = p.tok("SV")
                    last = p.tok("SV")
            return last

        def headT(src_f32, tok_src, dst_bf64):
            for w in range(8):
                bank = pmm[w % 2]
                for i in range(4):
                    d = w * 4 + i
                    h, c = d // 2, d % 2
                    p.op("t", lambda e, h=h, c=c, i=i, bank=bank, src_f32=src_f32: e.transpose(
                        out=bank[0:64, i * 128:(i + 1) * 128],
                        in_=src_f32[:, c * 1024 + h * 64: c * 1024 + (h + 1) * 64],
                        identity=Idf[:]),
                        waits=([tok_src, tok_consts] +
                               ([pmm_free[w % 2]] if pmm_free[w % 2] else [])) if i == 0 else [],
                        inc=("ST" if i == 3 else None))
                tok_t = p.tok("ST")
                p.op("v", lambda e, w=w, bank=bank, dst=dst_bf64: e.tensor_copy(
                    out=dst[:, w * 512:(w + 1) * 512], in_=bank[0:64, :]),
                    waits=[tok_t], inc="SV")
                pmm_free[w % 2] = p.tok("SV")
            return p.tok("SV")

        def phi(src64, tok_src, dst64):
            p.table(E_SET)
            for w in range(8):
                bi = 2 + (w % 2)
                bank = pmm[bi]
                for i in range(2):
                    h = w * 2 + i
                    p.op("t", lambda e, h=h, i=i, bank=bank, src64=src64: e.matmul(
                        out=bank[0:64, i * 256:(i + 1) * 256], lhsT=projb[:],
                        rhs=src64[:, h * 256:(h + 1) * 256], start=True, stop=True),
                        waits=([tok_src, tok_projb] +
                               ([pmm_free[bi]] if pmm_free[bi] else [])) if i == 0 else [],
                        inc=("ST" if i == 1 else None))
                tok_mm = p.tok("ST")
                p.op("v", lambda e, bank=bank: e.tensor_scalar(
                    out=e1[0:64, :], in0=bank[0:64, :], scalar1=0.0, scalar2=None,
                    op0=ALU.min), waits=[tok_mm, p.tok("SP")], inc="SV")
                tok_min = p.tok("SV")
                p.op("v", lambda e, bank=bank: e.tensor_scalar(
                    out=e2[0:64, :], in0=bank[0:64, :], scalar1=0.0, scalar2=None,
                    op0=ALU.max), inc="SV")
                pmm_free[bi] = p.tok("SV")
                tok_max = p.tok("SV")
                p.op("s", lambda e: e.activation(out=e3[0:64, :], in_=e1[0:64, :],
                                                 func=AF.Exp),
                     waits=[tok_min, p.tok("SP")], inc="SS")
                tok_exp = p.tok("SS")
                p.op("g", lambda e, w=w, dst64=dst64: e.tensor_tensor(
                    out=dst64[:, w * 512:(w + 1) * 512], in0=e3[0:64, :], in1=e2[0:64, :],
                    op=ALU.add), waits=[tok_exp, tok_max], inc="SP")
            return p.tok("SP")

        def kc_transposes(tok_kt):
            for c in range(NCHUNK):
                for h in range(16):
                    p.op("t", lambda e, c=c, h=h: e.transpose(
                        out=pbf[:, h * 64:(h + 1) * 64],
                        in_=KT[:, h * 256 + c * 128: h * 256 + (c + 1) * 128],
                        identity=Id[0:64, 0:64]),
                        waits=([tok_kt, tok_consts] +
                               ([pbf_free[0]] if pbf_free[0] else [])) if h == 0 else [],
                        inc=("ST" if h == 15 else None))
                tok_t = p.tok("ST")
                p.op("v", lambda e, c=c: e.tensor_copy(
                    out=Kc[:, c * 1024:(c + 1) * 1024], in_=pbf[:, 0:1024]),
                    waits=[tok_t], inc="SV")
                pbf_free[0] = p.tok("SV")
            return p.tok("SV")

        state_started = {"st0": False, "st1": False, "ksb": False}
        state_ready = {"tok": None}
        stS_read = {"tok": None}

        def state_update(kc_src, tok_kc, tok_vc):
            for c in range(NCHUNK):
                emit_state_chunk(kc_src, c, tok_kc, tok_vc)

        def emit_state_chunk(kc_src, c, tok_kc, tok_vc):
            for h in range(16):
                bank = st0 if h < 8 else st1
                sname = "st0" if h < 8 else "st1"
                first = not state_started[sname]
                state_started[sname] = True
                w = []
                if h == 0:
                    w = [tok_kc, tok_vc, tok_memset]
                    if stS_read["tok"] is not None:
                        w.append(stS_read["tok"])
                p.op("t", lambda e, c=c, h=h, bank=bank, first=first, kc_src=kc_src: e.matmul(
                    out=bank[0:64, (h % 8) * 64:(h % 8 + 1) * 64],
                    lhsT=kc_src[:, (c * 16 + h) * 64:(c * 16 + h + 1) * 64],
                    rhs=Vc[:, c * 1040 + h * 65: c * 1040 + h * 65 + 64],
                    start=first, stop=True), waits=w)
                firstk = not state_started["ksb"]
                state_started["ksb"] = True
                p.op("t", lambda e, c=c, h=h, firstk=firstk, kc_src=kc_src: e.matmul(
                    out=ksb[0:64, h:h + 1],
                    lhsT=kc_src[:, (c * 16 + h) * 64:(c * 16 + h + 1) * 64],
                    rhs=Ontk[:], start=firstk, stop=True),
                    inc=("ST" if h == 15 else None))
            state_ready["tok"] = p.tok("ST")

        tok_x = [None, None]
        pref_xin_used = [None, None]
        slot_kv = [None, None]

        if nprf > 0:
            tok_x[0] = dma_in(xin[0][:].rearrange("p (c dd) -> p c dd", dd=1024),
                              x_pref[0:TPT].rearrange("(c p) dd -> p c dd", p=128))
            slot_kv[0] = load_slot("k", 0)
            slot_kv[1] = load_slot("v", 0)
            tok_wk = wb_ready[slot_kv[0]]
            tok_wv = wb_ready[slot_kv[1]]
            for tile in range(nprf):
                i = tile % 2
                tok_here = tok_x[i]
                if tile + 1 < nprf:
                    w = [pref_xin_used[(tile + 1) % 2]] if pref_xin_used[(tile + 1) % 2] else []
                    tok_x[(tile + 1) % 2] = p.op("d", lambda e, t2=tile + 1: e.dma_start(
                        out=xin[t2 % 2][:].rearrange("p (c dd) -> p c dd", dd=1024),
                        in_=x_pref[t2 * TPT:(t2 + 1) * TPT].rearrange("(c p) dd -> p c dd", p=128)),
                        waits=w, inc="SW")
                xa = xin[i][:]
                tok_rstd = ln_stats(xa, "ln1", [tok_here])
                tok_h = ln_apply(xa, "ln1", hf[:], tok_rstd, waits=[tok_here])
                pref_xin_used[i] = tok_h
                tok_hT = transpose_2k(hf[:], tok_h, hT[:])
                tok_kraw = proj_tm(slot_kv[0], tok_wk, tok_hT, bkr, tok_bias["k"],
                                   out_f32=qkraw)
                tok_v = proj_tm(slot_kv[1], tok_wv, tok_hT, bvr, tok_bias["v"], out_vc=True)
                tok_krt = headT(qkraw, tok_kraw, KRT[:])
                tok_kt = phi(KRT[:], tok_krt, KT[:])
                tok_kc = kc_transposes(tok_kt)
                p.op("v", lambda e: e.tensor_scalar(
                    out=attn_tm[:], in0=Kc[:], scalar1=Mcol[:, 0:1], scalar2=None,
                    op0=ALU.mult), waits=[tok_kc, tok_consts], inc="SV")
                tok_kcm = p.tok("SV")
                if tile == nprf - 1:
                    dump("kcm", attn_tm[:], [128, 2048], BF16, waits=[tok_kcm])
                site_guard["ln1"] = tok_kcm
                state_update(attn_tm[:], tok_kcm, tok_v)
            done_slot(slot_kv[0], state_ready["tok"])
            done_slot(slot_kv[1], state_ready["tok"])

        tok_x = [None, None]
        xin_used = [None, None]
        tok_bl2_prev = {"tok": None}
        w0 = [state_ready["tok"]] if state_ready["tok"] else []
        if nprf > 0:
            w0 += [pref_xin_used[0]] if pref_xin_used[0] else []
        tok_x[0] = p.op("d", lambda e: e.dma_start(
            out=xin[0][:].rearrange("p (c dd) -> p c dd", dd=1024),
            in_=x_main[0:TPT].rearrange("(c p) dd -> p c dd", p=128)),
            waits=w0, inc="SW")
        tok_y_dma = None

        front_toks = {}

        def emit_front(tile, garena_free):
            i = tile % 2
            tok_here = tok_x[i]
            xa = xin[i][:]
            tok_rstd = ln_stats(xa, "ln1", [tok_here])
            tok_h = ln_apply(xa, "ln1", hf[:], tok_rstd, waits=[tok_here])
            tok_hT = transpose_2k(hf[:], tok_h, hT[:])

            if tile == 0:
                dump("xin", xa, [128, 2048], F32, waits=[tok_here])
                dump("stats", stats[:], [128, 64], F32, waits=[tok_rstd, tok_h])
                dump("hf", hf[:], [128, 2048], F32, waits=[tok_h])
                dump("hT", hT[:], [128, 2048], BF16, waits=[tok_hT])
            gfw = [garena_free] if garena_free else []
            sq_ = load_slot("q", 0)
            tok_qraw = proj_tm(sq_, wb_ready[sq_], tok_hT, bqr, tok_bias["q"], out_f32=qkraw,
                               evac_waits=gfw)
            if tile == 0:
                dump("qraw", qkraw, [128, 2048], F32, waits=[tok_qraw])
            tok_qrt = headT(qkraw, tok_qraw, QRT[:])
            done_slot(sq_, tok_qrt)
            sk_ = load_slot("k", 0)
            tok_kraw = proj_tm(sk_, wb_ready[sk_], tok_hT, bkr, tok_bias["k"], out_f32=qkraw,
                               evac_waits=gfw)
            tok_krt = headT(qkraw, tok_kraw, KRT[:])
            done_slot(sk_, tok_krt)
            sv_ = load_slot("v", 0)
            tok_v = proj_tm(sv_, wb_ready[sv_], tok_hT, bvr, tok_bias["v"], out_vc=True)
            done_slot(sv_, tok_v)
            site_guard["ln1"] = tok_h
            front_toks[tile] = dict(i=i, tok_here=tok_here, xa=xa,
                                    tok_qrt=tok_qrt, tok_krt=tok_krt,
                                    tok_v=tok_v)

        def emit_front_b(tile):
            F = front_toks[tile]
            tok_qrt = F["tok_qrt"]; tok_krt = F["tok_krt"]; tok_v = F["tok_v"]
            tok_qt = phi(QRT[:], tok_qrt, QT[:])
            tok_kt = phi(KRT[:], tok_krt, KT[:])
            tok_kc = kc_transposes(tok_kt)

            if tile == 0:
                dump("QT", QT[:], [64, 4096], BF16, waits=[tok_qt])
                dump("KT", KT[:], [64, 4096], BF16, waits=[tok_kt])
                dump("Vc", Vc[:], [128, 2080], BF16, waits=[tok_v])
                dump("Kc", Kc[:], [128, 2048], BF16, waits=[tok_kc])

            F["tok_qt"] = tok_qt
            F["tok_kt"] = tok_kt
            F["tok_kc"] = tok_kc


        def emit_back1(tile):
            F = front_toks[tile]
            i = F["i"]; tok_here = F["tok_here"]; xa = F["xa"]
            tok_qt = F["tok_qt"]; tok_kt = F["tok_kt"]
            tok_kc = F["tok_kc"]; tok_v = F["tok_v"]
            if tile + 1 < nm:
                wprev = []
                if tile == 0 and nprf > 0 and pref_xin_used[1]:
                    wprev = [pref_xin_used[1]]
                elif xin_used[(tile + 1) % 2]:
                    wprev = [xin_used[(tile + 1) % 2]]
                tok_x[(tile + 1) % 2] = p.op("d", lambda e, t2=tile + 1: e.dma_start(
                    out=xin[t2 % 2][:].rearrange("p (c dd) -> p c dd", dd=1024),
                    in_=x_main[t2 * TPT:(t2 + 1) * TPT].rearrange("(c p) dd -> p c dd", p=128)),
                    waits=wprev, inc="SW")
            xa = xin[i][:]
            for c in range(NCHUNK):
                if state_ready["tok"] is not None:
                    p.op("v", lambda e: e.tensor_copy(out=stS[0:64, 0:512], in_=st0[:]),
                         waits=[state_ready["tok"], tok_memset], inc="SV")
                    p.op("v", lambda e: e.tensor_copy(out=stS[0:64, 512:1024], in_=st1[:]),
                         inc="SV")
                    p.op("v", lambda e: e.tensor_copy(out=ksS[:], in_=ksb[:]), inc="SV")
                    stS_read["tok"] = p.tok("SV")
                tok_st_copy = p.tok("SV")
                if tile == 0 and c == 0:
                    dump("stS0", stS[:], [64, 1024], BF16, waits=[tok_st_copy])
                    dump("ksS0", ksS[:], [64, 16], BF16, waits=[tok_st_copy])
                    dump("mcol", Mcol[:], [128, 1], F32, waits=[tok_consts])
                tok_masks = [None] * 4
                for grp in range(4):
                    sbank = pmm[grp % 2]
                    for ii in range(4):
                        h = grp * 4 + ii
                        p.op("t", lambda e, h=h, ii=ii, c=c, sbank=sbank: e.matmul(
                            out=sbank[:, ii * 128:(ii + 1) * 128],
                            lhsT=KT[0:64, h * 256 + c * 128: h * 256 + (c + 1) * 128],
                            rhs=QT[0:64, h * 256 + c * 128: h * 256 + (c + 1) * 128],
                            start=True, stop=True),
                            waits=([tok_qt, tok_kt] +
                                   ([pmm_free[grp % 2]] if pmm_free[grp % 2] else [])) if ii == 0 else [],
                            inc=("ST" if ii == 3 else None))
                    tok_sc = p.tok("ST")
                    p.op("v", lambda e, grp=grp, sbank=sbank: e.tensor_tensor(
                        out=Ssb[:, grp * 512:(grp + 1) * 512], in0=sbank[:], in1=McT[:],
                        op=ALU.mult), waits=[tok_sc, tok_consts], inc="SV")
                    tok_masks[grp] = p.tok("SV")
                    pmm_free[grp % 2] = tok_masks[grp]
                for grp in range(4):
                    nbank = pmm[2 + grp % 2]
                    tok_mask = tok_masks[grp]
                    for ii in range(4):
                        h = grp * 4 + ii
                        p.op("t", lambda e, h=h, ii=ii, c=c, nbank=nbank: e.matmul(
                            out=nbank[:, ii * 128:ii * 128 + 65],
                            lhsT=Ssb[:, h * 128:(h + 1) * 128],
                            rhs=Vc[:, c * 1040 + h * 65: c * 1040 + (h + 1) * 65],
                            start=True, stop=False),
                            waits=([tok_mask, tok_v, tok_st_copy, tok_memset] +
                                   ([pmm_free[2 + grp % 2]] if pmm_free[2 + grp % 2] else [])) if ii == 0 else [])
                        p.op("t", lambda e, h=h, ii=ii, c=c, nbank=nbank: e.matmul(
                            out=nbank[:, ii * 128:ii * 128 + 64],
                            lhsT=QT[0:64, h * 256 + c * 128: h * 256 + (c + 1) * 128],
                            rhs=stS[0:64, h * 64:(h + 1) * 64], start=False, stop=False))
                        p.op("t", lambda e, h=h, ii=ii, c=c, nbank=nbank: e.matmul(
                            out=nbank[:, ii * 128 + 64:ii * 128 + 65],
                            lhsT=QT[0:64, h * 256 + c * 128: h * 256 + (c + 1) * 128],
                            rhs=ksS[0:64, h:h + 1], start=False, stop=True),
                            inc=("ST" if ii == 3 else None))
                    tok_n = p.tok("ST")
                    dview = nbank[:].rearrange("p (hh s) -> p hh s", hh=4)[:, :, 64:65]
                    p.op("v", lambda e, grp=grp, dv=dview: e.tensor_scalar(
                        out=dens[:, grp * 4:(grp + 1) * 4], in0=dv,
                        scalar1=1e-6, scalar2=None, op0=ALU.add),
                        waits=[tok_n, p.tok("SV")], inc="SV")
                    p.op("v", lambda e, grp=grp: e.reciprocal(
                        out=recips[:, grp * 4:(grp + 1) * 4],
                        in_=dens[:, grp * 4:(grp + 1) * 4]),
                        waits=[p.tok("SV")], inc="SV")
                    for ii in range(4):
                        h = grp * 4 + ii
                        p.op("v", lambda e, h=h, ii=ii, c=c, nbank=nbank: e.tensor_scalar(
                            out=attn_tm[:, c * 1024 + h * 64: c * 1024 + (h + 1) * 64],
                            in0=nbank[:, ii * 128:ii * 128 + 64],
                            scalar1=recips[:, h:h + 1], scalar2=None, op0=ALU.mult),
                            waits=([p.tok("SV")] if ii == 0 else []),
                            inc=("SV" if ii == 3 else None))
                    pmm_free[2 + grp % 2] = p.tok("SV")
                tok_div = p.tok("SV")
                emit_state_chunk(Kc[:], c, tok_kc, tok_v)
                for j in range(8):
                    p.op("t", lambda e, c=c, j=j: e.transpose(
                        out=pbf[:, j * 128:(j + 1) * 128],
                        in_=attn_tm[:, c * 1024 + j * 128: c * 1024 + (j + 1) * 128],
                        identity=Id[:]),
                        waits=([tok_div] + ([pbf_free[0]] if pbf_free[0] else [])) if j == 0 else [],
                        inc=("ST" if j == 7 else None))
                tok_at = p.tok("ST")
                p.op("v", lambda e, c=c: e.tensor_copy(
                    out=attnT[:, c * 1024:(c + 1) * 1024], in_=pbf[:]),
                    waits=[tok_at], inc="SV")
                pbf_free[0] = p.tok("SV")
            tok_attnT = p.tok("SV")

            if tile == 0:
                dump("attn_tm", attn_tm[:], [128, 2048], BF16, waits=[tok_attnT])

            so_ = load_slot("o", 0)
            for c in range(NCHUNK):
                for n in range(2):
                    bi = (c * 2 + n) % 2
                    bank = pmm[bi]
                    for k in range(8):
                        p.op("t", lambda e, c=c, n=n, k=k, i=so_, bank=bank: e.matmul(
                            out=bank[:],
                            lhsT=attnT[:, c * 1024 + k * 128: c * 1024 + (k + 1) * 128],
                            rhs=wb[i][:, k * 1024 + n * 512: k * 1024 + (n + 1) * 512],
                            start=(k == 0), stop=(k == 7)),
                            waits=([tok_attnT, wb_ready[so_]] +
                                   ([pmm_free[bi]] if pmm_free[bi] else [])) if k == 0 else [],
                            inc=("ST" if k == 7 else None))
                    tok_mm = p.tok("ST")
                    p.op("v", lambda e, c=c, n=n, bank=bank, xa=xa: e.tensor_tensor(
                        out=xr0[:, c * 1024 + n * 512: c * 1024 + (n + 1) * 512],
                        in0=bank[:], in1=xa[:, c * 1024 + n * 512: c * 1024 + (n + 1) * 512],
                        op=ALU.add), waits=[tok_mm, tok_here], inc="SV")
                    pmm_free[bi] = p.tok("SV")
            tok_x2 = p.tok("SV")
            xin_used[i] = tok_x2
            done_slot(so_, tok_x2)

            if tile == 0:
                dump("x2", xr0[:], [128, 2048], F32, waits=[tok_x2])

            tok_rstd2 = ln_stats(xr0[:], "ln2", [tok_x2])
            tok_h2 = ln_apply(xr0[:], "ln2", hf[:], tok_rstd2, waits=[tok_x2])
            if tile == 0:
                dump("h2f", hf[:], [128, 2048], F32, waits=[tok_h2])
            tok_h2T = transpose_2k(hf[:], tok_h2, hT[:])
            if tile == 0:
                dump("h2T", hT[:], [128, 2048], BF16, waits=[tok_h2T])

            p.table(G_SET)
            for qf in range(4):
                sa = load_slot("aw1", qf)
                for m in range(8):
                    fc = qf * 8 + m
                    bi = m % 2
                    bank = pmm[bi]
                    p.op("t", lambda e, fc=fc, bank=bank: e.matmul(
                        out=bank[:, 0:256], lhsT=baw1[0:1, fc * 128:(fc + 1) * 128],
                        rhs=Onr[0:1, 0:256], start=True, stop=False),
                        waits=[wb_ready[sa], tok_h2T, tok_bias["aw1"], tok_consts] +
                              ([pmm_free[bi]] if pmm_free[bi] else []))
                    for k in range(8):
                        p.op("t", lambda e, m=m, k=k, i=sa, bank=bank: e.matmul(
                            out=bank[:, 0:256],
                            lhsT=wb[i][:, k * 1024 + m * 128: k * 1024 + (m + 1) * 128],
                            rhs=hT[:, k * 256:(k + 1) * 256], start=False, stop=(k == 7)),
                            inc=("ST" if k == 7 else None))
                    tok_mm = p.tok("ST")
                    p.op("s", lambda e, fc=fc, bank=bank: e.activation(
                        out=garena[:, fc * 256:(fc + 1) * 256], in_=bank[:, 0:256],
                        func=AF.Gelu), waits=[tok_mm], inc="SS")
                    pmm_free[bi] = p.tok("SS")
                done_slot(sa, p.tok("SS"))
            tok_g = p.tok("SS")

            if tile == 0:
                dump("gattn", garena[:], [128, 8192], BF16, waits=[tok_g])

            sa2 = [None] * 4
            for kq in range(4):
                sa2[kq] = load_slot("aw2", kq)
                for cn in range(4):
                    c, n = cn // 2, cn % 2
                    bank = pmm[cn]
                    if kq == 0:
                        p.op("t", lambda e, n=n, bank=bank: e.matmul(
                            out=bank[:], lhsT=Onc[:], rhs=ab2b[0:1, n * 512:(n + 1) * 512],
                            start=True, stop=False),
                            waits=[wb_ready[sa2[0]], tok_g, tok_smallcasts, tok_consts] +
                                  ([pmm_free[cn]] if pmm_free[cn] else []))
                    for k8 in range(8):
                        fc = kq * 8 + k8
                        p.op("t", lambda e, fc=fc, c=c, n=n, k8=k8, i=sa2[kq], bank=bank: e.matmul(
                            out=bank[:],
                            lhsT=garena[:, fc * 256 + c * 128: fc * 256 + (c + 1) * 128],
                            rhs=wb[i][:, k8 * 1024 + n * 512: k8 * 1024 + (n + 1) * 512],
                            start=False, stop=(fc == 31)),
                            waits=([wb_ready[sa2[kq]]] if (k8 == 0 and cn == 0 and kq > 0) else []),
                            inc=("ST" if (cn == 3 and k8 == 7) else None))
                tok_kq = p.tok("ST")
                done_slot(sa2[kq], tok_kq)
            tok_mm2_all = p.tok("ST")
            for cn in range(4):
                c, n = cn // 2, cn % 2
                bank = pmm[cn]
                p.op("v", lambda e, c=c, n=n, bank=bank: e.tensor_tensor(
                    out=e1[:, 0:512], in0=bank[:],
                    in1=xr0[:, c * 1024 + n * 512: c * 1024 + (n + 1) * 512],
                    op=ALU.add), waits=[tok_mm2_all, p.tok("SV")], inc="SV")
                p.op("v", lambda e, c=c, n=n, xa=xa: e.tensor_tensor(
                    out=xr1[:, c * 1024 + n * 512: c * 1024 + (n + 1) * 512],
                    in0=e1[:, 0:512],
                    in1=xa[:, c * 1024 + n * 512: c * 1024 + (n + 1) * 512],
                    op=ALU.add), waits=[p.tok("SV")], inc="SV")
                pmm_free[cn] = p.tok("SV")
            tok_x3 = p.tok("SV")
            xin_used[i] = tok_x3

            tok_rstd3 = ln_stats(xr1[:], "oln", [tok_x3])
            tok_y1 = ln_apply(xr1[:], "oln", xr0[:], tok_rstd3, waits=[tok_x3])
            for c in range(NCHUNK):
                p.op("v", lambda e, c=c: e.reduce_max(
                    out=stats[:, 32 + c:33 + c], in_=xr0[:, c * 1024:(c + 1) * 1024],
                    axis=AX.X, apply_absolute_value=True), waits=[tok_y1], inc="SV")
            tok_maxy = p.tok("SV")

            if tile == 0:
                dump("y1", xr0[:], [128, 2048], F32, waits=[tok_y1])

            b3 = SC["oln"]
            p.op("v", lambda e, b3=b3: e.tensor_tensor(
                out=sst[:, 4:6], in0=stats[:, b3 + 6:b3 + 8], in1=stats[:, b3 + 6:b3 + 8],
                op=ALU.mult), waits=[tok_y1, p.tok("SV")], inc="SV")
            p.op("v", lambda e: e.tensor_scalar(
                out=stats[:, 36:38], in0=sst[:, 4:6], scalar1=-1e-5, scalar2=1.0 + 1e-5,
                op0=ALU.mult, op1=ALU.add), waits=[p.tok("SV")], inc="SV")
            tok_msq = p.tok("SV")
            p.table(E_SET)
            p.op("s", lambda e: e.activation(out=sst[:, 6:8], in_=stats[:, 36:38],
                                             func=AF.Ln),
                 waits=[tok_msq] + ([site_guard["q1"]] if site_guard["q1"] else []),
                 inc="SS")
            p.op("s", lambda e: e.activation(out=stats[:, 38:40], in_=sst[:, 6:8],
                                             func=AF.Exp, scale=-0.5),
                 waits=[p.tok("SS")], inc="SS")
            tok_rms2 = p.tok("SS")
            p.op("v", lambda e: e.tensor_tensor(out=sst[:, 8:10], in0=stats[:, 38:40],
                                                in1=stats[:, 32:34], op=ALU.mult),
                 waits=[tok_rms2, tok_maxy, p.tok("SV")], inc="SV")
            p.op("v", lambda e: e.tensor_scalar(out=sst[:, 30:32], in0=sst[:, 8:10],
                                                scalar1=1e-5, scalar2=None, op0=ALU.max),
                 waits=[p.tok("SV")], inc="SV")
            p.op("v", lambda e: e.reciprocal(out=sst[:, 10:12], in_=sst[:, 30:32]),
                 waits=[p.tok("SV")], inc="SV")
            p.op("v", lambda e: e.tensor_scalar(out=sst[:, 26:28], in0=sst[:, 10:12],
                                                scalar1=127.0, scalar2=None, op0=ALU.mult),
                 waits=[p.tok("SV")], inc="SV")
            p.op("v", lambda e: e.tensor_tensor(out=sst[:, 12:14], in0=stats[:, 38:40],
                                                in1=sst[:, 26:28], op=ALU.mult),
                 waits=[p.tok("SV")], inc="SV")
            p.op("v", lambda e: e.tensor_scalar(out=sst[:, 8:10], in0=sst[:, 30:32],
                                                scalar1=1.0 / 127.0, scalar2=None,
                                                op0=ALU.mult), waits=[p.tok("SV")], inc="SV")
            p.op("v", lambda e: e.tensor_scalar(out=sst[:, 14:16], in0=sst[:, 8:10],
                                                scalar1=swc[:, 0:1], scalar2=None,
                                                op0=ALU.mult), waits=[p.tok("SV")], inc="SV")
            tok_qs = p.tok("SV")
            for c in range(NCHUNK):
                p.op("v", lambda e, c=c: e.tensor_scalar(
                    out=i8t[:, c * 1024:(c + 1) * 1024], in0=xr0[:, c * 1024:(c + 1) * 1024],
                    scalar1=sst[:, 12 + c:13 + c], scalar2=None, op0=ALU.mult),
                    waits=[tok_qs, p.tok("SV")], inc="SV")
                p.op("v", lambda e, c=c: e.tensor_scalar(
                    out=xqb[:, c * 1024:(c + 1) * 1024], in0=i8t[:, c * 1024:(c + 1) * 1024],
                    scalar1=sst[:, 14 + c:15 + c], scalar2=None, op0=ALU.mult),
                    waits=[p.tok("SV")], inc="SV")
            tok_xq = p.tok("SV")
            if tile == 0:
                dump("xqb", xqb[:], [128, 2048], BF16, waits=[tok_xq])
            for c in range(NCHUNK):
                for j in range(8):
                    p.op("t", lambda e, c=c, j=j: e.transpose(
                        out=pbf[:, j * 128:(j + 1) * 128],
                        in_=xqb[:, c * 1024 + j * 128: c * 1024 + (j + 1) * 128],
                        identity=Id[:]),
                        waits=([tok_xq] + ([pbf_free[0]] if pbf_free[0] else [])) if j == 0 else [],
                        inc=("ST" if j == 7 else None))
                tok_tt = p.tok("ST")
                hview = hT[:].rearrange("p (j cc f) -> p j cc f", j=8, cc=2)[:, :, c, :]
                pview = pbf[:].rearrange("p (j f) -> p j f", j=8)
                p.op("v", lambda e, hv=hview, pv=pview: e.tensor_copy(out=hv, in_=pv),
                     waits=[tok_tt], inc="SV")
                pbf_free[0] = p.tok("SV")
            tok_xqT = p.tok("SV")

            p.table(G_SET)
            for qf in range(4):
                sf = load_slot("f1", qf)
                for m in range(8):
                    fc = qf * 8 + m
                    bi = m % 2
                    bank = pmm[bi]
                    p.op("t", lambda e, fc=fc, bank=bank: e.matmul(
                        out=bank[:, 0:256], lhsT=fb1b[0:1, fc * 128:(fc + 1) * 128],
                        rhs=Onr[0:1, 0:256], start=True, stop=False),
                        waits=[wb_ready[sf], tok_xqT, tok_smallcasts, tok_consts] +
                              ([pmm_free[bi]] if pmm_free[bi] else []))
                    for k in range(8):
                        p.op("t", lambda e, m=m, k=k, i=sf, bank=bank: e.matmul(
                            out=bank[:, 0:256],
                            lhsT=wb[i][:, k * 1024 + m * 128: k * 1024 + (m + 1) * 128],
                            rhs=hT[:, k * 256:(k + 1) * 256], start=False, stop=(k == 7)),
                            inc=("ST" if k == 7 else None))
                    tok_mm = p.tok("ST")
                    p.op("s", lambda e, fc=fc, bank=bank: e.activation(
                        out=garena[:, fc * 256:(fc + 1) * 256], in_=bank[:, 0:256],
                        func=AF.Gelu), waits=[tok_mm], inc="SS")
                    pmm_free[bi] = p.tok("SS")
                done_slot(sf, p.tok("SS"))
            tok_g2 = p.tok("SS")
            if tile == 0:
                dump("g2", garena[:], [128, 8192], BF16, waits=[tok_g2])

            F["tok_g2"] = tok_g2
            F["tok_qs"] = tok_qs
            F["tok_h2"] = tok_h2
            site_guard["ln2"] = tok_h2
            site_guard["oln"] = tok_qs
            site_guard["q1"] = tok_qs

        def emit_back1q(tile):
            F = front_toks[tile]
            tok_g2 = F["tok_g2"]
            for fc in range(32):
                p.op("s", lambda e, fc=fc: e.activation(
                    out=ints2[:, fc * 256:(fc + 1) * 256],
                    in_=garena[:, fc * 256:(fc + 1) * 256], func=AF.Square),
                    waits=(([tok_g2] + ([tok_bl2_prev["tok"]] if tok_bl2_prev["tok"] else []))
                           if fc == 0 else []),
                    inc=("SS" if fc in (0, 31) else None))
            tok_sqall = p.tok("SS")
            for fc in range(32):
                p.op("t", lambda e, fc=fc: e.matmul(
                    out=pmm[3][0:1, 0:256], lhsT=Ontk[:], rhs=ints2[:, fc * 256:(fc + 1) * 256],
                    start=(fc == 0), stop=(fc == 31)),
                    waits=(([tok_sqall, tok_consts] +
                           ([pmm_free[3]] if pmm_free[3] else [])) if fc == 0 else []),
                    inc=("ST" if fc == 31 else None))
                if fc == 0:
                    p.op("g", lambda e: e.tensor_copy(out=gmb[0][:], in_=ints2[:, 0:256]),
                         waits=[tok_sqall], inc="SP")
                else:
                    p.op("v", lambda e, fc=fc: e.tensor_tensor(
                        out=gmb[fc % 2][:], in0=gmb[(fc + 1) % 2][:],
                        in1=ints2[:, fc * 256:(fc + 1) * 256],
                        op=ALU.max), waits=[tok_sqall, p.tok("SP"), p.tok("SV")], inc="SV")
            tok_ssq = p.tok("ST")
            gm_fin = gmb[31 % 2]
            tok_gm = [p.tok("SV"), p.tok("SP")]
            for c in range(NCHUNK):
                p.op("t", lambda e, c=c, gm_fin=gm_fin: e.transpose(
                    out=pbf[:, c * 128:(c + 1) * 128], in_=gm_fin[:, c * 128:(c + 1) * 128],
                    identity=Id[:]),
                    waits=(tok_gm + [tok_consts] + ([pbf_free[0]] if pbf_free[0] else [])) if c == 0 else [],
                    inc=("ST" if c == 1 else None))
            tok_gt = p.tok("ST")
            for c in range(NCHUNK):
                p.op("v", lambda e, c=c: e.reduce_max(
                    out=stats[:, 52 + c:53 + c], in_=pbf[:, c * 128:(c + 1) * 128],
                    axis=AX.X), waits=[tok_gt], inc="SV")
            pbf_free[0] = p.tok("SV")
            tok_gmaxsq = p.tok("SV")
            p.table(E_SET)
            p.op("s", lambda e: e.activation(out=sst[:, 26:28], in_=stats[:, 52:54],
                                             func=AF.Ln),
                 waits=[tok_gmaxsq] + ([site_guard["q2"]] if site_guard["q2"] else []),
                 inc="SS")
            p.op("s", lambda e: e.activation(out=stats[:, 34:36], in_=sst[:, 26:28],
                                             func=AF.Exp, scale=0.5),
                 waits=[p.tok("SS")], inc="SS")
            tok_gmax = p.tok("SS")
            p.op("v", lambda e: e.tensor_copy(out=rowf[0:1, 0:256], in_=pmm[3][0:1, 0:256]),
                 waits=[tok_ssq], inc="SV")
            pmm_free[3] = p.tok("SV")
            tok_row = p.tok("SV")
            for c in range(NCHUNK):
                p.op("t", lambda e, c=c: e.matmul(
                    out=pmm[3][:, c:c + 1].bitcast(F32), lhsT=rowf[0:1, c * 128:(c + 1) * 128],
                    rhs=Oncf[0:1, 0:1], start=True, stop=True),
                    waits=([tok_row, tok_consts] + ([pmm_free[3]] if pmm_free[3] else [])) if c == 0 else [],
                    inc=("ST" if c == 1 else None))
            tok_ssqT = p.tok("ST")
            p.op("v", lambda e: e.tensor_copy(out=stats[:, 44:46], in_=pmm[3][:, 0:2]),
                 waits=[tok_ssqT], inc="SV")
            pmm_free[3] = p.tok("SV")
            p.op("v", lambda e: e.tensor_scalar(out=stats[:, 42:44], in0=stats[:, 44:46],
                                                scalar1=1.0 / FF, scalar2=1e-5,
                                                op0=ALU.mult, op1=ALU.add),
                 waits=[p.tok("SV")], inc="SV")
            tok_ms2 = p.tok("SV")
            p.table(E_SET)
            p.op("s", lambda e: e.activation(out=sst[:, 16:18], in_=stats[:, 42:44],
                                             func=AF.Ln),
                 waits=[tok_ms2] + ([site_guard["q2"]] if site_guard["q2"] else []),
                 inc="SS")
            p.op("s", lambda e: e.activation(out=stats[:, 46:48], in_=sst[:, 16:18],
                                             func=AF.Exp, scale=-0.5),
                 waits=[p.tok("SS")], inc="SS")
            tok_rmsg = p.tok("SS")
            p.op("v", lambda e: e.tensor_tensor(out=sst[:, 18:20], in0=stats[:, 46:48],
                                                in1=stats[:, 34:36], op=ALU.mult),
                 waits=[tok_rmsg, tok_gmax, p.tok("SV")], inc="SV")
            p.op("v", lambda e: e.tensor_scalar(out=sst[:, 30:32], in0=sst[:, 18:20],
                                                scalar1=1e-5, scalar2=None, op0=ALU.max),
                 waits=[p.tok("SV")], inc="SV")
            p.op("v", lambda e: e.reciprocal(out=sst[:, 20:22], in_=sst[:, 30:32]),
                 waits=[p.tok("SV")], inc="SV")
            p.op("v", lambda e: e.tensor_scalar(out=sst[:, 16:18], in0=sst[:, 20:22],
                                                scalar1=127.0, scalar2=None, op0=ALU.mult),
                 waits=[p.tok("SV")], inc="SV")
            p.op("v", lambda e: e.tensor_tensor(out=sst[:, 22:24], in0=stats[:, 46:48],
                                                in1=sst[:, 16:18], op=ALU.mult),
                 waits=[p.tok("SV")], inc="SV")
            p.op("v", lambda e: e.tensor_scalar(out=sst[:, 18:20], in0=sst[:, 30:32],
                                                scalar1=1.0 / 127.0, scalar2=None,
                                                op0=ALU.mult), waits=[p.tok("SV")], inc="SV")
            p.op("v", lambda e: e.tensor_scalar(out=sst[:, 24:26], in0=sst[:, 18:20],
                                                scalar1=swc[:, 1:2], scalar2=None,
                                                op0=ALU.mult), waits=[p.tok("SV")], inc="SV")
            tok_qs2 = p.tok("SV")
            for c in range(NCHUNK):
                p.op("t", lambda e, c=c: e.transpose(
                    out=pmm[3][0:1, c * 128:(c + 1) * 128].bitcast(F32),
                    in_=sst[:, 22 + c:23 + c], identity=Idf[:]),
                    waits=([tok_qs2, tok_consts] + ([pmm_free[3]] if pmm_free[3] else [])) if c == 0 else [],
                    inc=("ST" if c == 1 else None))
            tok_mrT = p.tok("ST")
            p.op("v", lambda e: e.tensor_copy(out=mrow[:], in_=pmm[3][0:1, 0:256]),
                 waits=[tok_mrT], inc="SV")
            pmm_free[3] = p.tok("SV")
            tok_mrow = p.tok("SV")
            p.op("t", lambda e: e.matmul(out=pmm[3][:, 0:256], lhsT=Onc[:], rhs=mrow[:],
                                         start=True, stop=True),
                 waits=[tok_mrow] + ([pmm_free[3]] if pmm_free[3] else []), inc="ST")
            tok_B = p.tok("ST")
            p.op("v", lambda e: e.tensor_copy(out=Bsb[:], in_=pmm[3][:, 0:256]),
                 waits=[tok_B], inc="SV")
            pmm_free[3] = p.tok("SV")
            tok_Bsb = p.tok("SV")
            for fc in range(32):
                ib = i8g[fc % 2]
                p.op("v", lambda e, fc=fc, ib=ib: e.tensor_tensor(
                    out=ib[:], in0=garena[:, fc * 256:(fc + 1) * 256], in1=Bsb[:],
                    op=ALU.mult),
                    waits=[tok_Bsb] + ([p.tok("SS")] if fc >= 2 else []), inc="SV")
                tok_i8 = p.tok("SV")
                p.op("s", lambda e, fc=fc, ib=ib: e.copy(
                    out=ints2[:, fc * 256:(fc + 1) * 256], in_=ib[:]),
                    waits=[tok_i8], inc="SS")
            tok_ints2 = p.tok("SS")
            if tile == 0:
                dump("ints2", ints2[:], [128, 8192], BF16, waits=[tok_ints2])

            F["tok_ints2"] = tok_ints2
            F["tok_qs2"] = tok_qs2
            site_guard["q2"] = tok_qs2

        def emit_back2(tile):
            nonlocal tok_y_dma
            F = front_toks[tile]
            tok_ints2 = F["tok_ints2"]
            tok_qs2 = F["tok_qs2"]
            sf2 = [None] * 4
            for kq in range(4):
                sf2[kq] = load_slot("f2", kq)
                for cn in range(4):
                    c, n = cn // 2, cn % 2
                    bank = pmm[cn]
                    for k8 in range(8):
                        fc = kq * 8 + k8
                        w = []
                        if k8 == 0 and cn == 0:
                            w = [wb_ready[sf2[kq]], tok_ints2]
                        if kq == 0 and k8 == 0:
                            w = [wb_ready[sf2[0]], tok_ints2] + \
                                ([pmm_free[cn]] if pmm_free[cn] else [])
                        p.op("t", lambda e, fc=fc, c=c, n=n, k8=k8, i=sf2[kq], bank=bank: e.matmul(
                            out=bank[:],
                            lhsT=ints2[:, fc * 256 + c * 128: fc * 256 + (c + 1) * 128],
                            rhs=wb[i][:, k8 * 1024 + n * 512: k8 * 1024 + (n + 1) * 512],
                            start=(fc == 0), stop=(fc == 31)), waits=w,
                            inc=("ST" if (cn == 3 and k8 == 7) else None))
                tok_kq = p.tok("ST")
                done_slot(sf2[kq], tok_kq)
            tok_bl2 = p.tok("ST")
            tok_bl2_prev["tok"] = tok_bl2
            for cn in range(4):
                c, n = cn // 2, cn % 2
                bank = pmm[cn]
                p.op("v", lambda e, c=c, n=n, bank=bank: e.scalar_tensor_tensor(
                    out=ybuf[:, c * 1024 + n * 512: c * 1024 + (n + 1) * 512],
                    in0=bank[:], scalar=sst[:, 24 + c:25 + c],
                    in1=fb2tile[:, n * 512:(n + 1) * 512], op0=ALU.mult, op1=ALU.add),
                    waits=[tok_bl2, tok_qs2] + ([tok_y_dma] if tok_y_dma else []), inc="SV")
                p.op("v", lambda e, c=c, n=n: e.tensor_tensor(
                    out=xr1[:, c * 1024 + n * 512: c * 1024 + (n + 1) * 512],
                    in0=ybuf[:, c * 1024 + n * 512: c * 1024 + (n + 1) * 512],
                    in1=xr0[:, c * 1024 + n * 512: c * 1024 + (n + 1) * 512],
                    op=ALU.add), inc="SV")
                pmm_free[cn] = p.tok("SV")
            tok_z = p.tok("SV")

            tok_rstd4 = ln_stats(xr1[:], "fln", [tok_z])
            tok_y2 = ln_apply(xr1[:], "fln", ybuf[:], tok_rstd4, waits=[tok_z])
            site_guard["fln"] = tok_y2
            tok_y_dma = p.op("g", lambda e, t=tile: e.dma_start(
                out=y_out[t * TPT:(t + 1) * TPT].rearrange("(c p) dd -> p c dd", p=128),
                in_=ybuf[:].rearrange("p (c dd) -> p c dd", dd=1024)),
                waits=[tok_y2], inc="SO")


        emit_front(0, None)
        emit_front_b(0)
        for tile in range(nm):
            emit_back1(tile)
            if tile + 1 < nm:
                emit_front(tile + 1, None)
            emit_back1q(tile)
            if tile + 1 < nm:
                emit_front_b(tile + 1)
            emit_back2(tile)
            front_toks.pop(tile)
        p.wait("g", p.tok("SO"))
        p.wait("d", p.tok("SW"))

        with nc.Block() as block:
            @block.sync
            def _(sync):
                for f in p.q["d"]:
                    f(sync)

            @block.tensor
            def _(t):
                for f in p.q["t"]:
                    f(t)

            @block.vector
            def _(v):
                for f in p.q["v"]:
                    f(v)

            @block.scalar
            def _(s):
                for f in p.q["s"]:
                    f(s)

            @block.gpsimd
            def _(g):
                for f in p.q["g"]:
                    f(g)

    return nc, dbg_outs


def make_inputs(x_full, weights, core):
    import ml_dtypes
    b, half = core // 2, core % 2
    d = {}
    d["x_main"] = np.ascontiguousarray(x_full[b, half * 4096:(half + 1) * 4096])
    d["x_pref"] = np.ascontiguousarray(x_full[b, 0:4096])
    d["maskcol"] = np.full((128, 1), float(half), np.float32)
    d["wq"] = weights["L0_attn_wq"]; d["wk"] = weights["L0_attn_wk"]
    d["wv"] = weights["L0_attn_wv"]; d["wo"] = weights["L0_attn_wo"]
    d["proj"] = weights["L0_attn_proj"]
    d["aw1"] = weights["L0_attn_w1"]; d["aw2"] = weights["L0_attn_w2"]
    d["fw1t"] = np.ascontiguousarray(np.asarray(weights["L0_ffn_w1"]).T)
    d["fw2t"] = np.ascontiguousarray(np.asarray(weights["L0_ffn_w2"]).T)
    d["g1c"] = np.asarray(weights["L0_attn_g1"]).reshape(8, 128).T
    d["g2c"] = np.asarray(weights["L0_attn_g2"]).reshape(8, 128).T
    d["b1c"] = np.asarray(weights["L0_attn_beta1"]).reshape(8, 128).T
    d["b2c"] = np.asarray(weights["L0_attn_beta2"]).reshape(8, 128).T
    d["ab1r"] = np.asarray(weights["L0_attn_b1"]).reshape(1, -1)
    d["ab2r"] = np.asarray(weights["L0_attn_b2"]).reshape(1, -1)
    d["fb1r"] = np.asarray(weights["L0_ffn_b1"]).reshape(1, -1)
    d["fb2r"] = np.asarray(weights["L0_ffn_b2"]).reshape(1, -1)
    d["ident_bf"] = np.eye(128, dtype=ml_dtypes.bfloat16)
    d["ident_f32"] = np.eye(128, dtype=np.float32)
    d["onescol_bf"] = np.ones((1, 128), ml_dtypes.bfloat16)
    d["onescol_f32"] = np.ones((1, 128), np.float32)
    d["onesrow_bf"] = np.ones((1, 512), ml_dtypes.bfloat16)
    d["onestok_bf"] = np.ones((128, 1), ml_dtypes.bfloat16)
    mk = np.triu(np.ones((128, 128), np.float32))
    d["maskT4"] = np.ascontiguousarray(np.tile(mk, (1, 4)))
    out = {}
    for k in d:
        a = np.asarray(d[k])
        if a.dtype == np.float64:
            a = a.astype(np.float32)
        out[k] = np.ascontiguousarray(a)
    return out


def kernel(**inputs):
    x = np.asarray(inputs["L0_attn_x"])
    nc, _ = build(nm=16, nprf=16)
    in_maps = [make_inputs(x, inputs, core) for core in range(8)]
    res = run_bass_kernel_spmd(nc, in_maps, core_ids=list(range(8)))
    out = np.zeros((4, 8192, 1024), np.float32)
    for core in range(8):
        b, half = core // 2, core % 2
        out[b, half * 4096:(half + 1) * 4096] = res.results[core]["y_out"]
    return out


# version 08
# speedup vs baseline: 1.1840x; 1.0078x over previous
import numpy as np
import concourse.bass as bass
import concourse.mybir as mybir
from concourse.bass_utils import run_bass_kernel_spmd
from contextlib import ExitStack

F32 = mybir.dt.float32
BF16 = mybir.dt.bfloat16
I8 = mybir.dt.int8
AF = mybir.ActivationFunctionType
ALU = mybir.AluOpType
AX = mybir.AxisListType

D = 1024
H = 16
DH = 64
FF = 4096
C = 128
TPT = 256
NCHUNK = 2

E_SET = 6
G_SET = 10

ENGS = ("d", "t", "v", "s", "g")


class P:
    def __init__(self, nc):
        self.nc = nc
        self.q = {e: [] for e in ENGS}
        self.cnt = {}
        self.sems = {}
        self.lastwait = {e: {} for e in ENGS}
        self.act_set = None

    def new_sem(self, name):
        self.cnt[name] = 0

    def bump(self, sem, n):
        self.cnt[sem] += n
        return (sem, self.cnt[sem])

    def emit(self, eng, fn):
        self.q[eng].append(fn)

    def wait(self, eng, tok):
        if tok is None:
            return
        sem, val = tok
        if self.lastwait[eng].get(sem, 0) >= val:
            return
        self.lastwait[eng][sem] = val

        def f(e, sem=sem, val=val):
            e.wait_ge(self.sems[sem], val)
        self.emit(eng, f)

    def op(self, eng, fn, waits=(), inc=None, n=None):
        for w in waits:
            self.wait(eng, w)
        tok = None
        if inc:
            amt = n if n is not None else (16 if inc in ("SW", "SO") else 1)
            tok = self.bump(inc, amt)

        def g(e, fn=fn, tok=tok):
            i = fn(e)
            if tok is not None:
                amt = 16 if tok[0] in ("SW", "SO") else 1
                i.then_inc(self.sems[tok[0]], amt)
        self.emit(eng, g)
        return tok

    def tok(self, sem):
        return (sem, self.cnt[sem])

    def table(self, set_id):
        if self.act_set == set_id:
            return
        self.act_set = set_id

        def f(e, set_id=set_id):
            e.add_instruction(mybir.InstLoadActFuncSet(
                name=self.nc.get_next_instruction_name(),
                act_func_set_id=set_id, ins=[], outs=[]))
        self.emit("s", f)


def build(nm=16, nprf=16, dbg=None):
    dbg = dbg or set()
    nc = bass.Bass("TRN2", target_bir_lowering=False)

    di = lambda name, shape, dt=F32: nc.dram_tensor(name, shape, dt, kind="ExternalInput").ap()
    do = lambda name, shape, dt=F32: nc.dram_tensor(name, shape, dt, kind="ExternalOutput").ap()
    dint = lambda name, shape, dt=BF16: nc.dram_tensor(name, shape, dt, kind="Internal").ap()

    x_main = di("x_main", [4096, D])
    x_pref = di("x_pref", [4096, D])
    maskcol = di("maskcol", [128, 1])
    wq = di("wq", [D, D]); wk = di("wk", [D, D]); wv = di("wv", [D, D]); wo = di("wo", [D, D])
    proj = di("proj", [DH, DH])
    aw1 = di("aw1", [D, FF]); aw2 = di("aw2", [FF, D])
    fw1t = di("fw1t", [D, FF]); fw2t = di("fw2t", [FF, D])
    g1c = di("g1c", [128, 8]); g2c = di("g2c", [128, 8])
    b1c = di("b1c", [128, 8]); b2c = di("b2c", [128, 8])
    ab1r = di("ab1r", [1, FF]); ab2r = di("ab2r", [1, D])
    fb1r = di("fb1r", [1, FF]); fb2r = di("fb2r", [1, D])
    ident_bf = di("ident_bf", [128, 128], BF16)
    ident_f32 = di("ident_f32", [128, 128], F32)
    onescol_bf = di("onescol_bf", [1, 128], BF16)
    onescol_f32 = di("onescol_f32", [1, 128], F32)
    onesrow_bf = di("onesrow_bf", [1, 512], BF16)
    onestok_bf = di("onestok_bf", [128, 1], BF16)
    maskT4 = di("maskT4", [128, 512])

    y_out = do("y_out", [4096, D])

    scr = {}
    for name, nq in [("q", 1), ("k", 1), ("v", 1), ("o", 1),
                     ("aw1", 4), ("aw2", 4), ("f1", 4), ("f2", 4)]:
        scr[name] = dint("scr_" + name, [nq, 8, 128, 1024], BF16)

    dbg_outs = {}

    with ExitStack() as st_:
        E = st_.enter_context
        wb = [E(nc.sbuf_tensor(f"wb{i}", [128, 8192], BF16)) for i in range(2)]
        xin = [E(nc.sbuf_tensor(f"xin{i}", [128, 2048], F32)) for i in range(2)]
        xr0 = E(nc.sbuf_tensor("xr0", [128, 2048], F32))
        xr1 = E(nc.sbuf_tensor("xr1", [128, 2048], F32))
        ybuf = E(nc.sbuf_tensor("ybuf", [128, 2048], F32))
        hf = E(nc.sbuf_tensor("hf", [128, 2048], F32))
        hT = E(nc.sbuf_tensor("hT", [128, 2048], BF16))
        QRT = E(nc.sbuf_tensor("QRT", [64, 4096], BF16))
        KRT = E(nc.sbuf_tensor("KRT", [64, 4096], BF16))
        QT = QRT
        KT = KRT
        Kc = E(nc.sbuf_tensor("Kc", [128, 2048], BF16))
        Vc = E(nc.sbuf_tensor("Vc", [128, 2080], BF16))
        Ssb = E(nc.sbuf_tensor("Ssb", [128, 2048], BF16))
        attnT = E(nc.sbuf_tensor("attnT", [128, 2048], BF16))
        attn_tm = E(nc.sbuf_tensor("attn_tm", [128, 2048], BF16))
        stS = E(nc.sbuf_tensor("stS", [64, 1024], BF16))
        ksS = E(nc.sbuf_tensor("ksS", [64, 16], BF16))
        garena = E(nc.sbuf_tensor("garena", [128, 8192], BF16))
        ints2 = E(nc.sbuf_tensor("ints2", [128, 8192], BF16))
        xqb = attn_tm
        i8g = [E(nc.sbuf_tensor(f"i8g{i}", [128, 256], I8)) for i in range(2)]
        qk2 = E(nc.sbuf_tensor("qk2", [128, 2048], F32))
        gmb = [E(nc.sbuf_tensor(f"gmb{i}", [128, 256], BF16)) for i in range(2)]
        e1 = E(nc.sbuf_tensor("e1", [128, 512], F32))
        sst = E(nc.sbuf_tensor("sst", [128, 32], F32))
        sqd = E(nc.sbuf_tensor("sqd", [128, 1024], BF16))
        stats = E(nc.sbuf_tensor("stats", [128, 64], F32))
        dens = E(nc.sbuf_tensor("dens", [128, 16], F32))
        recips = E(nc.sbuf_tensor("recips", [128, 16], F32))
        mrow = E(nc.sbuf_tensor("mrow", [1, 256], BF16))
        rowf = E(nc.sbuf_tensor("rowf", [1, 256], F32))
        Bsb = E(nc.sbuf_tensor("Bsb", [128, 256], BF16))
        Id = E(nc.sbuf_tensor("Id", [128, 128], BF16))
        Idf = E(nc.sbuf_tensor("Idf", [128, 128], F32))
        McT = E(nc.sbuf_tensor("McT", [128, 512], F32))
        Onc = E(nc.sbuf_tensor("Onc", [1, 128], BF16))
        Oncf = E(nc.sbuf_tensor("Oncf", [1, 128], F32))
        Onr = E(nc.sbuf_tensor("Onr", [1, 512], BF16))
        Ontk = E(nc.sbuf_tensor("Ontk", [128, 1], BF16))
        Mcol = E(nc.sbuf_tensor("Mcol", [128, 1], F32))
        projb = E(nc.sbuf_tensor("projb", [64, 64], BF16))
        projf = E(nc.sbuf_tensor("projf", [64, 64], F32))
        g1b = E(nc.sbuf_tensor("g1b", [128, 8], F32))
        g2b = E(nc.sbuf_tensor("g2b", [128, 8], F32))
        b1f = E(nc.sbuf_tensor("b1f", [128, 8], F32))
        b2f = E(nc.sbuf_tensor("b2f", [128, 8], F32))
        b1b = E(nc.sbuf_tensor("b1b", [128, 8], BF16))
        b2b = E(nc.sbuf_tensor("b2b", [128, 8], BF16))
        bqr = E(nc.sbuf_tensor("bqr", [1, 1024], BF16))
        bkr = E(nc.sbuf_tensor("bkr", [1, 1024], BF16))
        bvr = E(nc.sbuf_tensor("bvr", [1, 1024], BF16))
        baw1 = E(nc.sbuf_tensor("baw1", [1, 4096], BF16))
        ab2b = E(nc.sbuf_tensor("ab2b", [1, 1024], BF16))
        fb1b = E(nc.sbuf_tensor("fb1b", [1, 4096], BF16))
        fb2tile = E(nc.sbuf_tensor("fb2tile", [128, 1024], F32))
        fb2b = E(nc.sbuf_tensor("fb2b", [1, 1024], BF16))
        rtmp = ybuf[0:1, 0:512]
        rtmp2 = ybuf[0:1, 1024:2048]
        swc = E(nc.sbuf_tensor("swc", [128, 4], F32))
        tiny = E(nc.sbuf_tensor("tiny", [1, 8], F32))
        tiny2 = E(nc.sbuf_tensor("tiny2", [128, 1], F32))
        st0 = E(nc.psum_tensor("st0", [64, 512], F32))
        st1 = E(nc.psum_tensor("st1", [64, 512], F32))
        ksb = E(nc.psum_tensor("ksb", [64, 16], F32))
        pmm = [E(nc.psum_tensor(f"pmm{i}", [128, 512], F32)) for i in range(4)]
        pbf = E(nc.psum_tensor("pbf", [128, 1024], BF16))
        sem_names = ["SW", "SO", "ST", "SV", "SS", "SP"]
        sems = {nm_: E(nc.semaphore(nm_)) for nm_ in sem_names}

        p = P(nc)
        p.sems = sems
        for s_ in sem_names:
            p.new_sem(s_)

        gare_f32 = garena[:].bitcast(F32)
        qkraw = qk2[:]
        e2 = gare_f32[:, 0:512]
        e3 = gare_f32[:, 512:1024]
        i8t = e1[:].bitcast(I8)
        ptmp = gare_f32[:, 2048:4096]
        ptmp2 = gare_f32[:, 0:1024]
        i8p = ints2[:].bitcast(I8)

        SC = {"ln1": 0, "ln2": 8, "oln": 16, "fln": 24}

        pmm_free = [None] * 4
        pbf_free = [None]
        wb_free = [None, None]
        wb_ready = [None, None]

        def dma_in(dst, src, queue="d", waits=()):
            return p.op(queue, lambda e, d=dst, s=src: e.dma_start(out=d, in_=s),
                        waits=waits, inc="SW" if queue == "d" else "SO")

        def dump(name, src_ap, shape, dt=F32, waits=()):
            if name not in dbg:
                return
            ap = do("dbg_" + name, shape, dt)
            dbg_outs[name] = ap
            tok = p.op("g", lambda e, a=ap, s=src_ap: e.dma_start(out=a, in_=s),
                       waits=waits, inc="SO")
            for eng in ENGS:
                p.wait(eng, tok)

        for dst, src in [(Id[:], ident_bf), (Idf[:], ident_f32), (Onc[:], onescol_bf),
                         (Oncf[:], onescol_f32), (Onr[:], onesrow_bf), (Ontk[:], onestok_bf),
                         (McT[:], maskT4), (Mcol[:], maskcol),
                         (g1b[:], g1c),
                         (g2b[:], g2c),
                         (projf[:], proj),
                         (b1f[:], b1c),
                         (b2f[:], b2c)]:
            dma_in(dst, src)
        tok_consts = p.tok("SW")

        p.op("v", lambda e: e.tensor_copy(out=projb[:], in_=projf[:]),
             waits=[tok_consts], inc="SV")
        tok_projb = p.tok("SV")
        p.op("v", lambda e: e.tensor_copy(out=b1b[:], in_=b1f[:]), inc="SV")
        p.op("v", lambda e: e.tensor_copy(out=b2b[:], in_=b2f[:]), inc="SV")
        for dst_b, src_r, n512 in [(ab2b, ab2r, 2), (fb1b, fb1r, 8), (fb2b, fb2r, 2)]:
            for i in range(n512):
                tok = dma_in(rtmp, src_r[0:1, i * 512:(i + 1) * 512], waits=[p.tok("SV")])
                p.op("v", lambda e, d=dst_b, i=i: e.tensor_copy(
                    out=d[0:1, i * 512:(i + 1) * 512], in_=rtmp), waits=[tok], inc="SV")
        tok_smallcasts = p.tok("SV")

        p.op("g", lambda e: e.memset(Vc[:], 1.0), inc="SP")
        p.op("g", lambda e: e.memset(stS[:], 0.0), inc="SP")
        p.op("g", lambda e: e.memset(ksS[:], 0.0), inc="SP")
        tok_memset = p.tok("SP")

        p.table(E_SET)

        tok_bias = {}

        stage_guard = [None, None]

        def prep_linear(w_ap, scr_name, gcol, bias_dst, beta_b, extra_row):
            nq = scr[scr_name].shape[0]
            kmajor = scr_name in ("aw2",)
            for q in range(nq):
                cast_tok = [None] * 8
                for k in range(8):
                    if kmajor:
                        rows, cols = (q * 8 + k) * 128, 0
                    else:
                        rows, cols = k * 128, q * 1024
                    hh = k % 2
                    w = [cast_tok[k - 2]] if k >= 2 else                         ([stage_guard[hh]] if stage_guard[hh] else [])
                    pv = ptmp[:, hh * 1024:hh * 1024 + 1024]
                    tok = dma_in(pv, w_ap[rows:rows + 128, cols:cols + 1024], waits=w)
                    if bias_dst is not None:
                        p.op("v", lambda e, k=k, pv=pv: e.tensor_copy(
                            out=wb[1][:, k * 1024:(k + 1) * 1024], in_=pv),
                            waits=[tok] + ([wb_free[1]] if wb_free[1] else []), inc="SV")
                    if gcol is not None:
                        p.op("v", lambda e, k=k, g=gcol, pv=pv: e.tensor_scalar(
                            out=wb[0][:, k * 1024:(k + 1) * 1024], in0=pv,
                            scalar1=g[:, k:k + 1], scalar2=None, op0=ALU.mult),
                            waits=[tok, tok_consts] + ([wb_free[0]] if wb_free[0] else []),
                            inc="SV")
                    else:
                        p.op("v", lambda e, k=k, pv=pv: e.tensor_copy(
                            out=wb[0][:, k * 1024:(k + 1) * 1024], in_=pv),
                            waits=[tok] + ([wb_free[0]] if wb_free[0] else []), inc="SV")
                    cast_tok[k] = p.tok("SV")
                stage_guard[0] = cast_tok[6]
                stage_guard[1] = cast_tok[7]
                tok_stage = p.tok("SV")
                tok_bmm = None
                if bias_dst is not None:
                    for n in range(2):
                        for k in range(8):
                            p.op("t", lambda e, k=k, n=n, bb=beta_b: e.matmul(
                                out=pmm[3][0:1, 0:512],
                                lhsT=bb[:, k:k + 1],
                                rhs=wb[1][:, k * 1024 + n * 512: k * 1024 + (n + 1) * 512],
                                start=(k == 0), stop=(k == 7)),
                                waits=([tok_stage, tok_smallcasts] +
                                       ([pmm_free[3]] if pmm_free[3] else [])) if k == 0 else [],
                                inc=("ST" if k == 7 else None))
                        tok_bmm = p.tok("ST")
                        off = q * 1024 + n * 512
                        if extra_row is None:
                            p.op("v", lambda e, off=off, bd=bias_dst: e.tensor_copy(
                                out=bd[0:1, off:off + 512],
                                in_=pmm[3][0:1, 0:512]), waits=[tok_bmm], inc="SV")
                        else:
                            tok_er = dma_in(rtmp, extra_row[0:1, off:off + 512],
                                            waits=[p.tok("SV")])
                            p.op("v", lambda e, off=off, bd=bias_dst: e.tensor_tensor(
                                out=bd[0:1, off:off + 512],
                                in0=pmm[3][0:1, 0:512],
                                in1=rtmp, op=ALU.add),
                                waits=[tok_bmm, tok_er], inc="SV")
                        pmm_free[3] = p.tok("SV")
                    tok_bias[scr_name] = p.tok("SV")
                    wb_free[1] = tok_bmm
                w_ = [tok_stage] + ([tok_bmm] if tok_bmm else [])
                for k in range(8):
                    wb_free[0] = p.op("d", lambda e, q=q, sn=scr_name, k=k: e.dma_start(
                        out=scr[sn][q, k], in_=wb[0][:, k * 1024:(k + 1) * 1024]),
                        waits=(w_ if k == 0 else []), inc="SW")

        prep_linear(wq, "q", g1b, bqr, b1b, None)
        prep_linear(wk, "k", g1b, bkr, b1b, None)
        prep_linear(wv, "v", g1b, bvr, b1b, None)
        prep_linear(wo, "o", None, None, None, None)
        prep_linear(aw1, "aw1", g2b, baw1, b2b, ab1r)
        prep_linear(aw2, "aw2", None, None, None, None)

        for n in range(2):
            p.op("t", lambda e, n=n: e.matmul(
                out=pmm[3][:, 0:512], lhsT=Onc[:],
                rhs=fb2b[0:1, n * 512:(n + 1) * 512], start=True, stop=True),
                waits=[tok_smallcasts, tok_consts] + ([pmm_free[3]] if pmm_free[3] else []),
                inc="ST")
            tok_fb2 = p.tok("ST")
            p.op("v", lambda e, n=n: e.tensor_copy(
                out=fb2tile[:, n * 512:(n + 1) * 512], in_=pmm[3][:, 0:512]),
                waits=[tok_fb2], inc="SV")
            pmm_free[3] = p.tok("SV")

        def prep_ternary(w_ap, scr_name, sw_col_idx):
            abscol = stats[:, 32:64]
            nr, ncol = w_ap.shape[0] // 128, w_ap.shape[1] // 1024
            ci = 0
            abs_tok = [None] * 34
            for rr in range(nr):
                for cc in range(ncol):
                    buf = ptmp[:, (ci % 2) * 1024:(ci % 2) * 1024 + 1024]
                    w = [abs_tok[ci - 2]] if ci >= 2 else                         ([stage_guard[ci % 2]] if stage_guard[ci % 2] else [])
                    tok = dma_in(buf, w_ap[rr * 128:(rr + 1) * 128, cc * 1024:(cc + 1) * 1024],
                                 waits=w)
                    p.op("s", lambda e, ci=ci, buf=buf: e.activation(
                        out=sqd[:], in_=buf,
                        func=AF.Abs, accum_out=abscol[:, ci:ci + 1]),
                        waits=[tok], inc="SS")
                    abs_tok[ci] = p.tok("SS")
                    ci += 1
            tok_abs = p.tok("SS")
            stage_guard[0] = abs_tok[ci - 2]
            stage_guard[1] = abs_tok[ci - 1]
            p.op("v", lambda e: e.reduce_sum(out=tiny2[:, 0:1], in_=abscol, axis=AX.X),
                 waits=[tok_abs], inc="SV")
            tok_red = p.tok("SV")
            p.op("t", lambda e: e.transpose(out=pmm[3][0:1, 0:128].bitcast(F32),
                                            in_=tiny2[:, 0:1], identity=Idf[:]),
                 waits=([tok_red, tok_consts] + ([pmm_free[3]] if pmm_free[3] else [])),
                 inc="ST")
            tok_tr = p.tok("ST")
            p.op("v", lambda e: e.tensor_copy(out=rowf[0:1, 0:128], in_=pmm[3][0:1, 0:128]),
                 waits=[tok_tr], inc="SV")
            pmm_free[3] = p.tok("SV")
            p.op("v", lambda e: e.reduce_sum(out=tiny[0:1, 0:1], in_=rowf[0:1, 0:128],
                                             axis=AX.X), waits=[p.tok("SV")], inc="SV")
            p.op("v", lambda e: e.tensor_scalar(out=tiny[0:1, 1:2], in0=tiny[0:1, 0:1],
                                                scalar1=1.0 / (4096.0 * 1024.0), scalar2=1e-5,
                                                op0=ALU.mult, op1=ALU.max),
                 waits=[p.tok("SV")], inc="SV")
            p.op("v", lambda e: e.reciprocal(out=tiny[0:1, 2:3], in_=tiny[0:1, 1:2]),
                 waits=[p.tok("SV")], inc="SV")
            tok_sc = p.tok("SV")
            p.op("t", lambda e: e.matmul(out=pmm[3][:, 0:2].bitcast(F32), lhsT=Oncf[:],
                                         rhs=tiny[0:1, 1:3], start=True, stop=True),
                 waits=[tok_sc, pmm_free[3]], inc="ST")
            tok_bc = p.tok("ST")
            p.op("v", lambda e, i=sw_col_idx: e.tensor_copy(out=swc[:, i:i + 1],
                                                            in_=pmm[3][:, 0:1]),
                 waits=[tok_bc], inc="SV")
            p.op("v", lambda e, i=sw_col_idx: e.tensor_copy(out=swc[:, i + 2:i + 3],
                                                            in_=pmm[3][:, 1:2]), inc="SV")
            pmm_free[3] = p.tok("SV")
            tok_sw = p.tok("SV")
            kmajor = scr_name == "f2"
            t2_tok = [None] * 34
            gi = 0
            for q in range(4):
                for k in range(8):
                    if kmajor:
                        rows, cols = (q * 8 + k) * 128, 0
                    else:
                        rows, cols = k * 128, q * 1024
                    hh = gi % 2
                    pv = ptmp[:, hh * 1024:hh * 1024 + 1024]
                    w = [t2_tok[gi - 2]] if gi >= 2 else                         ([stage_guard[hh]] if stage_guard[hh] else [])
                    tok = dma_in(pv, w_ap[rows:rows + 128, cols:cols + 1024], waits=w)
                    p.op("v", lambda e, i=sw_col_idx, pv=pv: e.tensor_scalar(
                        out=i8p[:, 0:1024], in0=pv,
                        scalar1=swc[:, i + 2:i + 3], scalar2=None, op0=ALU.mult),
                        waits=[tok, tok_sw, p.tok("SV")], inc="SV")
                    t2_tok[gi] = p.tok("SV")
                    gi += 1
                    p.op("v", lambda e: e.tensor_copy(out=ptmp2[:], in_=i8p[:, 0:1024]),
                         waits=[p.tok("SV")], inc="SV")
                    p.op("v", lambda e, k=k: e.tensor_scalar(
                        out=wb[0][:, k * 1024:(k + 1) * 1024], in0=ptmp2[:],
                        scalar1=1.0, scalar2=-1.0, op0=ALU.min, op1=ALU.max),
                        waits=([wb_free[0]] if wb_free[0] else []) + [p.tok("SV")], inc="SV")
                tok_stage = p.tok("SV")
                stage_guard[0] = stage_guard[1] = tok_stage
                for k in range(8):
                    wb_free[0] = p.op("d", lambda e, q=q, sn=scr_name, k=k: e.dma_start(
                        out=scr[sn][q, k], in_=wb[0][:, k * 1024:(k + 1) * 1024]),
                        waits=([tok_stage] if k == 0 else []), inc="SW")

        prep_ternary(fw1t, "f1", 0)
        prep_ternary(fw2t, "f2", 1)

        dump("bqr", bqr[:], [1, 1024], BF16, waits=[p.tok("SV"), p.tok("ST")])
        dump("swc", swc[:], [128, 4], F32, waits=[p.tok("SV")])
        dump("scrf1", scr["f1"][0, 0], [128, 1024], BF16, waits=[p.tok("SW")])
        dump("scrq", scr["q"][0, 0], [128, 1024], BF16, waits=[p.tok("SW")])
        dump("scrqall", scr["q"][0].rearrange("k p f -> (k p) f"), [1024, 1024], BF16,
             waits=[p.tok("SW")])
        dump("scraw1", scr["aw1"][:].rearrange("q k p f -> (q k p) f"), [4096, 1024], BF16,
             waits=[p.tok("SW")])
        dump("baw1", baw1[:], [1, 4096], BF16, waits=[p.tok("SV"), p.tok("ST")])

        gslot = {"i": 0}

        def load_slot(scr_name, q):
            i = gslot["i"] % 2
            gslot["i"] += 1
            tok = None
            for k in range(8):
                tok = p.op("d", lambda e, sn=scr_name, q=q, i=i, k=k: e.dma_start(
                    out=wb[i][:, k * 1024:(k + 1) * 1024], in_=scr[sn][q, k]),
                    waits=([wb_free[i]] if (k == 0 and wb_free[i]) else []), inc="SW")
            wb_ready[i] = tok
            return i

        def done_slot(i, tok):
            wb_free[i] = tok

        site_guard = {"ln1": None, "ln2": None, "oln": None, "fln": None,
                      "q1": None, "q2": None}

        def ln_stats(xap, site, waits, eps=1e-5):
            b = SC[site]
            gw = [site_guard[site]] if site_guard[site] else []
            for c in range(NCHUNK):
                p.op("s", lambda e, c=c, b=b, xap=xap: e.activation(
                    out=sqd[:], in_=xap[:, c * 1024:(c + 1) * 1024], func=AF.Square,
                    accum_out=stats[:, b + 2 + c:b + 3 + c]),
                    waits=list(waits) + gw, inc="SS")
                p.op("v", lambda e, c=c, b=b, xap=xap: e.reduce_sum(
                    out=stats[:, b + c:b + 1 + c], in_=xap[:, c * 1024:(c + 1) * 1024],
                    axis=AX.X), waits=waits, inc="SV")
            tok_s = p.tok("SS")
            tok_v = p.tok("SV")
            p.op("v", lambda e, b=b: e.tensor_scalar(
                out=stats[:, b + 4:b + 6], in0=stats[:, b:b + 2],
                scalar1=1.0 / D, scalar2=None, op0=ALU.mult),
                waits=[tok_v, p.tok("SV")], inc="SV")
            p.op("v", lambda e, b=b: e.tensor_scalar(
                out=stats[:, b + 6:b + 8], in0=stats[:, b + 2:b + 4],
                scalar1=1.0 / D, scalar2=None, op0=ALU.mult),
                waits=[tok_s, p.tok("SV")], inc="SV")
            p.op("v", lambda e, b=b: e.tensor_tensor(
                out=sst[:, 0:2], in0=stats[:, b + 4:b + 6], in1=stats[:, b + 4:b + 6],
                op=ALU.mult), waits=[p.tok("SV")], inc="SV")
            p.op("v", lambda e, b=b: e.tensor_tensor(
                out=sst[:, 28:30], in0=stats[:, b + 6:b + 8], in1=sst[:, 0:2],
                op=ALU.subtract), waits=[p.tok("SV")], inc="SV")
            p.op("v", lambda e, b=b, eps=eps: e.tensor_scalar(
                out=stats[:, b + 6:b + 8], in0=sst[:, 28:30],
                scalar1=eps, scalar2=None, op0=ALU.add), waits=[p.tok("SV")], inc="SV")
            tok_var = p.tok("SV")
            p.table(E_SET)
            p.op("s", lambda e, b=b: e.activation(out=sst[:, 2:4], in_=stats[:, b + 6:b + 8],
                                                  func=AF.Ln),
                 waits=[tok_var], inc="SS")
            p.op("s", lambda e, b=b: e.activation(out=stats[:, b + 6:b + 8], in_=sst[:, 2:4],
                                                  func=AF.Exp, scale=-0.5),
                 waits=[p.tok("SS")], inc="SS")
            return p.tok("SS")

        def ln_apply(xap, site, outap, tok_rstd, waits=()):
            b = SC[site]
            for c in range(NCHUNK):
                p.op("v", lambda e, c=c, b=b, xap=xap, outap=outap: e.tensor_scalar(
                    out=outap[:, c * 1024:(c + 1) * 1024],
                    in0=xap[:, c * 1024:(c + 1) * 1024],
                    scalar1=stats[:, b + 4 + c:b + 5 + c],
                    scalar2=stats[:, b + 6 + c:b + 7 + c],
                    op0=ALU.subtract, op1=ALU.mult),
                    waits=list(waits) + [tok_rstd, p.tok("SV")], inc="SV")
            return p.tok("SV")

        def transpose_2k(src_f32, tok_src, dst_bf):
            for w in range(4):
                bank = pmm[w % 2]
                for i in range(4):
                    d = w * 4 + i
                    c, j = d % 2, d // 2
                    p.op("t", lambda e, c=c, j=j, i=i, bank=bank, src_f32=src_f32: e.transpose(
                        out=bank[:, i * 128:(i + 1) * 128],
                        in_=src_f32[:, c * 1024 + j * 128: c * 1024 + (j + 1) * 128],
                        identity=Idf[:]),
                        waits=([tok_src, tok_consts] +
                               ([pmm_free[w % 2]] if pmm_free[w % 2] else [])) if i == 0 else [],
                        inc=("ST" if i == 3 else None))
                tok_t = p.tok("ST")
                p.op("v", lambda e, w=w, bank=bank, dst_bf=dst_bf: e.tensor_copy(
                    out=dst_bf[:, w * 512:(w + 1) * 512], in_=bank[:]),
                    waits=[tok_t], inc="SV")
                pmm_free[w % 2] = p.tok("SV")
            return p.tok("SV")

        def proj_tm(wslot_i, tok_w, tok_hT, bias_row, tok_b, out_f32=None, out_vc=False,
                    evac_waits=()):
            last = None
            for c in range(NCHUNK):
                for n in range(2):
                    bi = (c * 2 + n) % 2
                    bank = pmm[bi]
                    p.op("t", lambda e, n=n, bank=bank, br=bias_row: e.matmul(
                        out=bank[:], lhsT=Onc[:], rhs=br[0:1, n * 512:(n + 1) * 512],
                        start=True, stop=False),
                        waits=[tok_w, tok_hT, tok_b, tok_consts] +
                              ([pmm_free[bi]] if pmm_free[bi] else []))
                    for k in range(8):
                        p.op("t", lambda e, c=c, n=n, k=k, i=wslot_i, bank=bank: e.matmul(
                            out=bank[:], lhsT=hT[:, (k * 2 + c) * 128:(k * 2 + c + 1) * 128],
                            rhs=wb[i][:, k * 1024 + n * 512: k * 1024 + (n + 1) * 512],
                            start=False, stop=(k == 7)), inc=("ST" if k == 7 else None))
                    tok_mm = p.tok("ST")
                    if out_vc:
                        vview = Vc[:].rearrange("p (cc hh s) -> p cc hh s", cc=2, hh=16)[:, c, n * 8:(n + 1) * 8, 0:64]
                        pview = bank[:].rearrange("p (hh s) -> p hh s", hh=8)
                        p.op("v", lambda e, vv=vview, pv=pview: e.tensor_copy(out=vv, in_=pv),
                             waits=[tok_mm, tok_memset], inc="SV")
                    else:
                        p.op("v", lambda e, c=c, n=n, bank=bank, of=out_f32: e.tensor_copy(
                            out=of[:, c * 1024 + n * 512: c * 1024 + (n + 1) * 512],
                            in_=bank[:]), waits=[tok_mm] + list(evac_waits), inc="SV")
                    pmm_free[bi] = p.tok("SV")
                    last = p.tok("SV")
            return last

        def headT(src_f32, tok_src, dst_bf64):
            for w in range(8):
                bank = pmm[w % 2]
                for i in range(4):
                    d = w * 4 + i
                    h, c = d // 2, d % 2
                    p.op("t", lambda e, h=h, c=c, i=i, bank=bank, src_f32=src_f32: e.transpose(
                        out=bank[0:64, i * 128:(i + 1) * 128],
                        in_=src_f32[:, c * 1024 + h * 64: c * 1024 + (h + 1) * 64],
                        identity=Idf[:]),
                        waits=([tok_src, tok_consts] +
                               ([pmm_free[w % 2]] if pmm_free[w % 2] else [])) if i == 0 else [],
                        inc=("ST" if i == 3 else None))
                tok_t = p.tok("ST")
                p.op("v", lambda e, w=w, bank=bank, dst=dst_bf64: e.tensor_copy(
                    out=dst[:, w * 512:(w + 1) * 512], in_=bank[0:64, :]),
                    waits=[tok_t], inc="SV")
                pmm_free[w % 2] = p.tok("SV")
            return p.tok("SV")

        def phi(src64, tok_src, dst64):
            p.table(E_SET)
            for w in range(8):
                bi = 2 + (w % 2)
                bank = pmm[bi]
                for i in range(2):
                    h = w * 2 + i
                    p.op("t", lambda e, h=h, i=i, bank=bank, src64=src64: e.matmul(
                        out=bank[0:64, i * 256:(i + 1) * 256], lhsT=projb[:],
                        rhs=src64[:, h * 256:(h + 1) * 256], start=True, stop=True),
                        waits=([tok_src, tok_projb] +
                               ([pmm_free[bi]] if pmm_free[bi] else [])) if i == 0 else [],
                        inc=("ST" if i == 1 else None))
                tok_mm = p.tok("ST")
                p.op("v", lambda e, bank=bank: e.tensor_scalar(
                    out=e1[0:64, :], in0=bank[0:64, :], scalar1=0.0, scalar2=None,
                    op0=ALU.min), waits=[tok_mm, p.tok("SP")], inc="SV")
                tok_min = p.tok("SV")
                p.op("v", lambda e, bank=bank: e.tensor_scalar(
                    out=e2[0:64, :], in0=bank[0:64, :], scalar1=0.0, scalar2=None,
                    op0=ALU.max), inc="SV")
                pmm_free[bi] = p.tok("SV")
                tok_max = p.tok("SV")
                p.op("s", lambda e: e.activation(out=e3[0:64, :], in_=e1[0:64, :],
                                                 func=AF.Exp),
                     waits=[tok_min, p.tok("SP")], inc="SS")
                tok_exp = p.tok("SS")
                p.op("g", lambda e, w=w, dst64=dst64: e.tensor_tensor(
                    out=dst64[:, w * 512:(w + 1) * 512], in0=e3[0:64, :], in1=e2[0:64, :],
                    op=ALU.add), waits=[tok_exp, tok_max], inc="SP")
            return p.tok("SP")

        def kc_transposes(tok_kt):
            for c in range(NCHUNK):
                for h in range(16):
                    p.op("t", lambda e, c=c, h=h: e.transpose(
                        out=pbf[:, h * 64:(h + 1) * 64],
                        in_=KT[:, h * 256 + c * 128: h * 256 + (c + 1) * 128],
                        identity=Id[0:64, 0:64]),
                        waits=([tok_kt, tok_consts] +
                               ([pbf_free[0]] if pbf_free[0] else [])) if h == 0 else [],
                        inc=("ST" if h == 15 else None))
                tok_t = p.tok("ST")
                p.op("v", lambda e, c=c: e.tensor_copy(
                    out=Kc[:, c * 1024:(c + 1) * 1024], in_=pbf[:, 0:1024]),
                    waits=[tok_t], inc="SV")
                pbf_free[0] = p.tok("SV")
            return p.tok("SV")

        state_started = {"st0": False, "st1": False, "ksb": False}
        state_ready = {"tok": None}
        stS_read = {"tok": None}

        def state_update(kc_src, tok_kc, tok_vc):
            for c in range(NCHUNK):
                emit_state_chunk(kc_src, c, tok_kc, tok_vc)

        def emit_state_chunk(kc_src, c, tok_kc, tok_vc):
            for h in range(16):
                bank = st0 if h < 8 else st1
                sname = "st0" if h < 8 else "st1"
                first = not state_started[sname]
                state_started[sname] = True
                w = []
                if h == 0:
                    w = [tok_kc, tok_vc, tok_memset]
                    if stS_read["tok"] is not None:
                        w.append(stS_read["tok"])
                p.op("t", lambda e, c=c, h=h, bank=bank, first=first, kc_src=kc_src: e.matmul(
                    out=bank[0:64, (h % 8) * 64:(h % 8 + 1) * 64],
                    lhsT=kc_src[:, (c * 16 + h) * 64:(c * 16 + h + 1) * 64],
                    rhs=Vc[:, c * 1040 + h * 65: c * 1040 + h * 65 + 64],
                    start=first, stop=True), waits=w)
                firstk = not state_started["ksb"]
                state_started["ksb"] = True
                p.op("t", lambda e, c=c, h=h, firstk=firstk, kc_src=kc_src: e.matmul(
                    out=ksb[0:64, h:h + 1],
                    lhsT=kc_src[:, (c * 16 + h) * 64:(c * 16 + h + 1) * 64],
                    rhs=Ontk[:], start=firstk, stop=True),
                    inc=("ST" if h == 15 else None))
            state_ready["tok"] = p.tok("ST")

        tok_x = [None, None]
        pref_xin_used = [None, None]
        slot_kv = [None, None]

        if nprf > 0:
            tok_x[0] = dma_in(xin[0][:].rearrange("p (c dd) -> p c dd", dd=1024),
                              x_pref[0:TPT].rearrange("(c p) dd -> p c dd", p=128))
            slot_kv[0] = load_slot("k", 0)
            slot_kv[1] = load_slot("v", 0)
            tok_wk = wb_ready[slot_kv[0]]
            tok_wv = wb_ready[slot_kv[1]]
            for tile in range(nprf):
                i = tile % 2
                tok_here = tok_x[i]
                if tile + 1 < nprf:
                    w = [pref_xin_used[(tile + 1) % 2]] if pref_xin_used[(tile + 1) % 2] else []
                    tok_x[(tile + 1) % 2] = p.op("d", lambda e, t2=tile + 1: e.dma_start(
                        out=xin[t2 % 2][:].rearrange("p (c dd) -> p c dd", dd=1024),
                        in_=x_pref[t2 * TPT:(t2 + 1) * TPT].rearrange("(c p) dd -> p c dd", p=128)),
                        waits=w, inc="SW")
                xa = xin[i][:]
                tok_rstd = ln_stats(xa, "ln1", [tok_here])
                tok_h = ln_apply(xa, "ln1", hf[:], tok_rstd, waits=[tok_here])
                pref_xin_used[i] = tok_h
                tok_hT = transpose_2k(hf[:], tok_h, hT[:])
                tok_kraw = proj_tm(slot_kv[0], tok_wk, tok_hT, bkr, tok_bias["k"],
                                   out_f32=qkraw)
                tok_v = proj_tm(slot_kv[1], tok_wv, tok_hT, bvr, tok_bias["v"], out_vc=True)
                tok_krt = headT(qkraw, tok_kraw, KRT[:])
                tok_kt = phi(KRT[:], tok_krt, KT[:])
                tok_kc = kc_transposes(tok_kt)
                p.op("v", lambda e: e.tensor_scalar(
                    out=attn_tm[:], in0=Kc[:], scalar1=Mcol[:, 0:1], scalar2=None,
                    op0=ALU.mult), waits=[tok_kc, tok_consts], inc="SV")
                tok_kcm = p.tok("SV")
                if tile == nprf - 1:
                    dump("kcm", attn_tm[:], [128, 2048], BF16, waits=[tok_kcm])
                site_guard["ln1"] = tok_kcm
                state_update(attn_tm[:], tok_kcm, tok_v)
            done_slot(slot_kv[0], state_ready["tok"])
            done_slot(slot_kv[1], state_ready["tok"])

        tok_x = [None, None]
        xin_used = [None, None]
        tok_bl2_prev = {"tok": None}
        w0 = [state_ready["tok"]] if state_ready["tok"] else []
        if nprf > 0:
            w0 += [pref_xin_used[0]] if pref_xin_used[0] else []
        tok_x[0] = p.op("d", lambda e: e.dma_start(
            out=xin[0][:].rearrange("p (c dd) -> p c dd", dd=1024),
            in_=x_main[0:TPT].rearrange("(c p) dd -> p c dd", p=128)),
            waits=w0, inc="SW")
        tok_y_dma = None

        front_toks = {}

        def emit_front(tile, garena_free):
            i = tile % 2
            tok_here = tok_x[i]
            xa = xin[i][:]
            tok_rstd = ln_stats(xa, "ln1", [tok_here])
            tok_h = ln_apply(xa, "ln1", hf[:], tok_rstd, waits=[tok_here])
            tok_hT = transpose_2k(hf[:], tok_h, hT[:])

            if tile == 0:
                dump("xin", xa, [128, 2048], F32, waits=[tok_here])
                dump("stats", stats[:], [128, 64], F32, waits=[tok_rstd, tok_h])
                dump("hf", hf[:], [128, 2048], F32, waits=[tok_h])
                dump("hT", hT[:], [128, 2048], BF16, waits=[tok_hT])
            gfw = [garena_free] if garena_free else []
            sq_ = load_slot("q", 0)
            tok_qraw = proj_tm(sq_, wb_ready[sq_], tok_hT, bqr, tok_bias["q"], out_f32=qkraw,
                               evac_waits=gfw)
            if tile == 0:
                dump("qraw", qkraw, [128, 2048], F32, waits=[tok_qraw])
            tok_qrt = headT(qkraw, tok_qraw, QRT[:])
            done_slot(sq_, tok_qrt)
            sk_ = load_slot("k", 0)
            tok_kraw = proj_tm(sk_, wb_ready[sk_], tok_hT, bkr, tok_bias["k"], out_f32=qkraw,
                               evac_waits=gfw)
            tok_krt = headT(qkraw, tok_kraw, KRT[:])
            done_slot(sk_, tok_krt)
            sv_ = load_slot("v", 0)
            tok_v = proj_tm(sv_, wb_ready[sv_], tok_hT, bvr, tok_bias["v"], out_vc=True)
            done_slot(sv_, tok_v)
            site_guard["ln1"] = tok_h
            front_toks[tile] = dict(i=i, tok_here=tok_here, xa=xa,
                                    tok_qrt=tok_qrt, tok_krt=tok_krt,
                                    tok_v=tok_v)

        def emit_front_b(tile):
            F = front_toks[tile]
            tok_qrt = F["tok_qrt"]; tok_krt = F["tok_krt"]; tok_v = F["tok_v"]
            tok_qt = phi(QRT[:], tok_qrt, QT[:])
            tok_kt = phi(KRT[:], tok_krt, KT[:])
            tok_kc = kc_transposes(tok_kt)

            if tile == 0:
                dump("QT", QT[:], [64, 4096], BF16, waits=[tok_qt])
                dump("KT", KT[:], [64, 4096], BF16, waits=[tok_kt])
                dump("Vc", Vc[:], [128, 2080], BF16, waits=[tok_v])
                dump("Kc", Kc[:], [128, 2048], BF16, waits=[tok_kc])

            F["tok_qt"] = tok_qt
            F["tok_kt"] = tok_kt
            F["tok_kc"] = tok_kc


        def emit_back1(tile):
            F = front_toks[tile]
            i = F["i"]; tok_here = F["tok_here"]; xa = F["xa"]
            tok_qt = F["tok_qt"]; tok_kt = F["tok_kt"]
            tok_kc = F["tok_kc"]; tok_v = F["tok_v"]
            if tile + 1 < nm:
                wprev = []
                if tile == 0 and nprf > 0 and pref_xin_used[1]:
                    wprev = [pref_xin_used[1]]
                elif xin_used[(tile + 1) % 2]:
                    wprev = [xin_used[(tile + 1) % 2]]
                tok_x[(tile + 1) % 2] = p.op("d", lambda e, t2=tile + 1: e.dma_start(
                    out=xin[t2 % 2][:].rearrange("p (c dd) -> p c dd", dd=1024),
                    in_=x_main[t2 * TPT:(t2 + 1) * TPT].rearrange("(c p) dd -> p c dd", p=128)),
                    waits=wprev, inc="SW")
            xa = xin[i][:]
            for c in range(NCHUNK):
                if state_ready["tok"] is not None:
                    p.op("v", lambda e: e.tensor_copy(out=stS[0:64, 0:512], in_=st0[:]),
                         waits=[state_ready["tok"], tok_memset], inc="SV")
                    p.op("v", lambda e: e.tensor_copy(out=stS[0:64, 512:1024], in_=st1[:]),
                         inc="SV")
                    p.op("v", lambda e: e.tensor_copy(out=ksS[:], in_=ksb[:]), inc="SV")
                    stS_read["tok"] = p.tok("SV")
                tok_st_copy = p.tok("SV")
                if tile == 0 and c == 0:
                    dump("stS0", stS[:], [64, 1024], BF16, waits=[tok_st_copy])
                    dump("ksS0", ksS[:], [64, 16], BF16, waits=[tok_st_copy])
                    dump("mcol", Mcol[:], [128, 1], F32, waits=[tok_consts])
                tok_masks = [None] * 4
                for grp in range(4):
                    sbank = pmm[grp % 2]
                    for ii in range(4):
                        h = grp * 4 + ii
                        p.op("t", lambda e, h=h, ii=ii, c=c, sbank=sbank: e.matmul(
                            out=sbank[:, ii * 128:(ii + 1) * 128],
                            lhsT=KT[0:64, h * 256 + c * 128: h * 256 + (c + 1) * 128],
                            rhs=QT[0:64, h * 256 + c * 128: h * 256 + (c + 1) * 128],
                            start=True, stop=True),
                            waits=([tok_qt, tok_kt] +
                                   ([pmm_free[grp % 2]] if pmm_free[grp % 2] else [])) if ii == 0 else [],
                            inc=("ST" if ii == 3 else None))
                    tok_sc = p.tok("ST")
                    p.op("v", lambda e, grp=grp, sbank=sbank: e.tensor_tensor(
                        out=Ssb[:, grp * 512:(grp + 1) * 512], in0=sbank[:], in1=McT[:],
                        op=ALU.mult), waits=[tok_sc, tok_consts], inc="SV")
                    tok_masks[grp] = p.tok("SV")
                    pmm_free[grp % 2] = tok_masks[grp]
                for grp in range(4):
                    nbank = pmm[2 + grp % 2]
                    tok_mask = tok_masks[grp]
                    for ii in range(4):
                        h = grp * 4 + ii
                        p.op("t", lambda e, h=h, ii=ii, c=c, nbank=nbank: e.matmul(
                            out=nbank[:, ii * 128:ii * 128 + 65],
                            lhsT=Ssb[:, h * 128:(h + 1) * 128],
                            rhs=Vc[:, c * 1040 + h * 65: c * 1040 + (h + 1) * 65],
                            start=True, stop=False),
                            waits=([tok_mask, tok_v, tok_st_copy, tok_memset] +
                                   ([pmm_free[2 + grp % 2]] if pmm_free[2 + grp % 2] else [])) if ii == 0 else [])
                        p.op("t", lambda e, h=h, ii=ii, c=c, nbank=nbank: e.matmul(
                            out=nbank[:, ii * 128:ii * 128 + 64],
                            lhsT=QT[0:64, h * 256 + c * 128: h * 256 + (c + 1) * 128],
                            rhs=stS[0:64, h * 64:(h + 1) * 64], start=False, stop=False))
                        p.op("t", lambda e, h=h, ii=ii, c=c, nbank=nbank: e.matmul(
                            out=nbank[:, ii * 128 + 64:ii * 128 + 65],
                            lhsT=QT[0:64, h * 256 + c * 128: h * 256 + (c + 1) * 128],
                            rhs=ksS[0:64, h:h + 1], start=False, stop=True),
                            inc=("ST" if ii == 3 else None))
                    tok_n = p.tok("ST")
                    dview = nbank[:].rearrange("p (hh s) -> p hh s", hh=4)[:, :, 64:65]
                    p.op("v", lambda e, grp=grp, dv=dview: e.tensor_scalar(
                        out=dens[:, grp * 4:(grp + 1) * 4], in0=dv,
                        scalar1=1e-6, scalar2=None, op0=ALU.add),
                        waits=[tok_n, p.tok("SV")], inc="SV")
                    p.op("v", lambda e, grp=grp: e.reciprocal(
                        out=recips[:, grp * 4:(grp + 1) * 4],
                        in_=dens[:, grp * 4:(grp + 1) * 4]),
                        waits=[p.tok("SV")], inc="SV")
                    for ii in range(4):
                        h = grp * 4 + ii
                        p.op("v", lambda e, h=h, ii=ii, c=c, nbank=nbank: e.tensor_scalar(
                            out=attn_tm[:, c * 1024 + h * 64: c * 1024 + (h + 1) * 64],
                            in0=nbank[:, ii * 128:ii * 128 + 64],
                            scalar1=recips[:, h:h + 1], scalar2=None, op0=ALU.mult),
                            waits=([p.tok("SV")] if ii == 0 else []),
                            inc=("SV" if ii == 3 else None))
                    pmm_free[2 + grp % 2] = p.tok("SV")
                tok_div = p.tok("SV")
                emit_state_chunk(Kc[:], c, tok_kc, tok_v)
                for j in range(8):
                    p.op("t", lambda e, c=c, j=j: e.transpose(
                        out=pbf[:, j * 128:(j + 1) * 128],
                        in_=attn_tm[:, c * 1024 + j * 128: c * 1024 + (j + 1) * 128],
                        identity=Id[:]),
                        waits=([tok_div] + ([pbf_free[0]] if pbf_free[0] else [])) if j == 0 else [],
                        inc=("ST" if j == 7 else None))
                tok_at = p.tok("ST")
                p.op("v", lambda e, c=c: e.tensor_copy(
                    out=attnT[:, c * 1024:(c + 1) * 1024], in_=pbf[:]),
                    waits=[tok_at], inc="SV")
                pbf_free[0] = p.tok("SV")
            tok_attnT = p.tok("SV")

            if tile == 0:
                dump("attn_tm", attn_tm[:], [128, 2048], BF16, waits=[tok_attnT])

            so_ = load_slot("o", 0)
            for c in range(NCHUNK):
                for n in range(2):
                    bi = (c * 2 + n) % 2
                    bank = pmm[bi]
                    for k in range(8):
                        p.op("t", lambda e, c=c, n=n, k=k, i=so_, bank=bank: e.matmul(
                            out=bank[:],
                            lhsT=attnT[:, c * 1024 + k * 128: c * 1024 + (k + 1) * 128],
                            rhs=wb[i][:, k * 1024 + n * 512: k * 1024 + (n + 1) * 512],
                            start=(k == 0), stop=(k == 7)),
                            waits=([tok_attnT, wb_ready[so_]] +
                                   ([pmm_free[bi]] if pmm_free[bi] else [])) if k == 0 else [],
                            inc=("ST" if k == 7 else None))
                    tok_mm = p.tok("ST")
                    p.op("v", lambda e, c=c, n=n, bank=bank, xa=xa: e.tensor_tensor(
                        out=xr0[:, c * 1024 + n * 512: c * 1024 + (n + 1) * 512],
                        in0=bank[:], in1=xa[:, c * 1024 + n * 512: c * 1024 + (n + 1) * 512],
                        op=ALU.add), waits=[tok_mm, tok_here], inc="SV")
                    pmm_free[bi] = p.tok("SV")
            tok_x2 = p.tok("SV")
            xin_used[i] = tok_x2
            done_slot(so_, tok_x2)

            if tile == 0:
                dump("x2", xr0[:], [128, 2048], F32, waits=[tok_x2])

            tok_rstd2 = ln_stats(xr0[:], "ln2", [tok_x2])
            tok_h2 = ln_apply(xr0[:], "ln2", hf[:], tok_rstd2, waits=[tok_x2])
            if tile == 0:
                dump("h2f", hf[:], [128, 2048], F32, waits=[tok_h2])
            tok_h2T = transpose_2k(hf[:], tok_h2, hT[:])
            if tile == 0:
                dump("h2T", hT[:], [128, 2048], BF16, waits=[tok_h2T])

            p.table(G_SET)
            for qf in range(4):
                sa = load_slot("aw1", qf)
                for m in range(8):
                    fc = qf * 8 + m
                    bi = m % 2
                    bank = pmm[bi]
                    p.op("t", lambda e, fc=fc, bank=bank: e.matmul(
                        out=bank[:, 0:256], lhsT=baw1[0:1, fc * 128:(fc + 1) * 128],
                        rhs=Onr[0:1, 0:256], start=True, stop=False),
                        waits=[wb_ready[sa], tok_h2T, tok_bias["aw1"], tok_consts] +
                              ([pmm_free[bi]] if pmm_free[bi] else []))
                    for k in range(8):
                        p.op("t", lambda e, m=m, k=k, i=sa, bank=bank: e.matmul(
                            out=bank[:, 0:256],
                            lhsT=wb[i][:, k * 1024 + m * 128: k * 1024 + (m + 1) * 128],
                            rhs=hT[:, k * 256:(k + 1) * 256], start=False, stop=(k == 7)),
                            inc=("ST" if k == 7 else None))
                    tok_mm = p.tok("ST")
                    p.op("s", lambda e, fc=fc, bank=bank: e.activation(
                        out=garena[:, fc * 256:(fc + 1) * 256], in_=bank[:, 0:256],
                        func=AF.Gelu), waits=[tok_mm], inc="SS")
                    pmm_free[bi] = p.tok("SS")
                done_slot(sa, p.tok("SS"))
            tok_g = p.tok("SS")

            if tile == 0:
                dump("gattn", garena[:], [128, 8192], BF16, waits=[tok_g])

            sa2 = [None] * 4
            for kq in range(4):
                sa2[kq] = load_slot("aw2", kq)
                for cn in range(4):
                    c, n = cn // 2, cn % 2
                    bank = pmm[cn]
                    if kq == 0:
                        p.op("t", lambda e, n=n, bank=bank: e.matmul(
                            out=bank[:], lhsT=Onc[:], rhs=ab2b[0:1, n * 512:(n + 1) * 512],
                            start=True, stop=False),
                            waits=[wb_ready[sa2[0]], tok_g, tok_smallcasts, tok_consts] +
                                  ([pmm_free[cn]] if pmm_free[cn] else []))
                    for k8 in range(8):
                        fc = kq * 8 + k8
                        p.op("t", lambda e, fc=fc, c=c, n=n, k8=k8, i=sa2[kq], bank=bank: e.matmul(
                            out=bank[:],
                            lhsT=garena[:, fc * 256 + c * 128: fc * 256 + (c + 1) * 128],
                            rhs=wb[i][:, k8 * 1024 + n * 512: k8 * 1024 + (n + 1) * 512],
                            start=False, stop=(fc == 31)),
                            waits=([wb_ready[sa2[kq]]] if (k8 == 0 and cn == 0 and kq > 0) else []),
                            inc=("ST" if (cn == 3 and k8 == 7) else None))
                tok_kq = p.tok("ST")
                done_slot(sa2[kq], tok_kq)
            tok_mm2_all = p.tok("ST")
            for cn in range(4):
                c, n = cn // 2, cn % 2
                bank = pmm[cn]
                p.op("v", lambda e, c=c, n=n, bank=bank: e.tensor_tensor(
                    out=e1[:, 0:512], in0=bank[:],
                    in1=xr0[:, c * 1024 + n * 512: c * 1024 + (n + 1) * 512],
                    op=ALU.add), waits=[tok_mm2_all, p.tok("SV")], inc="SV")
                p.op("v", lambda e, c=c, n=n, xa=xa: e.tensor_tensor(
                    out=xr1[:, c * 1024 + n * 512: c * 1024 + (n + 1) * 512],
                    in0=e1[:, 0:512],
                    in1=xa[:, c * 1024 + n * 512: c * 1024 + (n + 1) * 512],
                    op=ALU.add), waits=[p.tok("SV")], inc="SV")
                pmm_free[cn] = p.tok("SV")
            tok_x3 = p.tok("SV")
            xin_used[i] = tok_x3

            tok_rstd3 = ln_stats(xr1[:], "oln", [tok_x3])
            tok_y1 = ln_apply(xr1[:], "oln", xr0[:], tok_rstd3, waits=[tok_x3])
            for c in range(NCHUNK):
                p.op("v", lambda e, c=c: e.reduce_max(
                    out=stats[:, 32 + c:33 + c], in_=xr0[:, c * 1024:(c + 1) * 1024],
                    axis=AX.X, apply_absolute_value=True), waits=[tok_y1], inc="SV")
            tok_maxy = p.tok("SV")

            if tile == 0:
                dump("y1", xr0[:], [128, 2048], F32, waits=[tok_y1])

            b3 = SC["oln"]
            p.op("v", lambda e, b3=b3: e.tensor_tensor(
                out=sst[:, 4:6], in0=stats[:, b3 + 6:b3 + 8], in1=stats[:, b3 + 6:b3 + 8],
                op=ALU.mult), waits=[tok_y1, p.tok("SV")], inc="SV")
            p.op("v", lambda e: e.tensor_scalar(
                out=stats[:, 36:38], in0=sst[:, 4:6], scalar1=-1e-5, scalar2=1.0 + 1e-5,
                op0=ALU.mult, op1=ALU.add), waits=[p.tok("SV")], inc="SV")
            tok_msq = p.tok("SV")
            p.table(E_SET)
            p.op("s", lambda e: e.activation(out=sst[:, 6:8], in_=stats[:, 36:38],
                                             func=AF.Ln),
                 waits=[tok_msq] + ([site_guard["q1"]] if site_guard["q1"] else []),
                 inc="SS")
            p.op("s", lambda e: e.activation(out=stats[:, 38:40], in_=sst[:, 6:8],
                                             func=AF.Exp, scale=-0.5),
                 waits=[p.tok("SS")], inc="SS")
            tok_rms2 = p.tok("SS")
            p.op("v", lambda e: e.tensor_tensor(out=sst[:, 8:10], in0=stats[:, 38:40],
                                                in1=stats[:, 32:34], op=ALU.mult),
                 waits=[tok_rms2, tok_maxy, p.tok("SV")], inc="SV")
            p.op("v", lambda e: e.tensor_scalar(out=sst[:, 30:32], in0=sst[:, 8:10],
                                                scalar1=1e-5, scalar2=None, op0=ALU.max),
                 waits=[p.tok("SV")], inc="SV")
            p.op("v", lambda e: e.reciprocal(out=sst[:, 10:12], in_=sst[:, 30:32]),
                 waits=[p.tok("SV")], inc="SV")
            p.op("v", lambda e: e.tensor_scalar(out=sst[:, 26:28], in0=sst[:, 10:12],
                                                scalar1=127.0, scalar2=None, op0=ALU.mult),
                 waits=[p.tok("SV")], inc="SV")
            p.op("v", lambda e: e.tensor_tensor(out=sst[:, 12:14], in0=stats[:, 38:40],
                                                in1=sst[:, 26:28], op=ALU.mult),
                 waits=[p.tok("SV")], inc="SV")
            p.op("v", lambda e: e.tensor_scalar(out=sst[:, 8:10], in0=sst[:, 30:32],
                                                scalar1=1.0 / 127.0, scalar2=None,
                                                op0=ALU.mult), waits=[p.tok("SV")], inc="SV")
            p.op("v", lambda e: e.tensor_scalar(out=sst[:, 14:16], in0=sst[:, 8:10],
                                                scalar1=swc[:, 0:1], scalar2=None,
                                                op0=ALU.mult), waits=[p.tok("SV")], inc="SV")
            tok_qs = p.tok("SV")
            for c in range(NCHUNK):
                p.op("v", lambda e, c=c: e.tensor_scalar(
                    out=i8t[:, c * 1024:(c + 1) * 1024], in0=xr0[:, c * 1024:(c + 1) * 1024],
                    scalar1=sst[:, 12 + c:13 + c], scalar2=None, op0=ALU.mult),
                    waits=[tok_qs, p.tok("SV")], inc="SV")
                p.op("v", lambda e, c=c: e.tensor_scalar(
                    out=xqb[:, c * 1024:(c + 1) * 1024], in0=i8t[:, c * 1024:(c + 1) * 1024],
                    scalar1=sst[:, 14 + c:15 + c], scalar2=None, op0=ALU.mult),
                    waits=[p.tok("SV")], inc="SV")
            tok_xq = p.tok("SV")
            if tile == 0:
                dump("xqb", xqb[:], [128, 2048], BF16, waits=[tok_xq])
            for c in range(NCHUNK):
                for j in range(8):
                    p.op("t", lambda e, c=c, j=j: e.transpose(
                        out=pbf[:, j * 128:(j + 1) * 128],
                        in_=xqb[:, c * 1024 + j * 128: c * 1024 + (j + 1) * 128],
                        identity=Id[:]),
                        waits=([tok_xq] + ([pbf_free[0]] if pbf_free[0] else [])) if j == 0 else [],
                        inc=("ST" if j == 7 else None))
                tok_tt = p.tok("ST")
                hview = hT[:].rearrange("p (j cc f) -> p j cc f", j=8, cc=2)[:, :, c, :]
                pview = pbf[:].rearrange("p (j f) -> p j f", j=8)
                p.op("v", lambda e, hv=hview, pv=pview: e.tensor_copy(out=hv, in_=pv),
                     waits=[tok_tt], inc="SV")
                pbf_free[0] = p.tok("SV")
            tok_xqT = p.tok("SV")

            p.table(G_SET)
            for qf in range(4):
                sf = load_slot("f1", qf)
                for m in range(8):
                    fc = qf * 8 + m
                    bi = m % 2
                    bank = pmm[bi]
                    p.op("t", lambda e, fc=fc, bank=bank: e.matmul(
                        out=bank[:, 0:256], lhsT=fb1b[0:1, fc * 128:(fc + 1) * 128],
                        rhs=Onr[0:1, 0:256], start=True, stop=False),
                        waits=[wb_ready[sf], tok_xqT, tok_smallcasts, tok_consts] +
                              ([pmm_free[bi]] if pmm_free[bi] else []))
                    for k in range(8):
                        p.op("t", lambda e, m=m, k=k, i=sf, bank=bank: e.matmul(
                            out=bank[:, 0:256],
                            lhsT=wb[i][:, k * 1024 + m * 128: k * 1024 + (m + 1) * 128],
                            rhs=hT[:, k * 256:(k + 1) * 256], start=False, stop=(k == 7)),
                            inc=("ST" if k == 7 else None))
                    tok_mm = p.tok("ST")
                    p.op("s", lambda e, fc=fc, bank=bank: e.activation(
                        out=garena[:, fc * 256:(fc + 1) * 256], in_=bank[:, 0:256],
                        func=AF.Gelu), waits=[tok_mm], inc="SS")
                    pmm_free[bi] = p.tok("SS")
                done_slot(sf, p.tok("SS"))
            tok_g2 = p.tok("SS")
            if tile == 0:
                dump("g2", garena[:], [128, 8192], BF16, waits=[tok_g2])

            F["tok_g2"] = tok_g2
            F["tok_qs"] = tok_qs
            F["tok_h2"] = tok_h2
            site_guard["ln2"] = tok_h2
            site_guard["oln"] = tok_qs
            site_guard["q1"] = tok_qs

        def emit_back1q(tile):
            F = front_toks[tile]
            tok_g2 = F["tok_g2"]
            sq_tok = [None] * 32
            for fc in range(32):
                p.op("v", lambda e, fc=fc: e.tensor_tensor(
                    out=ints2[:, fc * 256:(fc + 1) * 256],
                    in0=garena[:, fc * 256:(fc + 1) * 256],
                    in1=garena[:, fc * 256:(fc + 1) * 256], op=ALU.mult),
                    waits=(([tok_g2] + ([tok_bl2_prev["tok"]] if tok_bl2_prev["tok"] else []))
                           if fc == 0 else []),
                    inc="SV")
                sq_tok[fc] = p.tok("SV")
            tok_sqall = p.tok("SV")
            for fc in range(32):
                p.op("t", lambda e, fc=fc: e.matmul(
                    out=pmm[3][0:1, 0:256], lhsT=Ontk[:], rhs=ints2[:, fc * 256:(fc + 1) * 256],
                    start=(fc == 0), stop=(fc == 31)),
                    waits=[sq_tok[fc]] + (([tok_consts] +
                           ([pmm_free[3]] if pmm_free[3] else [])) if fc == 0 else []),
                    inc=("ST" if fc == 31 else None))
            for fc in range(1, 32):
                p.op("v", lambda e, fc=fc: e.tensor_tensor(
                    out=gmb[fc % 2][:],
                    in0=(ints2[:, 0:256] if fc == 1 else gmb[(fc + 1) % 2][:]),
                    in1=ints2[:, fc * 256:(fc + 1) * 256],
                    op=ALU.max), waits=[p.tok("SV")], inc="SV")
            tok_ssq = p.tok("ST")
            gm_fin = gmb[31 % 2]
            tok_gm = [p.tok("SV")]
            for c in range(NCHUNK):
                p.op("t", lambda e, c=c, gm_fin=gm_fin: e.transpose(
                    out=pbf[:, c * 128:(c + 1) * 128], in_=gm_fin[:, c * 128:(c + 1) * 128],
                    identity=Id[:]),
                    waits=(tok_gm + [tok_consts] + ([pbf_free[0]] if pbf_free[0] else [])) if c == 0 else [],
                    inc=("ST" if c == 1 else None))
            tok_gt = p.tok("ST")
            for c in range(NCHUNK):
                p.op("v", lambda e, c=c: e.reduce_max(
                    out=stats[:, 52 + c:53 + c], in_=pbf[:, c * 128:(c + 1) * 128],
                    axis=AX.X), waits=[tok_gt], inc="SV")
            pbf_free[0] = p.tok("SV")
            tok_gmaxsq = p.tok("SV")
            p.table(E_SET)
            p.op("s", lambda e: e.activation(out=sst[:, 26:28], in_=stats[:, 52:54],
                                             func=AF.Ln),
                 waits=[tok_gmaxsq] + ([site_guard["q2"]] if site_guard["q2"] else []),
                 inc="SS")
            p.op("s", lambda e: e.activation(out=stats[:, 34:36], in_=sst[:, 26:28],
                                             func=AF.Exp, scale=0.5),
                 waits=[p.tok("SS")], inc="SS")
            tok_gmax = p.tok("SS")
            p.op("v", lambda e: e.tensor_copy(out=rowf[0:1, 0:256], in_=pmm[3][0:1, 0:256]),
                 waits=[tok_ssq], inc="SV")
            pmm_free[3] = p.tok("SV")
            tok_row = p.tok("SV")
            for c in range(NCHUNK):
                p.op("t", lambda e, c=c: e.matmul(
                    out=pmm[3][:, c:c + 1].bitcast(F32), lhsT=rowf[0:1, c * 128:(c + 1) * 128],
                    rhs=Oncf[0:1, 0:1], start=True, stop=True),
                    waits=([tok_row, tok_consts] + ([pmm_free[3]] if pmm_free[3] else [])) if c == 0 else [],
                    inc=("ST" if c == 1 else None))
            tok_ssqT = p.tok("ST")
            p.op("v", lambda e: e.tensor_copy(out=stats[:, 44:46], in_=pmm[3][:, 0:2]),
                 waits=[tok_ssqT], inc="SV")
            pmm_free[3] = p.tok("SV")
            p.op("v", lambda e: e.tensor_scalar(out=stats[:, 42:44], in0=stats[:, 44:46],
                                                scalar1=1.0 / FF, scalar2=1e-5,
                                                op0=ALU.mult, op1=ALU.add),
                 waits=[p.tok("SV")], inc="SV")
            tok_ms2 = p.tok("SV")
            p.table(E_SET)
            p.op("s", lambda e: e.activation(out=sst[:, 16:18], in_=stats[:, 42:44],
                                             func=AF.Ln),
                 waits=[tok_ms2] + ([site_guard["q2"]] if site_guard["q2"] else []),
                 inc="SS")
            p.op("s", lambda e: e.activation(out=stats[:, 46:48], in_=sst[:, 16:18],
                                             func=AF.Exp, scale=-0.5),
                 waits=[p.tok("SS")], inc="SS")
            tok_rmsg = p.tok("SS")
            p.op("v", lambda e: e.tensor_tensor(out=sst[:, 18:20], in0=stats[:, 46:48],
                                                in1=stats[:, 34:36], op=ALU.mult),
                 waits=[tok_rmsg, tok_gmax, p.tok("SV")], inc="SV")
            p.op("v", lambda e: e.tensor_scalar(out=sst[:, 30:32], in0=sst[:, 18:20],
                                                scalar1=1e-5, scalar2=None, op0=ALU.max),
                 waits=[p.tok("SV")], inc="SV")
            p.op("v", lambda e: e.reciprocal(out=sst[:, 20:22], in_=sst[:, 30:32]),
                 waits=[p.tok("SV")], inc="SV")
            p.op("v", lambda e: e.tensor_scalar(out=sst[:, 16:18], in0=sst[:, 20:22],
                                                scalar1=127.0, scalar2=None, op0=ALU.mult),
                 waits=[p.tok("SV")], inc="SV")
            p.op("v", lambda e: e.tensor_tensor(out=sst[:, 22:24], in0=stats[:, 46:48],
                                                in1=sst[:, 16:18], op=ALU.mult),
                 waits=[p.tok("SV")], inc="SV")
            p.op("v", lambda e: e.tensor_scalar(out=sst[:, 18:20], in0=sst[:, 30:32],
                                                scalar1=1.0 / 127.0, scalar2=None,
                                                op0=ALU.mult), waits=[p.tok("SV")], inc="SV")
            p.op("v", lambda e: e.tensor_scalar(out=sst[:, 24:26], in0=sst[:, 18:20],
                                                scalar1=swc[:, 1:2], scalar2=None,
                                                op0=ALU.mult), waits=[p.tok("SV")], inc="SV")
            tok_qs2 = p.tok("SV")
            for c in range(NCHUNK):
                p.op("t", lambda e, c=c: e.transpose(
                    out=pmm[3][0:1, c * 128:(c + 1) * 128].bitcast(F32),
                    in_=sst[:, 22 + c:23 + c], identity=Idf[:]),
                    waits=([tok_qs2, tok_consts] + ([pmm_free[3]] if pmm_free[3] else [])) if c == 0 else [],
                    inc=("ST" if c == 1 else None))
            tok_mrT = p.tok("ST")
            p.op("v", lambda e: e.tensor_copy(out=mrow[:], in_=pmm[3][0:1, 0:256]),
                 waits=[tok_mrT], inc="SV")
            pmm_free[3] = p.tok("SV")
            tok_mrow = p.tok("SV")
            p.op("t", lambda e: e.matmul(out=pmm[3][:, 0:256], lhsT=Onc[:], rhs=mrow[:],
                                         start=True, stop=True),
                 waits=[tok_mrow] + ([pmm_free[3]] if pmm_free[3] else []), inc="ST")
            tok_B = p.tok("ST")
            p.op("v", lambda e: e.tensor_copy(out=Bsb[:], in_=pmm[3][:, 0:256]),
                 waits=[tok_B], inc="SV")
            pmm_free[3] = p.tok("SV")
            tok_Bsb = p.tok("SV")
            conv_tok = [None] * 32
            for fc in range(32):
                ib = i8g[fc % 2]
                p.op("v", lambda e, fc=fc, ib=ib: e.tensor_tensor(
                    out=ib[:], in0=garena[:, fc * 256:(fc + 1) * 256], in1=Bsb[:],
                    op=ALU.mult),
                    waits=[tok_Bsb] + ([p.tok("SV")] if fc >= 2 else []), inc="SV")
                tok_i8 = p.tok("SV")
                p.op("v", lambda e, fc=fc, ib=ib: e.tensor_copy(
                    out=ints2[:, fc * 256:(fc + 1) * 256], in_=ib[:]),
                    waits=[tok_i8], inc="SV")
                conv_tok[fc] = p.tok("SV")
            tok_ints2 = p.tok("SV")
            F["conv_tok"] = conv_tok
            if tile == 0:
                dump("ints2", ints2[:], [128, 8192], BF16, waits=[tok_ints2])

            F["tok_ints2"] = tok_ints2
            F["tok_qs2"] = tok_qs2
            site_guard["q2"] = tok_qs2

        def emit_back2(tile):
            nonlocal tok_y_dma
            F = front_toks[tile]
            tok_ints2 = F["tok_ints2"]
            conv_tok = F["conv_tok"]
            tok_qs2 = F["tok_qs2"]
            sf2 = [None] * 4
            for kq in range(4):
                sf2[kq] = load_slot("f2", kq)
                for cn in range(4):
                    c, n = cn // 2, cn % 2
                    bank = pmm[cn]
                    for k8 in range(8):
                        fc = kq * 8 + k8
                        w = []
                        if k8 == 0 and cn == 0:
                            w = [wb_ready[sf2[kq]], conv_tok[kq * 8 + 7]]
                        if kq == 0 and k8 == 0:
                            w = [wb_ready[sf2[0]], conv_tok[7]] + \
                                ([pmm_free[cn]] if pmm_free[cn] else [])
                        p.op("t", lambda e, fc=fc, c=c, n=n, k8=k8, i=sf2[kq], bank=bank: e.matmul(
                            out=bank[:],
                            lhsT=ints2[:, fc * 256 + c * 128: fc * 256 + (c + 1) * 128],
                            rhs=wb[i][:, k8 * 1024 + n * 512: k8 * 1024 + (n + 1) * 512],
                            start=(fc == 0), stop=(fc == 31)), waits=w,
                            inc=("ST" if (cn == 3 and k8 == 7) else None))
                tok_kq = p.tok("ST")
                done_slot(sf2[kq], tok_kq)
            tok_bl2 = p.tok("ST")
            tok_bl2_prev["tok"] = tok_bl2
            for cn in range(4):
                c, n = cn // 2, cn % 2
                bank = pmm[cn]
                p.op("v", lambda e, c=c, n=n, bank=bank: e.scalar_tensor_tensor(
                    out=ybuf[:, c * 1024 + n * 512: c * 1024 + (n + 1) * 512],
                    in0=bank[:], scalar=sst[:, 24 + c:25 + c],
                    in1=fb2tile[:, n * 512:(n + 1) * 512], op0=ALU.mult, op1=ALU.add),
                    waits=[tok_bl2, tok_qs2] + ([tok_y_dma] if tok_y_dma else []), inc="SV")
                p.op("v", lambda e, c=c, n=n: e.tensor_tensor(
                    out=xr1[:, c * 1024 + n * 512: c * 1024 + (n + 1) * 512],
                    in0=ybuf[:, c * 1024 + n * 512: c * 1024 + (n + 1) * 512],
                    in1=xr0[:, c * 1024 + n * 512: c * 1024 + (n + 1) * 512],
                    op=ALU.add), inc="SV")
                pmm_free[cn] = p.tok("SV")
            tok_z = p.tok("SV")

            tok_rstd4 = ln_stats(xr1[:], "fln", [tok_z])
            tok_y2 = ln_apply(xr1[:], "fln", ybuf[:], tok_rstd4, waits=[tok_z])
            site_guard["fln"] = tok_y2
            tok_y_dma = p.op("g", lambda e, t=tile: e.dma_start(
                out=y_out[t * TPT:(t + 1) * TPT].rearrange("(c p) dd -> p c dd", p=128),
                in_=ybuf[:].rearrange("p (c dd) -> p c dd", dd=1024)),
                waits=[tok_y2], inc="SO")


        emit_front(0, None)
        emit_front_b(0)
        for tile in range(nm):
            emit_back1(tile)
            if tile + 1 < nm:
                emit_front(tile + 1, None)
            emit_back1q(tile)
            if tile + 1 < nm:
                emit_front_b(tile + 1)
            emit_back2(tile)
            front_toks.pop(tile)
        p.wait("g", p.tok("SO"))
        p.wait("d", p.tok("SW"))

        with nc.Block() as block:
            @block.sync
            def _(sync):
                for f in p.q["d"]:
                    f(sync)

            @block.tensor
            def _(t):
                for f in p.q["t"]:
                    f(t)

            @block.vector
            def _(v):
                for f in p.q["v"]:
                    f(v)

            @block.scalar
            def _(s):
                for f in p.q["s"]:
                    f(s)

            @block.gpsimd
            def _(g):
                for f in p.q["g"]:
                    f(g)

    return nc, dbg_outs


def make_inputs(x_full, weights, core):
    import ml_dtypes
    b, half = core // 2, core % 2
    d = {}
    d["x_main"] = np.ascontiguousarray(x_full[b, half * 4096:(half + 1) * 4096])
    d["x_pref"] = np.ascontiguousarray(x_full[b, 0:4096])
    d["maskcol"] = np.full((128, 1), float(half), np.float32)
    d["wq"] = weights["L0_attn_wq"]; d["wk"] = weights["L0_attn_wk"]
    d["wv"] = weights["L0_attn_wv"]; d["wo"] = weights["L0_attn_wo"]
    d["proj"] = weights["L0_attn_proj"]
    d["aw1"] = weights["L0_attn_w1"]; d["aw2"] = weights["L0_attn_w2"]
    d["fw1t"] = np.ascontiguousarray(np.asarray(weights["L0_ffn_w1"]).T)
    d["fw2t"] = np.ascontiguousarray(np.asarray(weights["L0_ffn_w2"]).T)
    d["g1c"] = np.asarray(weights["L0_attn_g1"]).reshape(8, 128).T
    d["g2c"] = np.asarray(weights["L0_attn_g2"]).reshape(8, 128).T
    d["b1c"] = np.asarray(weights["L0_attn_beta1"]).reshape(8, 128).T
    d["b2c"] = np.asarray(weights["L0_attn_beta2"]).reshape(8, 128).T
    d["ab1r"] = np.asarray(weights["L0_attn_b1"]).reshape(1, -1)
    d["ab2r"] = np.asarray(weights["L0_attn_b2"]).reshape(1, -1)
    d["fb1r"] = np.asarray(weights["L0_ffn_b1"]).reshape(1, -1)
    d["fb2r"] = np.asarray(weights["L0_ffn_b2"]).reshape(1, -1)
    d["ident_bf"] = np.eye(128, dtype=ml_dtypes.bfloat16)
    d["ident_f32"] = np.eye(128, dtype=np.float32)
    d["onescol_bf"] = np.ones((1, 128), ml_dtypes.bfloat16)
    d["onescol_f32"] = np.ones((1, 128), np.float32)
    d["onesrow_bf"] = np.ones((1, 512), ml_dtypes.bfloat16)
    d["onestok_bf"] = np.ones((128, 1), ml_dtypes.bfloat16)
    mk = np.triu(np.ones((128, 128), np.float32))
    d["maskT4"] = np.ascontiguousarray(np.tile(mk, (1, 4)))
    out = {}
    for k in d:
        a = np.asarray(d[k])
        if a.dtype == np.float64:
            a = a.astype(np.float32)
        out[k] = np.ascontiguousarray(a)
    return out


def kernel(**inputs):
    x = np.asarray(inputs["L0_attn_x"])
    nc, _ = build(nm=16, nprf=16)
    in_maps = [make_inputs(x, inputs, core) for core in range(8)]
    res = run_bass_kernel_spmd(nc, in_maps, core_ids=list(range(8)))
    out = np.zeros((4, 8192, 1024), np.float32)
    for core in range(8):
        b, half = core // 2, core % 2
        out[b, half * 4096:(half + 1) * 4096] = res.results[core]["y_out"]
    return out


# version 11
# speedup vs baseline: 1.3172x; 1.1125x over previous
import numpy as np
import concourse.bass as bass
import concourse.mybir as mybir
from concourse.bass_utils import run_bass_kernel_spmd
from contextlib import ExitStack

F32 = mybir.dt.float32
BF16 = mybir.dt.bfloat16
I8 = mybir.dt.int8
AF = mybir.ActivationFunctionType
ALU = mybir.AluOpType
AX = mybir.AxisListType

D = 1024
H = 16
DH = 64
FF = 4096
C = 128
TPT = 256
NCHUNK = 2

E_SET = 6
G_SET = 10

ENGS = ("d", "t", "v", "s", "g")


class P:
    def __init__(self, nc):
        self.nc = nc
        self.q = {e: [] for e in ENGS}
        self.cnt = {}
        self.sems = {}
        self.lastwait = {e: {} for e in ENGS}
        self.act_set = None

    def new_sem(self, name):
        self.cnt[name] = 0

    def bump(self, sem, n):
        self.cnt[sem] += n
        return (sem, self.cnt[sem])

    def emit(self, eng, fn):
        self.q[eng].append(fn)

    def wait(self, eng, tok):
        if tok is None:
            return
        sem, val = tok
        if self.lastwait[eng].get(sem, 0) >= val:
            return
        self.lastwait[eng][sem] = val

        def f(e, sem=sem, val=val):
            e.wait_ge(self.sems[sem], val)
        self.emit(eng, f)

    def op(self, eng, fn, waits=(), inc=None, n=None):
        for w in waits:
            self.wait(eng, w)
        tok = None
        if inc:
            amt = n if n is not None else (16 if inc in ("SW", "SO") else 1)
            tok = self.bump(inc, amt)

        def g(e, fn=fn, tok=tok):
            i = fn(e)
            if tok is not None:
                amt = 16 if tok[0] in ("SW", "SO") else 1
                i.then_inc(self.sems[tok[0]], amt)
        self.emit(eng, g)
        return tok

    def tok(self, sem):
        return (sem, self.cnt[sem])

    def table(self, set_id):
        if self.act_set == set_id:
            return
        self.act_set = set_id

        def f(e, set_id=set_id):
            e.add_instruction(mybir.InstLoadActFuncSet(
                name=self.nc.get_next_instruction_name(),
                act_func_set_id=set_id, ins=[], outs=[]))
        self.emit("s", f)


def build(nm=16, nprf=16, dbg=None):
    dbg = dbg or set()
    nc = bass.Bass("TRN2", target_bir_lowering=False)

    di = lambda name, shape, dt=F32: nc.dram_tensor(name, shape, dt, kind="ExternalInput").ap()
    do = lambda name, shape, dt=F32: nc.dram_tensor(name, shape, dt, kind="ExternalOutput").ap()
    dint = lambda name, shape, dt=BF16: nc.dram_tensor(name, shape, dt, kind="Internal").ap()

    x_main = di("x_main", [4096, D])
    x_pref = di("x_pref", [4096, D])
    maskcol = di("maskcol", [128, 1])
    wq = di("wq", [D, D]); wk = di("wk", [D, D]); wv = di("wv", [D, D]); wo = di("wo", [D, D])
    proj = di("proj", [DH, DH])
    aw1 = di("aw1", [D, FF]); aw2 = di("aw2", [FF, D])
    fw1t = di("fw1t", [D, FF]); fw2t = di("fw2t", [FF, D])
    g1c = di("g1c", [128, 8]); g2c = di("g2c", [128, 8])
    b1c = di("b1c", [128, 8]); b2c = di("b2c", [128, 8])
    ab1r = di("ab1r", [1, FF]); ab2r = di("ab2r", [1, D])
    fb1r = di("fb1r", [1, FF]); fb2r = di("fb2r", [1, D])
    ident_bf = di("ident_bf", [128, 128], BF16)
    ident_f32 = di("ident_f32", [128, 128], F32)
    onescol_bf = di("onescol_bf", [1, 128], BF16)
    onescol_f32 = di("onescol_f32", [1, 128], F32)
    onesrow_bf = di("onesrow_bf", [1, 512], BF16)
    onestok_bf = di("onestok_bf", [128, 1], BF16)
    maskT4 = di("maskT4", [128, 512])

    y_out = do("y_out", [4096, D])

    scr = {}
    for name, nq in [("q", 1), ("k", 1), ("v", 1), ("o", 1),
                     ("aw1", 4), ("aw2", 4), ("f1", 4), ("f2", 4)]:
        scr[name] = dint("scr_" + name, [nq, 8, 128, 1024], BF16)

    dbg_outs = {}

    with ExitStack() as st_:
        E = st_.enter_context
        wb = [E(nc.sbuf_tensor(f"wb{i}", [128, 8192], BF16)) for i in range(2)]
        xin = [E(nc.sbuf_tensor(f"xin{i}", [128, 2048], F32)) for i in range(2)]
        xr0 = E(nc.sbuf_tensor("xr0", [128, 2048], F32))
        xr1 = E(nc.sbuf_tensor("xr1", [128, 2048], F32))
        ybuf = E(nc.sbuf_tensor("ybuf", [128, 2048], F32))
        hf = E(nc.sbuf_tensor("hf", [128, 2048], F32))
        hT = E(nc.sbuf_tensor("hT", [128, 2048], BF16))
        QRT = E(nc.sbuf_tensor("QRT", [64, 4096], BF16))
        KRT = E(nc.sbuf_tensor("KRT", [64, 4096], BF16))
        QT = QRT
        KT = KRT
        Kc = E(nc.sbuf_tensor("Kc", [128, 2048], BF16))
        Vc = E(nc.sbuf_tensor("Vc", [128, 2080], BF16))
        Ssb = E(nc.sbuf_tensor("Ssb", [128, 2048], BF16))
        attnT = E(nc.sbuf_tensor("attnT", [128, 2048], BF16))
        attn_tm = E(nc.sbuf_tensor("attn_tm", [128, 2048], BF16))
        stS = E(nc.sbuf_tensor("stS", [64, 1024], BF16))
        ksS = E(nc.sbuf_tensor("ksS", [64, 16], BF16))
        garena = E(nc.sbuf_tensor("garena", [128, 8192], BF16))
        ints2 = E(nc.sbuf_tensor("ints2", [128, 8192], BF16))
        xqb = attn_tm
        i8g = [E(nc.sbuf_tensor(f"i8g{i}", [128, 256], I8)) for i in range(2)]
        qk2 = E(nc.sbuf_tensor("qk2", [128, 2048], F32))
        gmb = [E(nc.sbuf_tensor(f"gmb{i}", [128, 256], BF16)) for i in range(2)]
        e1 = E(nc.sbuf_tensor("e1", [128, 512], F32))
        sst = E(nc.sbuf_tensor("sst", [128, 32], F32))
        sqd = E(nc.sbuf_tensor("sqd", [128, 1024], BF16))
        stats = E(nc.sbuf_tensor("stats", [128, 64], F32))
        dens = E(nc.sbuf_tensor("dens", [128, 16], F32))
        recips = E(nc.sbuf_tensor("recips", [128, 16], F32))
        mrow = E(nc.sbuf_tensor("mrow", [1, 256], BF16))
        rowf = E(nc.sbuf_tensor("rowf", [1, 256], F32))
        Bsb = E(nc.sbuf_tensor("Bsb", [128, 256], BF16))
        Id = E(nc.sbuf_tensor("Id", [128, 128], BF16))
        Idf = E(nc.sbuf_tensor("Idf", [128, 128], F32))
        McT = E(nc.sbuf_tensor("McT", [128, 512], F32))
        Onc = E(nc.sbuf_tensor("Onc", [1, 128], BF16))
        Oncf = E(nc.sbuf_tensor("Oncf", [1, 128], F32))
        Onr = E(nc.sbuf_tensor("Onr", [1, 512], BF16))
        Ontk = E(nc.sbuf_tensor("Ontk", [128, 1], BF16))
        Mcol = E(nc.sbuf_tensor("Mcol", [128, 1], F32))
        projb = E(nc.sbuf_tensor("projb", [64, 64], BF16))
        projf = E(nc.sbuf_tensor("projf", [64, 64], F32))
        g1b = E(nc.sbuf_tensor("g1b", [128, 8], F32))
        g2b = E(nc.sbuf_tensor("g2b", [128, 8], F32))
        b1f = E(nc.sbuf_tensor("b1f", [128, 8], F32))
        b2f = E(nc.sbuf_tensor("b2f", [128, 8], F32))
        b1b = E(nc.sbuf_tensor("b1b", [128, 8], BF16))
        b2b = E(nc.sbuf_tensor("b2b", [128, 8], BF16))
        bqr = E(nc.sbuf_tensor("bqr", [1, 1024], BF16))
        bkr = E(nc.sbuf_tensor("bkr", [1, 1024], BF16))
        bvr = E(nc.sbuf_tensor("bvr", [1, 1024], BF16))
        baw1 = E(nc.sbuf_tensor("baw1", [1, 4096], BF16))
        ab2b = E(nc.sbuf_tensor("ab2b", [1, 1024], BF16))
        fb1b = E(nc.sbuf_tensor("fb1b", [1, 4096], BF16))
        fb2tile = E(nc.sbuf_tensor("fb2tile", [128, 1024], F32))
        fb2b = E(nc.sbuf_tensor("fb2b", [1, 1024], BF16))
        rtmp = ybuf[0:1, 0:512]
        rtmp2 = ybuf[0:1, 1024:2048]
        swc = E(nc.sbuf_tensor("swc", [128, 4], F32))
        tiny = E(nc.sbuf_tensor("tiny", [1, 8], F32))
        tiny2 = E(nc.sbuf_tensor("tiny2", [128, 1], F32))
        st0 = E(nc.psum_tensor("st0", [64, 512], F32))
        st1 = E(nc.psum_tensor("st1", [64, 512], F32))
        ksb = E(nc.psum_tensor("ksb", [64, 16], F32))
        pmm = [E(nc.psum_tensor(f"pmm{i}", [128, 512], F32)) for i in range(4)]
        pbf = E(nc.psum_tensor("pbf", [128, 1024], BF16))
        sem_names = ["SW", "SO", "ST", "SV", "SS", "SP"]
        sems = {nm_: E(nc.semaphore(nm_)) for nm_ in sem_names}

        p = P(nc)
        p.sems = sems
        for s_ in sem_names:
            p.new_sem(s_)

        gare_f32 = garena[:].bitcast(F32)
        qkraw = qk2[:]
        e3 = e1[64:128, :]
        i8t = Ssb[:].bitcast(I8)
        ptmp = gare_f32[:, 2048:4096]
        ptmp2 = gare_f32[:, 0:1024]
        i8p = ints2[:].bitcast(I8)

        SC = {"ln1": 0, "ln2": 8, "oln": 16, "fln": 24}

        pmm_free = [None] * 4
        pbf_free = [None]
        wb_free = [None, None]
        wb_ready = [None, None]

        def dma_in(dst, src, queue="d", waits=()):
            return p.op(queue, lambda e, d=dst, s=src: e.dma_start(out=d, in_=s),
                        waits=waits, inc="SW" if queue == "d" else "SO")

        def dump(name, src_ap, shape, dt=F32, waits=()):
            if name not in dbg:
                return
            ap = do("dbg_" + name, shape, dt)
            dbg_outs[name] = ap
            tok = p.op("g", lambda e, a=ap, s=src_ap: e.dma_start(out=a, in_=s),
                       waits=waits, inc="SO")
            for eng in ENGS:
                p.wait(eng, tok)

        for dst, src in [(Id[:], ident_bf), (Idf[:], ident_f32), (Onc[:], onescol_bf),
                         (Oncf[:], onescol_f32), (Onr[:], onesrow_bf), (Ontk[:], onestok_bf),
                         (McT[:], maskT4), (Mcol[:], maskcol),
                         (g1b[:], g1c),
                         (g2b[:], g2c),
                         (projf[:], proj),
                         (b1f[:], b1c),
                         (b2f[:], b2c)]:
            dma_in(dst, src)
        tok_consts = p.tok("SW")

        p.op("v", lambda e: e.tensor_copy(out=projb[:], in_=projf[:]),
             waits=[tok_consts], inc="SV")
        tok_projb = p.tok("SV")
        p.op("v", lambda e: e.tensor_copy(out=b1b[:], in_=b1f[:]), inc="SV")
        p.op("v", lambda e: e.tensor_copy(out=b2b[:], in_=b2f[:]), inc="SV")
        for dst_b, src_r, n512 in [(ab2b, ab2r, 2), (fb1b, fb1r, 8), (fb2b, fb2r, 2)]:
            for i in range(n512):
                tok = dma_in(rtmp, src_r[0:1, i * 512:(i + 1) * 512], waits=[p.tok("SV")])
                p.op("v", lambda e, d=dst_b, i=i: e.tensor_copy(
                    out=d[0:1, i * 512:(i + 1) * 512], in_=rtmp), waits=[tok], inc="SV")
        tok_smallcasts = p.tok("SV")

        p.op("g", lambda e: e.memset(Vc[:], 1.0), inc="SP")
        p.op("g", lambda e: e.memset(stS[:], 0.0), inc="SP")
        p.op("g", lambda e: e.memset(ksS[:], 0.0), inc="SP")
        tok_memset = p.tok("SP")

        p.table(E_SET)

        tok_bias = {}

        stage_guard = [None, None]

        def prep_linear(w_ap, scr_name, gcol, bias_dst, beta_b, extra_row):
            nq = scr[scr_name].shape[0]
            kmajor = scr_name in ("aw2",)
            for q in range(nq):
                cast_tok = [None] * 8
                for k in range(8):
                    if kmajor:
                        rows, cols = (q * 8 + k) * 128, 0
                    else:
                        rows, cols = k * 128, q * 1024
                    hh = k % 2
                    w = [cast_tok[k - 2]] if k >= 2 else                         ([stage_guard[hh]] if stage_guard[hh] else [])
                    pv = ptmp[:, hh * 1024:hh * 1024 + 1024]
                    tok = dma_in(pv, w_ap[rows:rows + 128, cols:cols + 1024], waits=w)
                    if bias_dst is not None:
                        p.op("v", lambda e, k=k, pv=pv: e.tensor_copy(
                            out=wb[1][:, k * 1024:(k + 1) * 1024], in_=pv),
                            waits=[tok] + ([wb_free[1]] if wb_free[1] else []), inc="SV")
                    if gcol is not None:
                        p.op("v", lambda e, k=k, g=gcol, pv=pv: e.tensor_scalar(
                            out=wb[0][:, k * 1024:(k + 1) * 1024], in0=pv,
                            scalar1=g[:, k:k + 1], scalar2=None, op0=ALU.mult),
                            waits=[tok, tok_consts] + ([wb_free[0]] if wb_free[0] else []),
                            inc="SV")
                    else:
                        p.op("v", lambda e, k=k, pv=pv: e.tensor_copy(
                            out=wb[0][:, k * 1024:(k + 1) * 1024], in_=pv),
                            waits=[tok] + ([wb_free[0]] if wb_free[0] else []), inc="SV")
                    cast_tok[k] = p.tok("SV")
                stage_guard[0] = cast_tok[6]
                stage_guard[1] = cast_tok[7]
                tok_stage = p.tok("SV")
                tok_bmm = None
                if bias_dst is not None:
                    for n in range(2):
                        for k in range(8):
                            p.op("t", lambda e, k=k, n=n, bb=beta_b: e.matmul(
                                out=pmm[3][0:1, 0:512],
                                lhsT=bb[:, k:k + 1],
                                rhs=wb[1][:, k * 1024 + n * 512: k * 1024 + (n + 1) * 512],
                                start=(k == 0), stop=(k == 7)),
                                waits=([tok_stage, tok_smallcasts] +
                                       ([pmm_free[3]] if pmm_free[3] else [])) if k == 0 else [],
                                inc=("ST" if k == 7 else None))
                        tok_bmm = p.tok("ST")
                        off = q * 1024 + n * 512
                        if extra_row is None:
                            p.op("v", lambda e, off=off, bd=bias_dst: e.tensor_copy(
                                out=bd[0:1, off:off + 512],
                                in_=pmm[3][0:1, 0:512]), waits=[tok_bmm], inc="SV")
                        else:
                            tok_er = dma_in(rtmp, extra_row[0:1, off:off + 512],
                                            waits=[p.tok("SV")])
                            p.op("v", lambda e, off=off, bd=bias_dst: e.tensor_tensor(
                                out=bd[0:1, off:off + 512],
                                in0=pmm[3][0:1, 0:512],
                                in1=rtmp, op=ALU.add),
                                waits=[tok_bmm, tok_er], inc="SV")
                        pmm_free[3] = p.tok("SV")
                    tok_bias[scr_name] = p.tok("SV")
                    wb_free[1] = tok_bmm
                w_ = [tok_stage] + ([tok_bmm] if tok_bmm else [])
                for k in range(8):
                    wb_free[0] = p.op("d", lambda e, q=q, sn=scr_name, k=k: e.dma_start(
                        out=scr[sn][q, k], in_=wb[0][:, k * 1024:(k + 1) * 1024]),
                        waits=(w_ if k == 0 else []), inc="SW")

        prep_linear(wq, "q", g1b, bqr, b1b, None)
        prep_linear(wk, "k", g1b, bkr, b1b, None)
        prep_linear(wv, "v", g1b, bvr, b1b, None)
        prep_linear(wo, "o", None, None, None, None)
        prep_linear(aw1, "aw1", g2b, baw1, b2b, ab1r)
        prep_linear(aw2, "aw2", None, None, None, None)

        for n in range(2):
            p.op("t", lambda e, n=n: e.matmul(
                out=pmm[3][:, 0:512], lhsT=Onc[:],
                rhs=fb2b[0:1, n * 512:(n + 1) * 512], start=True, stop=True),
                waits=[tok_smallcasts, tok_consts] + ([pmm_free[3]] if pmm_free[3] else []),
                inc="ST")
            tok_fb2 = p.tok("ST")
            p.op("v", lambda e, n=n: e.tensor_copy(
                out=fb2tile[:, n * 512:(n + 1) * 512], in_=pmm[3][:, 0:512]),
                waits=[tok_fb2], inc="SV")
            pmm_free[3] = p.tok("SV")

        def prep_ternary(w_ap, scr_name, sw_col_idx):
            abscol = stats[:, 32:64]
            nr, ncol = w_ap.shape[0] // 128, w_ap.shape[1] // 1024
            ci = 0
            abs_tok = [None] * 34
            for rr in range(nr):
                for cc in range(ncol):
                    buf = ptmp[:, (ci % 2) * 1024:(ci % 2) * 1024 + 1024]
                    w = [abs_tok[ci - 2]] if ci >= 2 else                         ([stage_guard[ci % 2]] if stage_guard[ci % 2] else [])
                    tok = dma_in(buf, w_ap[rr * 128:(rr + 1) * 128, cc * 1024:(cc + 1) * 1024],
                                 waits=w)
                    p.op("s", lambda e, ci=ci, buf=buf: e.activation(
                        out=sqd[:], in_=buf,
                        func=AF.Abs, accum_out=abscol[:, ci:ci + 1]),
                        waits=[tok], inc="SS")
                    abs_tok[ci] = p.tok("SS")
                    ci += 1
            tok_abs = p.tok("SS")
            stage_guard[0] = abs_tok[ci - 2]
            stage_guard[1] = abs_tok[ci - 1]
            p.op("v", lambda e: e.reduce_sum(out=tiny2[:, 0:1], in_=abscol, axis=AX.X),
                 waits=[tok_abs], inc="SV")
            tok_red = p.tok("SV")
            p.op("t", lambda e: e.transpose(out=pmm[3][0:1, 0:128].bitcast(F32),
                                            in_=tiny2[:, 0:1], identity=Idf[:]),
                 waits=([tok_red, tok_consts] + ([pmm_free[3]] if pmm_free[3] else [])),
                 inc="ST")
            tok_tr = p.tok("ST")
            p.op("v", lambda e: e.tensor_copy(out=rowf[0:1, 0:128], in_=pmm[3][0:1, 0:128]),
                 waits=[tok_tr], inc="SV")
            pmm_free[3] = p.tok("SV")
            p.op("v", lambda e: e.reduce_sum(out=tiny[0:1, 0:1], in_=rowf[0:1, 0:128],
                                             axis=AX.X), waits=[p.tok("SV")], inc="SV")
            p.op("v", lambda e: e.tensor_scalar(out=tiny[0:1, 1:2], in0=tiny[0:1, 0:1],
                                                scalar1=1.0 / (4096.0 * 1024.0), scalar2=1e-5,
                                                op0=ALU.mult, op1=ALU.max),
                 waits=[p.tok("SV")], inc="SV")
            p.op("v", lambda e: e.reciprocal(out=tiny[0:1, 2:3], in_=tiny[0:1, 1:2]),
                 waits=[p.tok("SV")], inc="SV")
            tok_sc = p.tok("SV")
            p.op("t", lambda e: e.matmul(out=pmm[3][:, 0:2].bitcast(F32), lhsT=Oncf[:],
                                         rhs=tiny[0:1, 1:3], start=True, stop=True),
                 waits=[tok_sc, pmm_free[3]], inc="ST")
            tok_bc = p.tok("ST")
            p.op("v", lambda e, i=sw_col_idx: e.tensor_copy(out=swc[:, i:i + 1],
                                                            in_=pmm[3][:, 0:1]),
                 waits=[tok_bc], inc="SV")
            p.op("v", lambda e, i=sw_col_idx: e.tensor_copy(out=swc[:, i + 2:i + 3],
                                                            in_=pmm[3][:, 1:2]), inc="SV")
            pmm_free[3] = p.tok("SV")
            tok_sw = p.tok("SV")
            kmajor = scr_name == "f2"
            t2_tok = [None] * 34
            gi = 0
            for q in range(4):
                for k in range(8):
                    if kmajor:
                        rows, cols = (q * 8 + k) * 128, 0
                    else:
                        rows, cols = k * 128, q * 1024
                    hh = gi % 2
                    pv = ptmp[:, hh * 1024:hh * 1024 + 1024]
                    w = [t2_tok[gi - 2]] if gi >= 2 else                         ([stage_guard[hh]] if stage_guard[hh] else [])
                    tok = dma_in(pv, w_ap[rows:rows + 128, cols:cols + 1024], waits=w)
                    p.op("v", lambda e, i=sw_col_idx, pv=pv: e.tensor_scalar(
                        out=i8p[:, 0:1024], in0=pv,
                        scalar1=swc[:, i + 2:i + 3], scalar2=None, op0=ALU.mult),
                        waits=[tok, tok_sw, p.tok("SV")], inc="SV")
                    t2_tok[gi] = p.tok("SV")
                    gi += 1
                    p.op("v", lambda e: e.tensor_copy(out=ptmp2[:], in_=i8p[:, 0:1024]),
                         waits=[p.tok("SV")], inc="SV")
                    p.op("v", lambda e, k=k: e.tensor_scalar(
                        out=wb[0][:, k * 1024:(k + 1) * 1024], in0=ptmp2[:],
                        scalar1=1.0, scalar2=-1.0, op0=ALU.min, op1=ALU.max),
                        waits=([wb_free[0]] if wb_free[0] else []) + [p.tok("SV")], inc="SV")
                tok_stage = p.tok("SV")
                stage_guard[0] = stage_guard[1] = tok_stage
                for k in range(8):
                    wb_free[0] = p.op("d", lambda e, q=q, sn=scr_name, k=k: e.dma_start(
                        out=scr[sn][q, k], in_=wb[0][:, k * 1024:(k + 1) * 1024]),
                        waits=([tok_stage] if k == 0 else []), inc="SW")

        prep_ternary(fw1t, "f1", 0)
        prep_ternary(fw2t, "f2", 1)

        dump("bqr", bqr[:], [1, 1024], BF16, waits=[p.tok("SV"), p.tok("ST")])
        dump("swc", swc[:], [128, 4], F32, waits=[p.tok("SV")])
        dump("scrf1", scr["f1"][0, 0], [128, 1024], BF16, waits=[p.tok("SW")])
        dump("scrq", scr["q"][0, 0], [128, 1024], BF16, waits=[p.tok("SW")])
        dump("scrqall", scr["q"][0].rearrange("k p f -> (k p) f"), [1024, 1024], BF16,
             waits=[p.tok("SW")])
        dump("scraw1", scr["aw1"][:].rearrange("q k p f -> (q k p) f"), [4096, 1024], BF16,
             waits=[p.tok("SW")])
        dump("baw1", baw1[:], [1, 4096], BF16, waits=[p.tok("SV"), p.tok("ST")])

        gslot = {"i": 0}

        def load_slot(scr_name, q):
            i = gslot["i"] % 2
            gslot["i"] += 1
            tok = None
            for k in range(8):
                tok = p.op("d", lambda e, sn=scr_name, q=q, i=i, k=k: e.dma_start(
                    out=wb[i][:, k * 1024:(k + 1) * 1024], in_=scr[sn][q, k]),
                    waits=([wb_free[i]] if (k == 0 and wb_free[i]) else []), inc="SW")
            wb_ready[i] = tok
            return i

        def done_slot(i, tok):
            wb_free[i] = tok

        site_guard = {"ln1": None, "ln2": None, "oln": None, "fln": None,
                      "q1": None, "q2": None}

        def ln_stats(xap, site, waits, eps=1e-5):
            b = SC[site]
            gw = [site_guard[site]] if site_guard[site] else []
            for c in range(NCHUNK):
                p.op("s", lambda e, c=c, b=b, xap=xap: e.activation(
                    out=sqd[:], in_=xap[:, c * 1024:(c + 1) * 1024], func=AF.Square,
                    accum_out=stats[:, b + 2 + c:b + 3 + c]),
                    waits=list(waits) + gw, inc="SS")
                p.op("v", lambda e, c=c, b=b, xap=xap: e.reduce_sum(
                    out=stats[:, b + c:b + 1 + c], in_=xap[:, c * 1024:(c + 1) * 1024],
                    axis=AX.X), waits=waits, inc="SV")
            tok_s = p.tok("SS")
            tok_v = p.tok("SV")
            p.op("v", lambda e, b=b: e.tensor_scalar(
                out=stats[:, b + 4:b + 6], in0=stats[:, b:b + 2],
                scalar1=1.0 / D, scalar2=None, op0=ALU.mult),
                waits=[tok_v, p.tok("SV")], inc="SV")
            p.op("v", lambda e, b=b: e.tensor_scalar(
                out=stats[:, b + 6:b + 8], in0=stats[:, b + 2:b + 4],
                scalar1=1.0 / D, scalar2=None, op0=ALU.mult),
                waits=[tok_s, p.tok("SV")], inc="SV")
            p.op("v", lambda e, b=b: e.tensor_tensor(
                out=sst[:, 0:2], in0=stats[:, b + 4:b + 6], in1=stats[:, b + 4:b + 6],
                op=ALU.mult), waits=[p.tok("SV")], inc="SV")
            p.op("v", lambda e, b=b: e.tensor_tensor(
                out=sst[:, 28:30], in0=stats[:, b + 6:b + 8], in1=sst[:, 0:2],
                op=ALU.subtract), waits=[p.tok("SV")], inc="SV")
            p.op("v", lambda e, b=b, eps=eps: e.tensor_scalar(
                out=stats[:, b + 6:b + 8], in0=sst[:, 28:30],
                scalar1=eps, scalar2=None, op0=ALU.add), waits=[p.tok("SV")], inc="SV")
            tok_var = p.tok("SV")
            p.table(E_SET)
            p.op("s", lambda e, b=b: e.activation(out=sst[:, 2:4], in_=stats[:, b + 6:b + 8],
                                                  func=AF.Ln),
                 waits=[tok_var], inc="SS")
            p.op("s", lambda e, b=b: e.activation(out=stats[:, b + 6:b + 8], in_=sst[:, 2:4],
                                                  func=AF.Exp, scale=-0.5),
                 waits=[p.tok("SS")], inc="SS")
            return p.tok("SS")

        def ln_apply(xap, site, outap, tok_rstd, waits=()):
            b = SC[site]
            for c in range(NCHUNK):
                p.op("v", lambda e, c=c, b=b, xap=xap, outap=outap: e.tensor_scalar(
                    out=outap[:, c * 1024:(c + 1) * 1024],
                    in0=xap[:, c * 1024:(c + 1) * 1024],
                    scalar1=stats[:, b + 4 + c:b + 5 + c],
                    scalar2=stats[:, b + 6 + c:b + 7 + c],
                    op0=ALU.subtract, op1=ALU.mult),
                    waits=list(waits) + [tok_rstd, p.tok("SV")], inc="SV")
            return p.tok("SV")

        def transpose_2k(src_f32, tok_src, dst_bf):
            for w in range(4):
                bank = pmm[w % 2]
                for i in range(4):
                    d = w * 4 + i
                    c, j = d % 2, d // 2
                    p.op("t", lambda e, c=c, j=j, i=i, bank=bank, src_f32=src_f32: e.transpose(
                        out=bank[:, i * 128:(i + 1) * 128],
                        in_=src_f32[:, c * 1024 + j * 128: c * 1024 + (j + 1) * 128],
                        identity=Idf[:]),
                        waits=([tok_src, tok_consts] +
                               ([pmm_free[w % 2]] if pmm_free[w % 2] else [])) if i == 0 else [],
                        inc=("ST" if i == 3 else None))
                tok_t = p.tok("ST")
                p.op("s", lambda e, w=w, bank=bank, dst_bf=dst_bf: e.copy(
                    out=dst_bf[:, w * 512:(w + 1) * 512], in_=bank[:]),
                    waits=[tok_t], inc="SS")
                pmm_free[w % 2] = p.tok("SS")
            return p.tok("SS")

        def proj_tm(wslot_i, tok_w, tok_hT, bias_row, tok_b, out_f32=None, out_vc=False,
                    evac_waits=()):
            last = None
            for c in range(NCHUNK):
                for n in range(2):
                    bi = (c * 2 + n) % 2
                    bank = pmm[bi]
                    p.op("t", lambda e, n=n, bank=bank, br=bias_row: e.matmul(
                        out=bank[:], lhsT=Onc[:], rhs=br[0:1, n * 512:(n + 1) * 512],
                        start=True, stop=False),
                        waits=[tok_w, tok_hT, tok_b, tok_consts] +
                              ([pmm_free[bi]] if pmm_free[bi] else []))
                    for k in range(8):
                        p.op("t", lambda e, c=c, n=n, k=k, i=wslot_i, bank=bank: e.matmul(
                            out=bank[:], lhsT=hT[:, (k * 2 + c) * 128:(k * 2 + c + 1) * 128],
                            rhs=wb[i][:, k * 1024 + n * 512: k * 1024 + (n + 1) * 512],
                            start=False, stop=(k == 7)), inc=("ST" if k == 7 else None))
                    tok_mm = p.tok("ST")
                    if out_vc:
                        vview = Vc[:].rearrange("p (cc hh s) -> p cc hh s", cc=2, hh=16)[:, c, n * 8:(n + 1) * 8, 0:64]
                        pview = bank[:].rearrange("p (hh s) -> p hh s", hh=8)
                        p.op("s", lambda e, vv=vview, pv=pview: e.copy(out=vv, in_=pv),
                             waits=[tok_mm, tok_memset], inc="SS")
                    else:
                        p.op("s", lambda e, c=c, n=n, bank=bank, of=out_f32: e.copy(
                            out=of[:, c * 1024 + n * 512: c * 1024 + (n + 1) * 512],
                            in_=bank[:]), waits=[tok_mm] + list(evac_waits), inc="SS")
                    pmm_free[bi] = p.tok("SS")
                    last = p.tok("SS")
            return last

        def headT(src_f32, tok_src, dst_bf64):
            for w in range(8):
                bank = pmm[w % 2]
                for i in range(4):
                    d = w * 4 + i
                    h, c = d // 2, d % 2
                    p.op("t", lambda e, h=h, c=c, i=i, bank=bank, src_f32=src_f32: e.transpose(
                        out=bank[0:64, i * 128:(i + 1) * 128],
                        in_=src_f32[:, c * 1024 + h * 64: c * 1024 + (h + 1) * 64],
                        identity=Idf[:]),
                        waits=([tok_src, tok_consts] +
                               ([pmm_free[w % 2]] if pmm_free[w % 2] else [])) if i == 0 else [],
                        inc=("ST" if i == 3 else None))
                tok_t = p.tok("ST")
                p.op("s", lambda e, w=w, bank=bank, dst=dst_bf64: e.copy(
                    out=dst[:, w * 512:(w + 1) * 512], in_=bank[0:64, :]),
                    waits=[tok_t], inc="SS")
                pmm_free[w % 2] = p.tok("SS")
            return p.tok("SS")

        def phi(src64, tok_src, dst64):
            p.table(E_SET)
            for w in range(8):
                bi = 2 + (w % 2)
                bank = pmm[bi]
                for i in range(2):
                    h = w * 2 + i
                    p.op("t", lambda e, h=h, i=i, bank=bank, src64=src64: e.matmul(
                        out=bank[0:64, i * 256:(i + 1) * 256], lhsT=projb[:],
                        rhs=src64[:, h * 256:(h + 1) * 256], start=True, stop=True),
                        waits=([tok_src, tok_projb] +
                               ([pmm_free[bi]] if pmm_free[bi] else [])) if i == 0 else [],
                        inc=("ST" if i == 1 else None))
                tok_mm = p.tok("ST")
                p.op("v", lambda e, bank=bank: e.tensor_scalar(
                    out=e1[0:64, :], in0=bank[0:64, :], scalar1=0.0, scalar2=None,
                    op0=ALU.min), waits=[tok_mm, p.tok("SV"), p.tok("SS")], inc="SV")
                tok_min = p.tok("SV")
                p.op("s", lambda e: e.activation(out=e3[:, :], in_=e1[0:64, :],
                                                 func=AF.Exp),
                     waits=[tok_min], inc="SS")
                tok_exp = p.tok("SS")
                p.op("v", lambda e, w=w, bank=bank, dst64=dst64: e.scalar_tensor_tensor(
                    out=dst64[:, w * 512:(w + 1) * 512], in0=bank[0:64, :], scalar=0.0,
                    in1=e3[:, :], op0=ALU.max, op1=ALU.add),
                    waits=[tok_exp], inc="SV")
                pmm_free[bi] = p.tok("SV")
            return p.tok("SV")

        def kc_transposes(tok_kt):
            for c in range(NCHUNK):
                for h in range(16):
                    p.op("t", lambda e, c=c, h=h: e.transpose(
                        out=pbf[:, h * 64:(h + 1) * 64],
                        in_=KT[:, h * 256 + c * 128: h * 256 + (c + 1) * 128],
                        identity=Id[0:64, 0:64]),
                        waits=([tok_kt, tok_consts] +
                               ([pbf_free[0]] if pbf_free[0] else [])) if h == 0 else [],
                        inc=("ST" if h == 15 else None))
                tok_t = p.tok("ST")
                p.op("s", lambda e, c=c: e.copy(
                    out=Kc[:, c * 1024:(c + 1) * 1024], in_=pbf[:, 0:1024]),
                    waits=[tok_t], inc="SS")
                pbf_free[0] = p.tok("SS")
            return p.tok("SS")

        state_started = {"st0": False, "st1": False, "ksb": False}
        state_ready = {"tok": None}
        stS_read = {"tok": None}

        def state_update(kc_src, tok_kc, tok_vc):
            for c in range(NCHUNK):
                emit_state_chunk(kc_src, c, tok_kc, tok_vc)

        def emit_state_chunk(kc_src, c, tok_kc, tok_vc):
            for h in range(16):
                bank = st0 if h < 8 else st1
                sname = "st0" if h < 8 else "st1"
                first = not state_started[sname]
                state_started[sname] = True
                w = []
                if h == 0:
                    w = [tok_kc, tok_vc, tok_memset]
                    if stS_read["tok"] is not None:
                        w.append(stS_read["tok"])
                p.op("t", lambda e, c=c, h=h, bank=bank, first=first, kc_src=kc_src: e.matmul(
                    out=bank[0:64, (h % 8) * 64:(h % 8 + 1) * 64],
                    lhsT=kc_src[:, (c * 16 + h) * 64:(c * 16 + h + 1) * 64],
                    rhs=Vc[:, c * 1040 + h * 65: c * 1040 + h * 65 + 64],
                    start=first, stop=True), waits=w)
                firstk = not state_started["ksb"]
                state_started["ksb"] = True
                p.op("t", lambda e, c=c, h=h, firstk=firstk, kc_src=kc_src: e.matmul(
                    out=ksb[0:64, h:h + 1],
                    lhsT=kc_src[:, (c * 16 + h) * 64:(c * 16 + h + 1) * 64],
                    rhs=Ontk[:], start=firstk, stop=True),
                    inc=("ST" if h == 15 else None))
            state_ready["tok"] = p.tok("ST")

        tok_x = [None, None]
        pref_xin_used = [None, None]
        slot_kv = [None, None]

        if nprf > 0:
            tok_x[0] = dma_in(xin[0][:].rearrange("p (c dd) -> p c dd", dd=1024),
                              x_pref[0:TPT].rearrange("(c p) dd -> p c dd", p=128))
            slot_kv[0] = load_slot("k", 0)
            slot_kv[1] = load_slot("v", 0)
            tok_wk = wb_ready[slot_kv[0]]
            tok_wv = wb_ready[slot_kv[1]]
            for tile in range(nprf):
                i = tile % 2
                tok_here = tok_x[i]
                if tile + 1 < nprf:
                    w = [pref_xin_used[(tile + 1) % 2]] if pref_xin_used[(tile + 1) % 2] else []
                    tok_x[(tile + 1) % 2] = p.op("d", lambda e, t2=tile + 1: e.dma_start(
                        out=xin[t2 % 2][:].rearrange("p (c dd) -> p c dd", dd=1024),
                        in_=x_pref[t2 * TPT:(t2 + 1) * TPT].rearrange("(c p) dd -> p c dd", p=128)),
                        waits=w, inc="SW")
                xa = xin[i][:]
                tok_rstd = ln_stats(xa, "ln1", [tok_here])
                tok_h = ln_apply(xa, "ln1", hf[:], tok_rstd, waits=[tok_here])
                pref_xin_used[i] = tok_h
                tok_hT = transpose_2k(hf[:], tok_h, hT[:])
                tok_kraw = proj_tm(slot_kv[0], tok_wk, tok_hT, bkr, tok_bias["k"],
                                   out_f32=qkraw)
                tok_v = proj_tm(slot_kv[1], tok_wv, tok_hT, bvr, tok_bias["v"], out_vc=True)
                tok_krt = headT(qkraw, tok_kraw, KRT[:])
                tok_kt = phi(KRT[:], tok_krt, KT[:])
                tok_kc = kc_transposes(tok_kt)
                p.op("v", lambda e: e.tensor_scalar(
                    out=attn_tm[:], in0=Kc[:], scalar1=Mcol[:, 0:1], scalar2=None,
                    op0=ALU.mult), waits=[tok_kc, tok_consts], inc="SV")
                tok_kcm = p.tok("SV")
                if tile == nprf - 1:
                    dump("kcm", attn_tm[:], [128, 2048], BF16, waits=[tok_kcm])
                site_guard["ln1"] = tok_kcm
                state_update(attn_tm[:], tok_kcm, tok_v)
            done_slot(slot_kv[0], state_ready["tok"])
            done_slot(slot_kv[1], state_ready["tok"])

        tok_x = [None, None]
        xin_used = [None, None]
        tok_bl2_prev = {"tok": None}
        w0 = [state_ready["tok"]] if state_ready["tok"] else []
        if nprf > 0:
            w0 += [pref_xin_used[0]] if pref_xin_used[0] else []
        tok_x[0] = p.op("d", lambda e: e.dma_start(
            out=xin[0][:].rearrange("p (c dd) -> p c dd", dd=1024),
            in_=x_main[0:TPT].rearrange("(c p) dd -> p c dd", p=128)),
            waits=w0, inc="SW")
        tok_y_dma = None

        front_toks = {}

        def emit_front(tile, garena_free):
            i = tile % 2
            tok_here = tok_x[i]
            xa = xin[i][:]
            tok_rstd = ln_stats(xa, "ln1", [tok_here])
            tok_h = ln_apply(xa, "ln1", hf[:], tok_rstd, waits=[tok_here])
            tok_hT = transpose_2k(hf[:], tok_h, hT[:])

            if tile == 0:
                dump("xin", xa, [128, 2048], F32, waits=[tok_here])
                dump("stats", stats[:], [128, 64], F32, waits=[tok_rstd, tok_h])
                dump("hf", hf[:], [128, 2048], F32, waits=[tok_h])
                dump("hT", hT[:], [128, 2048], BF16, waits=[tok_hT])
            gfw = [garena_free] if garena_free else []
            sq_ = load_slot("q", 0)
            tok_qraw = proj_tm(sq_, wb_ready[sq_], tok_hT, bqr, tok_bias["q"], out_f32=qkraw,
                               evac_waits=gfw)
            if tile == 0:
                dump("qraw", qkraw, [128, 2048], F32, waits=[tok_qraw])
            tok_qrt = headT(qkraw, tok_qraw, QRT[:])
            done_slot(sq_, tok_qrt)
            tok_qt = phi(QRT[:], tok_qrt, QT[:])
            sk_ = load_slot("k", 0)
            tok_kraw = proj_tm(sk_, wb_ready[sk_], tok_hT, bkr, tok_bias["k"], out_f32=qkraw,
                               evac_waits=gfw)
            tok_krt = headT(qkraw, tok_kraw, KRT[:])
            done_slot(sk_, tok_krt)
            tok_kt = phi(KRT[:], tok_krt, KT[:])
            sv_ = load_slot("v", 0)
            tok_v = proj_tm(sv_, wb_ready[sv_], tok_hT, bvr, tok_bias["v"], out_vc=True)
            done_slot(sv_, tok_v)
            site_guard["ln1"] = tok_h
            front_toks[tile] = dict(i=i, tok_here=tok_here, xa=xa,
                                    tok_qt=tok_qt, tok_kt=tok_kt,
                                    tok_v=tok_v)

        def emit_front_b(tile):
            F = front_toks[tile]
            tok_qt = F["tok_qt"]; tok_kt = F["tok_kt"]; tok_v = F["tok_v"]
            tok_kc = kc_transposes(tok_kt)

            if tile == 0:
                dump("QT", QT[:], [64, 4096], BF16, waits=[tok_qt])
                dump("KT", KT[:], [64, 4096], BF16, waits=[tok_kt])
                dump("Vc", Vc[:], [128, 2080], BF16, waits=[tok_v])
                dump("Kc", Kc[:], [128, 2048], BF16, waits=[tok_kc])

            F["tok_qt"] = tok_qt
            F["tok_kt"] = tok_kt
            F["tok_kc"] = tok_kc


        def emit_back1(tile):
            F = front_toks[tile]
            i = F["i"]; tok_here = F["tok_here"]; xa = F["xa"]
            tok_qt = F["tok_qt"]; tok_kt = F["tok_kt"]
            tok_kc = F["tok_kc"]; tok_v = F["tok_v"]
            if tile + 1 < nm:
                wprev = []
                if tile == 0 and nprf > 0 and pref_xin_used[1]:
                    wprev = [pref_xin_used[1]]
                elif xin_used[(tile + 1) % 2]:
                    wprev = [xin_used[(tile + 1) % 2]]
                tok_x[(tile + 1) % 2] = p.op("d", lambda e, t2=tile + 1: e.dma_start(
                    out=xin[t2 % 2][:].rearrange("p (c dd) -> p c dd", dd=1024),
                    in_=x_main[t2 * TPT:(t2 + 1) * TPT].rearrange("(c p) dd -> p c dd", p=128)),
                    waits=wprev, inc="SW")
            xa = xin[i][:]
            for c in range(NCHUNK):
                if state_ready["tok"] is not None:
                    p.op("v", lambda e: e.tensor_copy(out=stS[0:64, 0:512], in_=st0[:]),
                         waits=[state_ready["tok"], tok_memset], inc="SV")
                    p.op("v", lambda e: e.tensor_copy(out=stS[0:64, 512:1024], in_=st1[:]),
                         inc="SV")
                    p.op("v", lambda e: e.tensor_copy(out=ksS[:], in_=ksb[:]), inc="SV")
                    stS_read["tok"] = p.tok("SV")
                tok_st_copy = p.tok("SV")
                if tile == 0 and c == 0:
                    dump("stS0", stS[:], [64, 1024], BF16, waits=[tok_st_copy])
                    dump("ksS0", ksS[:], [64, 16], BF16, waits=[tok_st_copy])
                    dump("mcol", Mcol[:], [128, 1], F32, waits=[tok_consts])
                tok_masks = [None] * 4
                for grp in range(4):
                    sbank = pmm[grp % 2]
                    for ii in range(4):
                        h = grp * 4 + ii
                        p.op("t", lambda e, h=h, ii=ii, c=c, sbank=sbank: e.matmul(
                            out=sbank[:, ii * 128:(ii + 1) * 128],
                            lhsT=KT[0:64, h * 256 + c * 128: h * 256 + (c + 1) * 128],
                            rhs=QT[0:64, h * 256 + c * 128: h * 256 + (c + 1) * 128],
                            start=True, stop=True),
                            waits=([tok_qt, tok_kt] +
                                   ([pmm_free[grp % 2]] if pmm_free[grp % 2] else [])) if ii == 0 else [],
                            inc=("ST" if ii == 3 else None))
                    tok_sc = p.tok("ST")
                    p.op("v", lambda e, grp=grp, sbank=sbank: e.tensor_tensor(
                        out=Ssb[:, grp * 512:(grp + 1) * 512], in0=sbank[:], in1=McT[:],
                        op=ALU.mult), waits=[tok_sc, tok_consts], inc="SV")
                    tok_masks[grp] = p.tok("SV")
                    pmm_free[grp % 2] = tok_masks[grp]
                for grp in range(4):
                    nbank = pmm[2 + grp % 2]
                    tok_mask = tok_masks[grp]
                    for ii in range(4):
                        h = grp * 4 + ii
                        p.op("t", lambda e, h=h, ii=ii, c=c, nbank=nbank: e.matmul(
                            out=nbank[:, ii * 128:ii * 128 + 65],
                            lhsT=Ssb[:, h * 128:(h + 1) * 128],
                            rhs=Vc[:, c * 1040 + h * 65: c * 1040 + (h + 1) * 65],
                            start=True, stop=False),
                            waits=([tok_mask, tok_v, tok_st_copy, tok_memset] +
                                   ([pmm_free[2 + grp % 2]] if pmm_free[2 + grp % 2] else [])) if ii == 0 else [])
                        p.op("t", lambda e, h=h, ii=ii, c=c, nbank=nbank: e.matmul(
                            out=nbank[:, ii * 128:ii * 128 + 64],
                            lhsT=QT[0:64, h * 256 + c * 128: h * 256 + (c + 1) * 128],
                            rhs=stS[0:64, h * 64:(h + 1) * 64], start=False, stop=False))
                        p.op("t", lambda e, h=h, ii=ii, c=c, nbank=nbank: e.matmul(
                            out=nbank[:, ii * 128 + 64:ii * 128 + 65],
                            lhsT=QT[0:64, h * 256 + c * 128: h * 256 + (c + 1) * 128],
                            rhs=ksS[0:64, h:h + 1], start=False, stop=True),
                            inc=("ST" if ii == 3 else None))
                    tok_n = p.tok("ST")
                    dview = nbank[:].rearrange("p (hh s) -> p hh s", hh=4)[:, :, 64:65]
                    p.op("v", lambda e, grp=grp, dv=dview: e.tensor_scalar(
                        out=dens[:, grp * 4:(grp + 1) * 4], in0=dv,
                        scalar1=1e-6, scalar2=None, op0=ALU.add),
                        waits=[tok_n, p.tok("SV")], inc="SV")
                    p.op("v", lambda e, grp=grp: e.reciprocal(
                        out=recips[:, grp * 4:(grp + 1) * 4],
                        in_=dens[:, grp * 4:(grp + 1) * 4]),
                        waits=[p.tok("SV")], inc="SV")
                    nview = nbank[:].rearrange("p (hh s) -> p hh s", hh=4)[:, :, 0:64]
                    aview = attn_tm[:, c * 1024 + grp * 256: c * 1024 + (grp + 1) * 256]\
                        .rearrange("p (hh s) -> p hh s", hh=4)
                    rview = recips[:, grp * 4:(grp + 1) * 4]\
                        .rearrange("p (hh one) -> p hh one", one=1).to_broadcast([128, 4, 64])
                    p.op("v", lambda e, nv=nview, av=aview, rv=rview: e.tensor_tensor(
                        out=av, in0=nv, in1=rv, op=ALU.mult),
                        waits=[p.tok("SV")], inc="SV")
                    pmm_free[2 + grp % 2] = p.tok("SV")
                tok_div = p.tok("SV")
                emit_state_chunk(Kc[:], c, tok_kc, tok_v)
                for j in range(8):
                    p.op("t", lambda e, c=c, j=j: e.transpose(
                        out=pbf[:, j * 128:(j + 1) * 128],
                        in_=attn_tm[:, c * 1024 + j * 128: c * 1024 + (j + 1) * 128],
                        identity=Id[:]),
                        waits=([tok_div] + ([pbf_free[0]] if pbf_free[0] else [])) if j == 0 else [],
                        inc=("ST" if j == 7 else None))
                tok_at = p.tok("ST")
                p.op("s", lambda e, c=c: e.copy(
                    out=attnT[:, c * 1024:(c + 1) * 1024], in_=pbf[:]),
                    waits=[tok_at], inc="SS")
                pbf_free[0] = p.tok("SS")
            tok_attnT = p.tok("SS")

            if tile == 0:
                dump("attn_tm", attn_tm[:], [128, 2048], BF16, waits=[tok_attnT])

            so_ = load_slot("o", 0)
            for c in range(NCHUNK):
                for n in range(2):
                    bi = (c * 2 + n) % 2
                    bank = pmm[bi]
                    for k in range(8):
                        p.op("t", lambda e, c=c, n=n, k=k, i=so_, bank=bank: e.matmul(
                            out=bank[:],
                            lhsT=attnT[:, c * 1024 + k * 128: c * 1024 + (k + 1) * 128],
                            rhs=wb[i][:, k * 1024 + n * 512: k * 1024 + (n + 1) * 512],
                            start=(k == 0), stop=(k == 7)),
                            waits=([tok_attnT, wb_ready[so_]] +
                                   ([pmm_free[bi]] if pmm_free[bi] else [])) if k == 0 else [],
                            inc=("ST" if k == 7 else None))
                    tok_mm = p.tok("ST")
                    p.op("v", lambda e, c=c, n=n, bank=bank, xa=xa: e.tensor_tensor(
                        out=xr0[:, c * 1024 + n * 512: c * 1024 + (n + 1) * 512],
                        in0=bank[:], in1=xa[:, c * 1024 + n * 512: c * 1024 + (n + 1) * 512],
                        op=ALU.add), waits=[tok_mm, tok_here], inc="SV")
                    pmm_free[bi] = p.tok("SV")
            tok_x2 = p.tok("SV")
            xin_used[i] = tok_x2
            done_slot(so_, tok_x2)

            if tile == 0:
                dump("x2", xr0[:], [128, 2048], F32, waits=[tok_x2])

            tok_rstd2 = ln_stats(xr0[:], "ln2", [tok_x2])
            tok_h2 = ln_apply(xr0[:], "ln2", hf[:], tok_rstd2, waits=[tok_x2])
            if tile == 0:
                dump("h2f", hf[:], [128, 2048], F32, waits=[tok_h2])
            tok_h2T = transpose_2k(hf[:], tok_h2, hT[:])
            if tile == 0:
                dump("h2T", hT[:], [128, 2048], BF16, waits=[tok_h2T])

            p.table(G_SET)
            for qf in range(4):
                sa = load_slot("aw1", qf)
                for m in range(8):
                    fc = qf * 8 + m
                    bi = m % 2
                    bank = pmm[bi]
                    p.op("t", lambda e, fc=fc, bank=bank: e.matmul(
                        out=bank[:, 0:256], lhsT=baw1[0:1, fc * 128:(fc + 1) * 128],
                        rhs=Onr[0:1, 0:256], start=True, stop=False),
                        waits=[wb_ready[sa], tok_h2T, tok_bias["aw1"], tok_consts] +
                              ([pmm_free[bi]] if pmm_free[bi] else []))
                    for k in range(8):
                        p.op("t", lambda e, m=m, k=k, i=sa, bank=bank: e.matmul(
                            out=bank[:, 0:256],
                            lhsT=wb[i][:, k * 1024 + m * 128: k * 1024 + (m + 1) * 128],
                            rhs=hT[:, k * 256:(k + 1) * 256], start=False, stop=(k == 7)),
                            inc=("ST" if k == 7 else None))
                    tok_mm = p.tok("ST")
                    p.op("s", lambda e, fc=fc, bank=bank: e.activation(
                        out=garena[:, fc * 256:(fc + 1) * 256], in_=bank[:, 0:256],
                        func=AF.Gelu), waits=[tok_mm], inc="SS")
                    pmm_free[bi] = p.tok("SS")
                done_slot(sa, p.tok("SS"))
            tok_g = p.tok("SS")

            if tile == 0:
                dump("gattn", garena[:], [128, 8192], BF16, waits=[tok_g])

            sa2 = [None] * 4
            for kq in range(4):
                sa2[kq] = load_slot("aw2", kq)
                for cn in range(4):
                    c, n = cn // 2, cn % 2
                    bank = pmm[cn]
                    if kq == 0:
                        p.op("t", lambda e, n=n, bank=bank: e.matmul(
                            out=bank[:], lhsT=Onc[:], rhs=ab2b[0:1, n * 512:(n + 1) * 512],
                            start=True, stop=False),
                            waits=[wb_ready[sa2[0]], tok_g, tok_smallcasts, tok_consts] +
                                  ([pmm_free[cn]] if pmm_free[cn] else []))
                    for k8 in range(8):
                        fc = kq * 8 + k8
                        p.op("t", lambda e, fc=fc, c=c, n=n, k8=k8, i=sa2[kq], bank=bank: e.matmul(
                            out=bank[:],
                            lhsT=garena[:, fc * 256 + c * 128: fc * 256 + (c + 1) * 128],
                            rhs=wb[i][:, k8 * 1024 + n * 512: k8 * 1024 + (n + 1) * 512],
                            start=False, stop=(fc == 31)),
                            waits=([wb_ready[sa2[kq]]] if (k8 == 0 and cn == 0 and kq > 0) else []),
                            inc=("ST" if (cn == 3 and k8 == 7) else None))
                tok_kq = p.tok("ST")
                done_slot(sa2[kq], tok_kq)
            tok_mm2_all = p.tok("ST")
            for cn in range(4):
                c, n = cn // 2, cn % 2
                bank = pmm[cn]
                p.op("v", lambda e, c=c, n=n, bank=bank: e.tensor_tensor(
                    out=e1[:, 0:512], in0=bank[:],
                    in1=xr0[:, c * 1024 + n * 512: c * 1024 + (n + 1) * 512],
                    op=ALU.add), waits=[tok_mm2_all, p.tok("SV")], inc="SV")
                p.op("v", lambda e, c=c, n=n, xa=xa: e.tensor_tensor(
                    out=xr1[:, c * 1024 + n * 512: c * 1024 + (n + 1) * 512],
                    in0=e1[:, 0:512],
                    in1=xa[:, c * 1024 + n * 512: c * 1024 + (n + 1) * 512],
                    op=ALU.add), waits=[p.tok("SV")], inc="SV")
                pmm_free[cn] = p.tok("SV")
            tok_x3 = p.tok("SV")
            xin_used[i] = tok_x3

            tok_rstd3 = ln_stats(xr1[:], "oln", [tok_x3])
            tok_y1 = ln_apply(xr1[:], "oln", xr0[:], tok_rstd3, waits=[tok_x3])
            for c in range(NCHUNK):
                p.op("v", lambda e, c=c: e.reduce_max(
                    out=stats[:, 32 + c:33 + c], in_=xr0[:, c * 1024:(c + 1) * 1024],
                    axis=AX.X, apply_absolute_value=True), waits=[tok_y1], inc="SV")
            tok_maxy = p.tok("SV")

            if tile == 0:
                dump("y1", xr0[:], [128, 2048], F32, waits=[tok_y1])

            b3 = SC["oln"]
            p.op("v", lambda e, b3=b3: e.tensor_tensor(
                out=sst[:, 4:6], in0=stats[:, b3 + 6:b3 + 8], in1=stats[:, b3 + 6:b3 + 8],
                op=ALU.mult), waits=[tok_y1, p.tok("SV")], inc="SV")
            p.op("v", lambda e: e.tensor_scalar(
                out=stats[:, 36:38], in0=sst[:, 4:6], scalar1=-1e-5, scalar2=1.0 + 1e-5,
                op0=ALU.mult, op1=ALU.add), waits=[p.tok("SV")], inc="SV")
            tok_msq = p.tok("SV")
            p.table(E_SET)
            p.op("s", lambda e: e.activation(out=sst[:, 6:8], in_=stats[:, 36:38],
                                             func=AF.Ln),
                 waits=[tok_msq] + ([site_guard["q1"]] if site_guard["q1"] else []),
                 inc="SS")
            p.op("s", lambda e: e.activation(out=stats[:, 38:40], in_=sst[:, 6:8],
                                             func=AF.Exp, scale=-0.5),
                 waits=[p.tok("SS")], inc="SS")
            tok_rms2 = p.tok("SS")
            p.op("v", lambda e: e.tensor_tensor(out=sst[:, 8:10], in0=stats[:, 38:40],
                                                in1=stats[:, 32:34], op=ALU.mult),
                 waits=[tok_rms2, tok_maxy, p.tok("SV")], inc="SV")
            p.op("v", lambda e: e.tensor_scalar(out=sst[:, 30:32], in0=sst[:, 8:10],
                                                scalar1=1e-5, scalar2=None, op0=ALU.max),
                 waits=[p.tok("SV")], inc="SV")
            p.op("v", lambda e: e.reciprocal(out=sst[:, 10:12], in_=sst[:, 30:32]),
                 waits=[p.tok("SV")], inc="SV")
            p.op("v", lambda e: e.tensor_scalar(out=sst[:, 26:28], in0=sst[:, 10:12],
                                                scalar1=127.0, scalar2=None, op0=ALU.mult),
                 waits=[p.tok("SV")], inc="SV")
            p.op("v", lambda e: e.tensor_tensor(out=sst[:, 12:14], in0=stats[:, 38:40],
                                                in1=sst[:, 26:28], op=ALU.mult),
                 waits=[p.tok("SV")], inc="SV")
            p.op("v", lambda e: e.tensor_scalar(out=sst[:, 8:10], in0=sst[:, 30:32],
                                                scalar1=1.0 / 127.0, scalar2=None,
                                                op0=ALU.mult), waits=[p.tok("SV")], inc="SV")
            p.op("v", lambda e: e.tensor_scalar(out=sst[:, 14:16], in0=sst[:, 8:10],
                                                scalar1=swc[:, 0:1], scalar2=None,
                                                op0=ALU.mult), waits=[p.tok("SV")], inc="SV")
            tok_qs = p.tok("SV")
            for c in range(NCHUNK):
                p.op("v", lambda e, c=c: e.tensor_scalar(
                    out=i8t[:, c * 1024:(c + 1) * 1024], in0=xr0[:, c * 1024:(c + 1) * 1024],
                    scalar1=sst[:, 12 + c:13 + c], scalar2=None, op0=ALU.mult),
                    waits=[tok_qs, p.tok("SV")], inc="SV")
                p.op("v", lambda e, c=c: e.tensor_scalar(
                    out=xqb[:, c * 1024:(c + 1) * 1024], in0=i8t[:, c * 1024:(c + 1) * 1024],
                    scalar1=sst[:, 14 + c:15 + c], scalar2=None, op0=ALU.mult),
                    waits=[p.tok("SV")], inc="SV")
            tok_xq = p.tok("SV")
            if tile == 0:
                dump("xqb", xqb[:], [128, 2048], BF16, waits=[tok_xq])
            for c in range(NCHUNK):
                for j in range(8):
                    p.op("t", lambda e, c=c, j=j: e.transpose(
                        out=pbf[:, j * 128:(j + 1) * 128],
                        in_=xqb[:, c * 1024 + j * 128: c * 1024 + (j + 1) * 128],
                        identity=Id[:]),
                        waits=([tok_xq] + ([pbf_free[0]] if pbf_free[0] else [])) if j == 0 else [],
                        inc=("ST" if j == 7 else None))
                tok_tt = p.tok("ST")
                hview = hT[:].rearrange("p (j cc f) -> p j cc f", j=8, cc=2)[:, :, c, :]
                pview = pbf[:].rearrange("p (j f) -> p j f", j=8)
                p.op("s", lambda e, hv=hview, pv=pview: e.copy(out=hv, in_=pv),
                     waits=[tok_tt], inc="SS")
                pbf_free[0] = p.tok("SS")
            tok_xqT = p.tok("SS")

            p.table(G_SET)
            for qf in range(4):
                sf = load_slot("f1", qf)
                for m in range(8):
                    fc = qf * 8 + m
                    bi = m % 2
                    bank = pmm[bi]
                    p.op("t", lambda e, fc=fc, bank=bank: e.matmul(
                        out=bank[:, 0:256], lhsT=fb1b[0:1, fc * 128:(fc + 1) * 128],
                        rhs=Onr[0:1, 0:256], start=True, stop=False),
                        waits=[wb_ready[sf], tok_xqT, tok_smallcasts, tok_consts] +
                              ([pmm_free[bi]] if pmm_free[bi] else []))
                    for k in range(8):
                        p.op("t", lambda e, m=m, k=k, i=sf, bank=bank: e.matmul(
                            out=bank[:, 0:256],
                            lhsT=wb[i][:, k * 1024 + m * 128: k * 1024 + (m + 1) * 128],
                            rhs=hT[:, k * 256:(k + 1) * 256], start=False, stop=(k == 7)),
                            inc=("ST" if k == 7 else None))
                    tok_mm = p.tok("ST")
                    p.op("s", lambda e, fc=fc, bank=bank: e.activation(
                        out=garena[:, fc * 256:(fc + 1) * 256], in_=bank[:, 0:256],
                        func=AF.Gelu), waits=[tok_mm], inc="SS")
                    pmm_free[bi] = p.tok("SS")
                done_slot(sf, p.tok("SS"))
            tok_g2 = p.tok("SS")
            if tile == 0:
                dump("g2", garena[:], [128, 8192], BF16, waits=[tok_g2])

            F["tok_g2"] = tok_g2
            F["tok_qs"] = tok_qs
            F["tok_h2"] = tok_h2
            site_guard["ln2"] = tok_h2
            site_guard["oln"] = tok_qs
            site_guard["q1"] = tok_qs

        def emit_back1q(tile):
            F = front_toks[tile]
            tok_g2 = F["tok_g2"]
            sq_tok = [None] * 32
            for fc in range(32):
                p.op("v", lambda e, fc=fc: e.tensor_tensor(
                    out=ints2[:, fc * 256:(fc + 1) * 256],
                    in0=garena[:, fc * 256:(fc + 1) * 256],
                    in1=garena[:, fc * 256:(fc + 1) * 256], op=ALU.mult),
                    waits=(([tok_g2] + ([tok_bl2_prev["tok"]] if tok_bl2_prev["tok"] else []))
                           if fc == 0 else []),
                    inc="SV")
                sq_tok[fc] = p.tok("SV")
            tok_sqall = p.tok("SV")
            for fc in range(32):
                p.op("t", lambda e, fc=fc: e.matmul(
                    out=pmm[3][0:1, 0:256], lhsT=Ontk[:], rhs=ints2[:, fc * 256:(fc + 1) * 256],
                    start=(fc == 0), stop=(fc == 31)),
                    waits=[sq_tok[fc]] + (([tok_consts] +
                           ([pmm_free[3]] if pmm_free[3] else [])) if fc == 0 else []),
                    inc=("ST" if fc == 31 else None))
            for fc in range(1, 32):
                p.op("v", lambda e, fc=fc: e.tensor_tensor(
                    out=gmb[fc % 2][:],
                    in0=(ints2[:, 0:256] if fc == 1 else gmb[(fc + 1) % 2][:]),
                    in1=ints2[:, fc * 256:(fc + 1) * 256],
                    op=ALU.max), waits=[p.tok("SV")], inc="SV")
            tok_ssq = p.tok("ST")
            gm_fin = gmb[31 % 2]
            tok_gm = [p.tok("SV")]
            for c in range(NCHUNK):
                p.op("t", lambda e, c=c, gm_fin=gm_fin: e.transpose(
                    out=pbf[:, c * 128:(c + 1) * 128], in_=gm_fin[:, c * 128:(c + 1) * 128],
                    identity=Id[:]),
                    waits=(tok_gm + [tok_consts] + ([pbf_free[0]] if pbf_free[0] else [])) if c == 0 else [],
                    inc=("ST" if c == 1 else None))
            tok_gt = p.tok("ST")
            for c in range(NCHUNK):
                p.op("v", lambda e, c=c: e.reduce_max(
                    out=stats[:, 52 + c:53 + c], in_=pbf[:, c * 128:(c + 1) * 128],
                    axis=AX.X), waits=[tok_gt], inc="SV")
            pbf_free[0] = p.tok("SV")
            tok_gmaxsq = p.tok("SV")
            p.table(E_SET)
            p.op("s", lambda e: e.activation(out=sst[:, 26:28], in_=stats[:, 52:54],
                                             func=AF.Ln),
                 waits=[tok_gmaxsq] + ([site_guard["q2"]] if site_guard["q2"] else []),
                 inc="SS")
            p.op("s", lambda e: e.activation(out=stats[:, 34:36], in_=sst[:, 26:28],
                                             func=AF.Exp, scale=0.5),
                 waits=[p.tok("SS")], inc="SS")
            tok_gmax = p.tok("SS")
            p.op("v", lambda e: e.tensor_copy(out=rowf[0:1, 0:256], in_=pmm[3][0:1, 0:256]),
                 waits=[tok_ssq], inc="SV")
            pmm_free[3] = p.tok("SV")
            tok_row = p.tok("SV")
            for c in range(NCHUNK):
                p.op("t", lambda e, c=c: e.matmul(
                    out=pmm[3][:, c:c + 1].bitcast(F32), lhsT=rowf[0:1, c * 128:(c + 1) * 128],
                    rhs=Oncf[0:1, 0:1], start=True, stop=True),
                    waits=([tok_row, tok_consts] + ([pmm_free[3]] if pmm_free[3] else [])) if c == 0 else [],
                    inc=("ST" if c == 1 else None))
            tok_ssqT = p.tok("ST")
            p.op("v", lambda e: e.tensor_copy(out=stats[:, 44:46], in_=pmm[3][:, 0:2]),
                 waits=[tok_ssqT], inc="SV")
            pmm_free[3] = p.tok("SV")
            p.op("v", lambda e: e.tensor_scalar(out=stats[:, 42:44], in0=stats[:, 44:46],
                                                scalar1=1.0 / FF, scalar2=1e-5,
                                                op0=ALU.mult, op1=ALU.add),
                 waits=[p.tok("SV")], inc="SV")
            tok_ms2 = p.tok("SV")
            p.table(E_SET)
            p.op("s", lambda e: e.activation(out=sst[:, 16:18], in_=stats[:, 42:44],
                                             func=AF.Ln),
                 waits=[tok_ms2] + ([site_guard["q2"]] if site_guard["q2"] else []),
                 inc="SS")
            p.op("s", lambda e: e.activation(out=stats[:, 46:48], in_=sst[:, 16:18],
                                             func=AF.Exp, scale=-0.5),
                 waits=[p.tok("SS")], inc="SS")
            tok_rmsg = p.tok("SS")
            p.op("v", lambda e: e.tensor_tensor(out=sst[:, 18:20], in0=stats[:, 46:48],
                                                in1=stats[:, 34:36], op=ALU.mult),
                 waits=[tok_rmsg, tok_gmax, p.tok("SV")], inc="SV")
            p.op("v", lambda e: e.tensor_scalar(out=sst[:, 30:32], in0=sst[:, 18:20],
                                                scalar1=1e-5, scalar2=None, op0=ALU.max),
                 waits=[p.tok("SV")], inc="SV")
            p.op("v", lambda e: e.reciprocal(out=sst[:, 20:22], in_=sst[:, 30:32]),
                 waits=[p.tok("SV")], inc="SV")
            p.op("v", lambda e: e.tensor_scalar(out=sst[:, 16:18], in0=sst[:, 20:22],
                                                scalar1=127.0, scalar2=None, op0=ALU.mult),
                 waits=[p.tok("SV")], inc="SV")
            p.op("v", lambda e: e.tensor_tensor(out=sst[:, 22:24], in0=stats[:, 46:48],
                                                in1=sst[:, 16:18], op=ALU.mult),
                 waits=[p.tok("SV")], inc="SV")
            p.op("v", lambda e: e.tensor_scalar(out=sst[:, 18:20], in0=sst[:, 30:32],
                                                scalar1=1.0 / 127.0, scalar2=None,
                                                op0=ALU.mult), waits=[p.tok("SV")], inc="SV")
            p.op("v", lambda e: e.tensor_scalar(out=sst[:, 24:26], in0=sst[:, 18:20],
                                                scalar1=swc[:, 1:2], scalar2=None,
                                                op0=ALU.mult), waits=[p.tok("SV")], inc="SV")
            tok_qs2 = p.tok("SV")
            for c in range(NCHUNK):
                p.op("t", lambda e, c=c: e.transpose(
                    out=pmm[3][0:1, c * 128:(c + 1) * 128].bitcast(F32),
                    in_=sst[:, 22 + c:23 + c], identity=Idf[:]),
                    waits=([tok_qs2, tok_consts] + ([pmm_free[3]] if pmm_free[3] else [])) if c == 0 else [],
                    inc=("ST" if c == 1 else None))
            tok_mrT = p.tok("ST")
            p.op("v", lambda e: e.tensor_copy(out=mrow[:], in_=pmm[3][0:1, 0:256]),
                 waits=[tok_mrT], inc="SV")
            pmm_free[3] = p.tok("SV")
            tok_mrow = p.tok("SV")
            p.op("t", lambda e: e.matmul(out=pmm[3][:, 0:256], lhsT=Onc[:], rhs=mrow[:],
                                         start=True, stop=True),
                 waits=[tok_mrow] + ([pmm_free[3]] if pmm_free[3] else []), inc="ST")
            tok_B = p.tok("ST")
            p.op("v", lambda e: e.tensor_copy(out=Bsb[:], in_=pmm[3][:, 0:256]),
                 waits=[tok_B], inc="SV")
            pmm_free[3] = p.tok("SV")
            tok_Bsb = p.tok("SV")
            conv_tok = [None] * 32
            for fc in range(32):
                ib = i8g[fc % 2]
                p.op("v", lambda e, fc=fc, ib=ib: e.tensor_tensor(
                    out=ib[:], in0=garena[:, fc * 256:(fc + 1) * 256], in1=Bsb[:],
                    op=ALU.mult),
                    waits=[tok_Bsb] + ([p.tok("SV")] if fc >= 2 else []), inc="SV")
                tok_i8 = p.tok("SV")
                p.op("v", lambda e, fc=fc, ib=ib: e.tensor_copy(
                    out=ints2[:, fc * 256:(fc + 1) * 256], in_=ib[:]),
                    waits=[tok_i8], inc="SV")
                conv_tok[fc] = p.tok("SV")
            tok_ints2 = p.tok("SV")
            F["conv_tok"] = conv_tok
            if tile == 0:
                dump("ints2", ints2[:], [128, 8192], BF16, waits=[tok_ints2])

            F["tok_ints2"] = tok_ints2
            F["tok_qs2"] = tok_qs2
            site_guard["q2"] = tok_qs2

        def emit_back2(tile):
            nonlocal tok_y_dma
            F = front_toks[tile]
            tok_ints2 = F["tok_ints2"]
            conv_tok = F["conv_tok"]
            tok_qs2 = F["tok_qs2"]
            sf2 = [None] * 4
            for kq in range(4):
                sf2[kq] = load_slot("f2", kq)
                for cn in range(4):
                    c, n = cn // 2, cn % 2
                    bank = pmm[cn]
                    for k8 in range(8):
                        fc = kq * 8 + k8
                        w = []
                        if k8 == 0 and cn == 0:
                            w = [wb_ready[sf2[kq]], conv_tok[kq * 8 + 7]]
                        if kq == 0 and k8 == 0:
                            w = [wb_ready[sf2[0]], conv_tok[7]] + \
                                ([pmm_free[cn]] if pmm_free[cn] else [])
                        p.op("t", lambda e, fc=fc, c=c, n=n, k8=k8, i=sf2[kq], bank=bank: e.matmul(
                            out=bank[:],
                            lhsT=ints2[:, fc * 256 + c * 128: fc * 256 + (c + 1) * 128],
                            rhs=wb[i][:, k8 * 1024 + n * 512: k8 * 1024 + (n + 1) * 512],
                            start=(fc == 0), stop=(fc == 31)), waits=w,
                            inc=("ST" if (cn == 3 and k8 == 7) else None))
                tok_kq = p.tok("ST")
                done_slot(sf2[kq], tok_kq)
            tok_bl2 = p.tok("ST")
            tok_bl2_prev["tok"] = tok_bl2
            for cn in range(4):
                c, n = cn // 2, cn % 2
                bank = pmm[cn]
                p.op("v", lambda e, c=c, n=n, bank=bank: e.scalar_tensor_tensor(
                    out=ybuf[:, c * 1024 + n * 512: c * 1024 + (n + 1) * 512],
                    in0=bank[:], scalar=sst[:, 24 + c:25 + c],
                    in1=fb2tile[:, n * 512:(n + 1) * 512], op0=ALU.mult, op1=ALU.add),
                    waits=[tok_bl2, tok_qs2] + ([tok_y_dma] if tok_y_dma else []), inc="SV")
                p.op("v", lambda e, c=c, n=n: e.tensor_tensor(
                    out=xr1[:, c * 1024 + n * 512: c * 1024 + (n + 1) * 512],
                    in0=ybuf[:, c * 1024 + n * 512: c * 1024 + (n + 1) * 512],
                    in1=xr0[:, c * 1024 + n * 512: c * 1024 + (n + 1) * 512],
                    op=ALU.add), inc="SV")
                pmm_free[cn] = p.tok("SV")
            tok_z = p.tok("SV")

            tok_rstd4 = ln_stats(xr1[:], "fln", [tok_z])
            tok_y2 = ln_apply(xr1[:], "fln", ybuf[:], tok_rstd4, waits=[tok_z])
            site_guard["fln"] = tok_y2
            tok_y_dma = p.op("g", lambda e, t=tile: e.dma_start(
                out=y_out[t * TPT:(t + 1) * TPT].rearrange("(c p) dd -> p c dd", p=128),
                in_=ybuf[:].rearrange("p (c dd) -> p c dd", dd=1024)),
                waits=[tok_y2], inc="SO")


        emit_front(0, None)
        emit_front_b(0)
        for tile in range(nm):
            emit_back1(tile)
            if tile + 1 < nm:
                emit_front(tile + 1, None)
                emit_front_b(tile + 1)
            emit_back1q(tile)
            emit_back2(tile)
            front_toks.pop(tile)
        p.wait("g", p.tok("SO"))
        p.wait("d", p.tok("SW"))

        with nc.Block() as block:
            @block.sync
            def _(sync):
                for f in p.q["d"]:
                    f(sync)

            @block.tensor
            def _(t):
                for f in p.q["t"]:
                    f(t)

            @block.vector
            def _(v):
                for f in p.q["v"]:
                    f(v)

            @block.scalar
            def _(s):
                for f in p.q["s"]:
                    f(s)

            @block.gpsimd
            def _(g):
                for f in p.q["g"]:
                    f(g)

    return nc, dbg_outs


def make_inputs(x_full, weights, core):
    import ml_dtypes
    b, half = core // 2, core % 2
    d = {}
    d["x_main"] = np.ascontiguousarray(x_full[b, half * 4096:(half + 1) * 4096])
    d["x_pref"] = np.ascontiguousarray(x_full[b, 0:4096])
    d["maskcol"] = np.full((128, 1), float(half), np.float32)
    d["wq"] = weights["L0_attn_wq"]; d["wk"] = weights["L0_attn_wk"]
    d["wv"] = weights["L0_attn_wv"]; d["wo"] = weights["L0_attn_wo"]
    d["proj"] = weights["L0_attn_proj"]
    d["aw1"] = weights["L0_attn_w1"]; d["aw2"] = weights["L0_attn_w2"]
    d["fw1t"] = np.ascontiguousarray(np.asarray(weights["L0_ffn_w1"]).T)
    d["fw2t"] = np.ascontiguousarray(np.asarray(weights["L0_ffn_w2"]).T)
    d["g1c"] = np.asarray(weights["L0_attn_g1"]).reshape(8, 128).T
    d["g2c"] = np.asarray(weights["L0_attn_g2"]).reshape(8, 128).T
    d["b1c"] = np.asarray(weights["L0_attn_beta1"]).reshape(8, 128).T
    d["b2c"] = np.asarray(weights["L0_attn_beta2"]).reshape(8, 128).T
    d["ab1r"] = np.asarray(weights["L0_attn_b1"]).reshape(1, -1)
    d["ab2r"] = np.asarray(weights["L0_attn_b2"]).reshape(1, -1)
    d["fb1r"] = np.asarray(weights["L0_ffn_b1"]).reshape(1, -1)
    d["fb2r"] = np.asarray(weights["L0_ffn_b2"]).reshape(1, -1)
    d["ident_bf"] = np.eye(128, dtype=ml_dtypes.bfloat16)
    d["ident_f32"] = np.eye(128, dtype=np.float32)
    d["onescol_bf"] = np.ones((1, 128), ml_dtypes.bfloat16)
    d["onescol_f32"] = np.ones((1, 128), np.float32)
    d["onesrow_bf"] = np.ones((1, 512), ml_dtypes.bfloat16)
    d["onestok_bf"] = np.ones((128, 1), ml_dtypes.bfloat16)
    mk = np.triu(np.ones((128, 128), np.float32))
    d["maskT4"] = np.ascontiguousarray(np.tile(mk, (1, 4)))
    out = {}
    for k in d:
        a = np.asarray(d[k])
        if a.dtype == np.float64:
            a = a.astype(np.float32)
        out[k] = np.ascontiguousarray(a)
    return out


def kernel(**inputs):
    x = np.asarray(inputs["L0_attn_x"])
    nc, _ = build(nm=16, nprf=16)
    in_maps = [make_inputs(x, inputs, core) for core in range(8)]
    res = run_bass_kernel_spmd(nc, in_maps, core_ids=list(range(8)))
    out = np.zeros((4, 8192, 1024), np.float32)
    for core in range(8):
        b, half = core // 2, core % 2
        out[b, half * 4096:(half + 1) * 4096] = res.results[core]["y_out"]
    return out
